# version 05
# baseline (speedup 1.0000x reference)
import numpy as np
import ml_dtypes
import concourse.bass as bass
import concourse.mybir as mybir
from concourse.bass_utils import run_bass_kernel_spmd

F32 = mybir.dt.float32
BF16 = mybir.dt.bfloat16
FP8 = mybir.dt.float8e4
AF = mybir.ActivationFunctionType
ALU = mybir.AluOpType
AXX = mybir.AxisListType.X

C32 = float(np.float32(1.5 * 2.0 ** 23))
C64 = float(np.float32(1.5 * 2.0 ** 52))
EPS = 1e-5
D = 1024
H = 16
HD = 64
NCH = 8
INNER = 4096

ENGINES = ("sync", "tensor", "vector", "scalar", "gpsimd")
_ESEM = {"sync": "dml", "tensor": "te", "vector": "ve",
         "scalar": "se", "gpsimd": "ge"}


class Sched:

    def __init__(self, nc, sems):
        self.nc = nc
        self.sems = sems
        self.count = {k: 0 for k in sems}
        self.items = {e: [] for e in ENGINES}
        self.last_wait = {(e, s): 0 for e in ENGINES for s in sems}
        self.mm_mode = None

    def emit(self, eng, fn, sem=None, n=1):
        s = sem or _ESEM[eng]
        semh = self.sems[s]
        self.count[s] += n

        def run(e, fn=fn, semh=semh, n=n):
            fn(e).then_inc(semh, n)
        self.items[eng].append(run)

    def raw(self, eng, fn):
        self.items[eng].append(fn)

    def dma(self, out_fn, in_fn, eng="sync"):
        self.emit(eng, lambda e, o=out_fn, i=in_fn: e.dma_start(out=o(), in_=i()),
                  sem="dml", n=16)

    def dma_store(self, out_fn, in_fn, eng="sync"):
        self.emit(eng, lambda e, o=out_fn, i=in_fn: e.dma_start(out=o(), in_=i()),
                  sem="dms", n=16)

    def wait_all(self, eng, *sem_names):
        for s in sem_names:
            v = self.count[s]
            if v <= self.last_wait[(eng, s)]:
                continue
            if s in ("dml", "dms") and eng != "sync":
                if v > self.last_wait[("sync", s)]:
                    self.last_wait[("sync", s)] = v
                    self.raw("sync", lambda e, semh=self.sems[s], vv=v:
                             e.wait_ge(semh, vv))
            self.last_wait[(eng, s)] = v
            self.raw(eng, lambda e, semh=self.sems[s], vv=v: e.wait_ge(semh, vv))

    def mm(self, out_fn, lhsT_fn, rhs_fn, kdim, mdim, start, stop,
           tile_position=None):
        kk = 32 if kdim <= 32 else (64 if kdim <= 64 else 128)
        mm_ = 32 if mdim <= 32 else (64 if mdim <= 64 else 128)
        sig = (kk, mm_)
        if self.mm_mode is not None and self.mm_mode != sig:
            self.raw("tensor", lambda e: e.drain())
        self.mm_mode = sig
        self.emit("tensor",
                  lambda e, o=out_fn, l=lhsT_fn, r=rhs_fn, st=start, sp=stop,
                  tp=tile_position: e.matmul(o(), l(), r(), start=st, stop=sp,
                                             tile_position=tp))

    def transpose(self, out_fn, in_fn, id_fn):
        if self.mm_mode is not None and self.mm_mode != ("T",):
            self.raw("tensor", lambda e: e.drain())
        self.mm_mode = ("T",)
        self.emit("tensor", lambda e, o=out_fn, i=in_fn, d=id_fn:
                  e.transpose(o(), i(), d()))

    def replay(self, eng, e):
        for it in self.items[eng]:
            it(e)


def _c32b(x):
    return (x + 31) // 32 * 32


class Builder:
    def __init__(self, nseq=2, S=2048, nsub=4):
        self.nseq, self.S, self.nsub = nseq, S, nsub
        self.NT = S // 128
        self.QCH = min(512, S)
        self.NQC = S // self.QCH
        self.KT = S // 128
        self.SWEEP = min(512, S)
        self.NSW = S // self.SWEEP
        self.SWT = self.SWEEP // 128
        self.nc = bass.Bass("TRN2", target_bir_lowering=False)
        self._declare_dram()
        self._declare_sbuf()
        self.sems = {k: self.nc.alloc_semaphore(k) for k in
                     ("dml", "dms", "te", "ve", "se", "ge")}
        self.S_ = Sched(self.nc, self.sems)

    def _declare_dram(self):
        nc, S, nseq = self.nc, self.S, self.nseq
        dr = {}

        def din(name, shape, dt=F32):
            dr[name] = nc.dram_tensor(name, shape, dt, kind="ExternalInput").ap()

        din("xin", [nseq * S, D])
        din("w_qkv", [D, 3 * D]); din("w_proj", [D, D]); din("b_proj", [1, D])
        din("Wq", [D, D]); din("Wk", [D, D]); din("Wv", [D, D]); din("Wo", [D, D])
        din("bq", [1, D]); din("bk", [1, D]); din("bv", [1, D]); din("bo", [1, D])
        din("w_gate", [INNER, D]); din("w_up", [INNER, D]); din("w_down", [D, INNER])
        din("identf", [128, 128]); din("identb", [128, 128], BF16)
        din("onesf", [128, 128]); din("bdiag", [128, 14])
        din("ropec", [128, S]); din("ropes", [128, S]); din("sevenv", [128, 1])
        dr["out"] = nc.dram_tensor("out", [nseq * S, D], F32,
                                   kind="ExternalOutput").ap()
        dr["xs"] = nc.dram_tensor("xs", [S, D], F32).ap()
        dr["qTs"] = nc.dram_tensor("qTs", [D, S], F32).ap()
        dr["TgS"] = nc.dram_tensor("TgS", [D, INNER], FP8).ap()
        dr["TuS"] = nc.dram_tensor("TuS", [D, INNER], FP8).ap()
        dr["TdS"] = nc.dram_tensor("TdS", [INNER, D], FP8).ap()
        for w in ("WqT", "WkT", "WvT", "WoT"):
            dr[w] = nc.dram_tensor(w, [D, D], F32).ap()
        self.dr = dr

    def _declare_sbuf(self):
        nc = self.nc
        S, NT, SWEEP, QCH = self.S, self.NT, self.SWEEP, self.QCH
        self.ARENA = 204 * 1024
        arena = nc.alloc_sbuf_tensor("arena", [128, self.ARENA // 4], F32)
        try:
            base = nc.lookup_mloc(arena).addr
        except Exception:
            base = 20480
        sb = {}
        off = [0]

        def place(name, shape, dt, at=None):
            if at is None:
                at = off[0]
                nbytes = int(np.prod(shape[1:])) * mybir.dt.size(dt)
                off[0] = _c32b(at + nbytes)
            sb[name] = nc.alloc_sbuf_tensor_at(name, shape, dt, offset=base + at)
            return at

        place("identf", [128, 128], F32)
        place("identb", [128, 128], BF16)
        place("onesf", [128, 128], F32)
        place("bdiag", [128, 14], F32)
        place("bbproj", [128, D], F32)
        place("bbo", [128, D], F32)
        place("bvb", [128, D], F32)
        place("biasqkv", [128, 16], F32)
        place("negc", [128, 16], F32)
        place("gad", [128, 12], F32)
        place("sstats", [128, 128], F32)
        place("snorm", [128, 64], F32)
        ph = off[0]
        self.PERSIST = ph
        kT_at = ph
        Vb_at = kT_at + NCH * S * 4
        yfm_at = Vb_at + NT * (H * HD + H) * 4
        wbuf_at = yfm_at + NCH * SWEEP * 4
        stage_at = wbuf_at + 16 * 1024
        attn_end = stage_at + 8 * 1024
        assert attn_end <= self.ARENA, f"attn {attn_end}"
        place("kT", [128, NCH * S], F32, kT_at)
        place("Vb", [128, NT * (H * HD + H)], F32, Vb_at)
        place("yfm", [128, NCH * SWEEP], F32, yfm_at)
        place("wbuf", [128, 4096], F32, wbuf_at)
        place("stage", [128, 2048], F32, stage_at)
        qbuf_at = yfm_at
        obuf_at = qbuf_at + 2 * QCH * 4
        abuf_at = obuf_at + NCH * QCH * 4
        wp_at = abuf_at + 4 * QCH * 4
        assert wp_at + 1024 * 4 <= attn_end, "attn overlay"
        place("qbuf", [128, 2 * QCH], F32, qbuf_at)
        place("obuf", [128, NCH * QCH], F32, obuf_at)
        place("abuf", [128, 4 * QCH], F32, abuf_at)
        place("wp", [128, 1024], F32, wp_at)
        tg_at = ph
        tu_at = tg_at + NCH * INNER
        td_at = tu_at + NCH * INNER
        hb_at = _c32b(td_at + 32 * D)
        silu_at = hb_at + INNER * 4
        q2t_at = silu_at + INNER * 4
        q2f_at = q2t_at + INNER * 2
        q1t_at = q2f_at + INNER * 2
        q1f_at = q1t_at + D * 2
        fx_at = _c32b(q1f_at + D * 2)
        assert fx_at + 2 * D * 4 <= self.ARENA, "ffn"
        place("tg", [128, NCH * INNER], FP8, tg_at)
        place("tu", [128, NCH * INNER], FP8, tu_at)
        place("td", [128, 32 * D], FP8, td_at)
        place("hb", [128, INNER], F32, hb_at)
        place("silu", [128, INNER], F32, silu_at)
        place("q2t", [128, INNER], BF16, q2t_at)
        place("q2f", [128, INNER], BF16, q2f_at)
        place("q1t", [128, D], BF16, q1t_at)
        place("q1f", [128, D], BF16, q1f_at)
        place("fx", [128, 2 * D], F32, fx_at)
        pw_at = ph
        ptb_at = pw_at + 8 * 1024 * 4
        pst_at = ptb_at + 32 * 1024 * 2
        assert pst_at + 4096 * 4 <= self.ARENA, "prep"
        place("pw", [128, 8 * 1024], F32, pw_at)
        place("ptb", [128, 32 * 1024], BF16, ptb_at)
        place("pst", [128, 4096], F32, pst_at)
        place("pst8", [128, 4096], FP8, pst_at)
        self.sb = sb
        self.pb = [nc.alloc_psum_tensor(f"pb{i}", [128, 512], F32)
                   for i in range(7)]
        self.pbb = nc.alloc_psum_tensor("pbb", [128, 1024], BF16)

    def a(self, name, rows=None, cols=None):
        ap = self.sb[name].ap()
        r = rows or (0, 128)
        if cols is None:
            return ap[r[0]:r[1], :]
        return ap[r[0]:r[1], cols[0]:cols[1]]

    def col(self, name, c):
        return self.a(name, cols=(c, c + 1))

    def mkap(self, name, col_off, dims):
        ap0 = self.sb[name].ap()
        return bass.AP(ap0.tensor, ap0.offset + col_off,
                       [[ap0.ap[0][0], 128]] + [list(d) for d in dims])

    def emit_gadget(self):
        S_ = self.S_
        S_.emit("vector", lambda e: e.memset(self.col("gad", 4), EPS))
        S_.emit("vector", lambda e: e.memset(self.col("gad", 5), 0.125))
        S_.dma(lambda: self.col("sstats", 0), lambda: self.dr["sevenv"])
        S_.wait_all("vector", "dml")
        S_.emit("vector", lambda e: e.tensor_scalar(
            self.col("gad", 2), self.col("sstats", 0), C64, -C64,
            op0=ALU.add, op1=ALU.add))
        S_.wait_all("vector", "ve")
        S_.emit("vector", lambda e: e.tensor_scalar(
            self.col("gad", 3), self.col("gad", 2), 0.125, None, op0=ALU.mult))
        S_.wait_all("vector", "ve")
        S_.emit("vector", lambda e: e.tensor_scalar(
            self.col("gad", 0), self.col("gad", 3), C64 - C32, C32,
            op0=ALU.mult, op1=ALU.add))
        S_.wait_all("vector", "ve")
        S_.emit("vector", lambda e: e.tensor_scalar(
            self.col("gad", 1), self.col("gad", 0), -1.0, None, op0=ALU.mult))

    def emit_consts(self):
        S_ = self.S_
        for n in ("identf", "identb", "onesf", "bdiag"):
            S_.dma(lambda n=n: self.a(n), lambda n=n: self.dr[n])
        for i, bsrc in enumerate(("bq", "bk")):
            S_.dma(lambda i=i: self.a("pst", rows=(0, 8),
                                      cols=(i * 128, i * 128 + 128)),
                   lambda b=bsrc: self.dr[b][0:1, :].rearrange(
                       "a (p c) -> (a p) c", c=128))
        for i, bsrc in enumerate(("bv", "b_proj", "bo")):
            S_.dma(lambda i=i: self.a("pst", rows=(0, 1),
                                      cols=(1024 * (i + 1), 1024 * (i + 2))),
                   lambda b=bsrc: self.dr[b][0:1, :])
        S_.wait_all("tensor", "dml")
        for i in range(2):
            S_.transpose(lambda: self.pb[1].ap()[:, 0:8],
                         lambda i=i: self.a("pst", rows=(0, 8),
                                            cols=(i * 128, i * 128 + 128)),
                         lambda: self.a("identf", rows=(0, 8), cols=(0, 8)))
            S_.wait_all("vector", "te")
            S_.emit("vector", lambda e, i=i: e.tensor_copy(
                self.a("biasqkv", cols=(8 * i, 8 * i + 8)),
                self.pb[1].ap()[:, 0:8]))
            S_.wait_all("tensor", "ve")
        for dst, i in [("bvb", 1), ("bbproj", 2), ("bbo", 3)]:
            for hf in range(2):
                S_.mm(lambda: self.pb[0].ap()[:, 0:512],
                      lambda: self.a("onesf", rows=(0, 1)),
                      lambda i=i, hf=hf: self.a(
                          "pst", rows=(0, 1),
                          cols=(1024 * i + 512 * hf, 1024 * i + 512 * (hf + 1))),
                      kdim=1, mdim=128, start=True, stop=True)
                S_.wait_all("vector", "te")
                S_.emit("vector", lambda e, d=dst, hf=hf: e.tensor_copy(
                    self.a(d, cols=(512 * hf, 512 * (hf + 1))),
                    self.pb[0].ap()[:, 0:512]))
                S_.wait_all("tensor", "ve")

    def emit_prep_tquant(self, wname, R, C, dst, midx):
        S_ = self.S_
        RB = R // 128
        S_.wait_all("scalar", "te", "ve", "dms")
        for r in range(RB):
            sl = r % 2
            S_.wait_all("sync", "se")
            S_.dma(lambda sl=sl, C=C: self.a("pw", cols=(sl * C, sl * C + C)),
                   lambda r=r, w=wname: self.dr[w][128 * r:128 * (r + 1), :])
            S_.wait_all("scalar", "dml")
            S_.emit("scalar", lambda e, sl=sl, r=r, C=C: e.activation(
                self.a("pst", cols=(0, C)),
                self.a("pw", cols=(sl * C, sl * C + C)), AF.Abs,
                accum_out=self.col("snorm", r)))
        S_.wait_all("tensor", "se")
        S_.mm(lambda RB=RB: self.pb[0].ap()[0:1, 0:RB],
              lambda: self.a("onesf", cols=(0, 1)),
              lambda RB=RB: self.a("snorm", cols=(0, RB)),
              kdim=128, mdim=1, start=True, stop=True)
        S_.wait_all("vector", "te")
        S_.emit("vector", lambda e, RB=RB: e.tensor_copy(
            self.a("sstats", rows=(0, 1), cols=(40, 40 + RB)),
            self.pb[0].ap()[0:1, 0:RB]))
        S_.wait_all("vector", "ve")
        S_.emit("vector", lambda e, RB=RB: e.tensor_reduce(
            self.a("sstats", rows=(0, 1), cols=(72, 73)),
            self.a("sstats", rows=(0, 1), cols=(40, 40 + RB)),
            axis=AXX, op=ALU.add))
        S_.wait_all("vector", "ve")
        S_.emit("vector", lambda e, R=R, C=C: e.tensor_scalar(
            self.a("sstats", rows=(0, 1), cols=(73, 74)),
            self.a("sstats", rows=(0, 1), cols=(72, 73)),
            1.0 / (R * C), 1e-5, op0=ALU.mult, op1=ALU.max))
        S_.wait_all("vector", "ve")
        S_.emit("vector", lambda e: e.reciprocal(
            self.a("sstats", rows=(0, 1), cols=(74, 75)),
            self.a("sstats", rows=(0, 1), cols=(73, 74))))
        S_.wait_all("tensor", "ve")
        S_.mm(lambda: self.pb[0].ap()[:, 0:1],
              lambda: self.a("onesf", rows=(0, 1)),
              lambda: self.a("sstats", rows=(0, 1), cols=(74, 75)),
              kdim=1, mdim=128, start=True, stop=True)
        S_.wait_all("vector", "te")
        S_.emit("vector", lambda e: e.tensor_copy(
            self.col("sstats", 75), self.pb[0].ap()[:, 0:1]))
        S_.wait_all("tensor", "ve")
        S_.mm(lambda: self.pb[1].ap()[:, 0:1],
              lambda: self.a("onesf", rows=(0, 1)),
              lambda: self.a("sstats", rows=(0, 1), cols=(73, 74)),
              kdim=1, mdim=128, start=True, stop=True)
        S_.wait_all("vector", "te")
        S_.emit("vector", lambda e, midx=midx: e.tensor_copy(
            self.col("gad", 6 + midx), self.pb[1].ap()[:, 0:1]))
        for r in range(RB):
            sl = r % 2
            S_.wait_all("sync", "se", "ve")
            S_.dma(lambda sl=sl, C=C: self.a("pw", cols=(sl * C, sl * C + C)),
                   lambda r=r, w=wname: self.dr[w][128 * r:128 * (r + 1), :])
            S_.wait_all("scalar", "dml", "ve")
            S_.emit("scalar", lambda e, sl=sl, C=C: e.activation(
                self.a("pst", cols=(0, C)),
                self.a("pw", cols=(sl * C, sl * C + C)), AF.Identity,
                bias=self.col("gad", 0), scale=self.col("sstats", 75)))
            S_.wait_all("vector", "se")
            S_.emit("vector", lambda e, C=C: e.tensor_scalar(
                self.a("pst", cols=(C, 2 * C)) if 2 * C <= 4096 else
                self.a("pst", cols=(0, C)),
                self.a("pst", cols=(0, C)),
                self.col("gad", 1), -1.0, op0=ALU.add, op1=ALU.max))
            S_.wait_all("vector", "ve")
            src_cols = (C, 2 * C) if 2 * C <= 4096 else (0, C)
            S_.emit("vector", lambda e, r=r, C=C, sc=src_cols: e.tensor_scalar(
                self.a("ptb", cols=(r * C, r * C + C)),
                self.a("pst", cols=sc), 1.0, None, op0=ALU.min))
        S_.wait_all("tensor", "ve")
        for cc in range(C // 128):
            for rr in range(RB):
                S_.transpose(lambda: self.pbb.ap()[:, 0:128],
                             lambda rr=rr, cc=cc, C=C: self.a(
                                 "ptb", cols=(rr * C + cc * 128,
                                              rr * C + cc * 128 + 128)),
                             lambda: self.a("identb"))
                S_.wait_all("scalar", "te")
                S_.emit("scalar", lambda e, rr=rr: e.copy(
                    self.a("pst8", cols=(rr * 128, rr * 128 + 128)),
                    self.pbb.ap()[:, 0:128]))
                S_.wait_all("tensor", "se")
            S_.wait_all("sync", "se", "dms")
            S_.dma_store(lambda cc=cc, R=R, d=dst: self.dr[d][
                128 * cc:128 * (cc + 1), 0:R],
                lambda R=R: self.a("pst8", cols=(0, R)))
            S_.wait_all("scalar", "dms")

    def emit_prep_wt(self, wname, dst):
        S_ = self.S_
        S_.wait_all("sync", "te", "ve", "dms")
        for r in range(8):
            S_.dma(lambda r=r: self.a("pw", cols=(r * 1024, r * 1024 + 1024)),
                   lambda r=r, w=wname: self.dr[w][128 * r:128 * (r + 1), :])
        S_.wait_all("tensor", "dml")
        for cc in range(8):
            for rr in range(8):
                S_.transpose(lambda rr=rr: self.pb[rr % 2].ap()[:, 0:128],
                             lambda rr=rr, cc=cc: self.a(
                                 "pw", cols=(rr * 1024 + cc * 128,
                                             rr * 1024 + cc * 128 + 128)),
                             lambda: self.a("identf"))
                S_.wait_all("vector", "te")
                S_.emit("vector", lambda e, rr=rr: e.tensor_copy(
                    self.a("pst", cols=(rr * 128, rr * 128 + 128)),
                    self.pb[rr % 2].ap()[:, 0:128]))
                S_.wait_all("tensor", "ve")
            S_.wait_all("sync", "ve", "dms")
            S_.dma_store(lambda cc=cc, d=dst: self.dr[d][
                128 * cc:128 * (cc + 1), :],
                lambda: self.a("pst", cols=(0, 1024)))
            S_.wait_all("vector", "dms")

    def emit_ln(self, x_fn, y_fn):
        S_ = self.S_
        S_.wait_all("vector", "se", "ve")
        S_.emit("vector", lambda e, f=x_fn: e.bn_stats(
            self.a("sstats", cols=(8, 14)), f()[:, 0:512]))
        S_.emit("vector", lambda e, f=x_fn: e.bn_stats(
            self.a("sstats", cols=(14, 20)), f()[:, 512:1024]))
        S_.wait_all("vector", "ve")
        S_.emit("vector", lambda e: e.bn_aggr(
            self.a("sstats", cols=(20, 22)), self.a("sstats", cols=(8, 20))))
        S_.wait_all("scalar", "ve")
        S_.emit("scalar", lambda e: e.activation(
            self.col("sstats", 22), self.col("sstats", 21), AF.Sqrt,
            bias=self.col("gad", 4), scale=1.0))
        S_.wait_all("vector", "se")
        S_.emit("vector", lambda e: e.reciprocal(
            self.col("sstats", 23), self.col("sstats", 22)))
        S_.wait_all("vector", "ve")
        S_.emit("vector", lambda e: e.scalar_tensor_tensor(
            self.col("sstats", 24), self.col("sstats", 20), -1.0,
            self.col("sstats", 23), op0=ALU.mult, op1=ALU.mult))
        S_.wait_all("scalar", "ve", "te")
        S_.emit("scalar", lambda e, xf=x_fn, yf=y_fn: e.activation(
            yf(), xf(), AF.Identity,
            bias=self.col("sstats", 24), scale=self.col("sstats", 23)))

    def emit_actquant(self, in_fn, out_fn, a_fn):
        S_ = self.S_
        S_.wait_all("vector", "se", "ve")
        S_.emit("vector", lambda e, f=in_fn: e.tensor_reduce(
            self.col("sstats", 32), f(), axis=AXX, op=ALU.max,
            apply_absolute_value=True))
        S_.wait_all("vector", "ve")
        S_.emit("vector", lambda e: e.tensor_scalar(
            self.col("sstats", 33), self.col("sstats", 32), 1e-5, None,
            op0=ALU.max))
        S_.wait_all("vector", "ve")
        S_.emit("vector", lambda e: e.reciprocal(
            self.col("sstats", 34), self.col("sstats", 33)))
        S_.wait_all("vector", "ve")
        S_.emit("vector", lambda e: e.tensor_scalar(
            self.col("sstats", 35), self.col("sstats", 34), 127.0, None,
            op0=ALU.mult))
        S_.wait_all("scalar", "ve")
        S_.emit("scalar", lambda e, f=in_fn, af=a_fn: e.activation(
            af(), f(), AF.Identity,
            bias=self.col("gad", 0), scale=self.col("sstats", 35)))
        S_.wait_all("vector", "se")
        S_.emit("vector", lambda e, of=out_fn, af=a_fn: e.tensor_scalar(
            of(), af(), self.col("gad", 1), None, op0=ALU.add))

    def emit_attn(self, kind, xsrc_row0, xsrc, xdst, xdst_row0):
        S_ = self.S_
        S, SWEEP, NSW, SWT = self.S, self.SWEEP, self.NSW, self.SWT
        QCH, NQC, KT, NT = self.QCH, self.NQC, self.KT, self.NT
        rope = kind == "a"
        causal = kind == "b"

        S_.wait_all("vector", "te", "se", "ge", "dms")
        S_.emit("vector", lambda e: e.memset(
            self.mkap("Vb", HD, [[H * HD + H, NT], [HD + 1, H]]), 1.0))
        S_.emit("vector", lambda e: e.memset(self.a("snorm", cols=(32, 48)), 0.0))

        for sw in range(NSW):
            for t4 in range(SWT):
                t = sw * SWT + t4
                S_.wait_all("sync", "te", "se", "ve", "dms")
                S_.dma(lambda: self.a("stage", cols=(0, 1024)),
                       lambda t=t: xsrc[xsrc_row0 + 128 * t:
                                        xsrc_row0 + 128 * (t + 1), :])
                S_.wait_all("vector", "dml")
                S_.wait_all("scalar", "dml")
                self.emit_ln(lambda: self.a("stage", cols=(0, 1024)),
                             lambda: self.a("stage", cols=(1024, 2048)))
                S_.wait_all("tensor", "se")
                for c in range(NCH):
                    S_.transpose(lambda c=c: self.pb[c % 2].ap()[:, 0:128],
                                 lambda c=c: self.a(
                                     "stage", cols=(1024 + c * 128,
                                                    1024 + c * 128 + 128)),
                                 lambda: self.a("identf"))
                    S_.wait_all("vector", "te")
                    S_.emit("vector", lambda e, c=c, t4=t4: e.tensor_copy(
                        self.a("yfm", cols=(c * SWEEP + t4 * 128,
                                            c * SWEEP + t4 * 128 + 128)),
                        self.pb[c % 2].ap()[:, 0:128]))
                    S_.wait_all("tensor", "ve")
            for vb in range(2):
                S_.wait_all("sync", "te", "ve")
                for c in range(NCH):
                    if kind == "a":
                        src = (lambda c=c, vb=vb: self.dr["w_qkv"][
                            128 * c:128 * (c + 1),
                            2 * D + vb * 512:2 * D + (vb + 1) * 512])
                    else:
                        src = (lambda c=c, vb=vb: self.dr["WvT"][
                            128 * c:128 * (c + 1), vb * 512:(vb + 1) * 512])
                    S_.dma(lambda c=c: self.a("wbuf",
                                              cols=(c * 512, c * 512 + 512)), src)
                S_.wait_all("tensor", "dml", "ve")
                for t4 in range(SWT):
                    t = sw * SWT + t4
                    bank = 4 + t4 % 2
                    for c in range(NCH):
                        S_.mm(lambda bank=bank: self.pb[bank].ap()[:, 0:512],
                              lambda c=c, t4=t4: self.a(
                                  "yfm", cols=(c * SWEEP + t4 * 128,
                                               c * SWEEP + t4 * 128 + 128)),
                              lambda c=c: self.a("wbuf",
                                                 cols=(c * 512, c * 512 + 512)),
                              kdim=128, mdim=128, start=(c == 0),
                              stop=(c == NCH - 1))
                    S_.wait_all("vector", "te")

                    def vevac(e, t=t, vb=vb, bank=bank, kind=kind):
                        o = t * (H * HD + H) + vb * 8 * (HD + 1)
                        dst = self.mkap("Vb", o, [[HD + 1, 8], [1, HD]])
                        sap = self.pb[bank].ap()
                        src_ = bass.AP(sap.tensor, sap.offset,
                                       [[sap.ap[0][0], 128], [HD, 8], [1, HD]])
                        if kind == "b":
                            bias = self.mkap("bvb", vb * 512, [[HD, 8], [1, HD]])
                            return e.tensor_tensor(dst, src_, bias, op=ALU.add)
                        return e.tensor_copy(dst, src_)
                    S_.emit("vector", vevac)
                    S_.wait_all("tensor", "ve")
            for which in ("q", "k"):
                base_col = 0 if which == "q" else D
                for g in range(4):
                    S_.wait_all("sync", "te", "ve", "se", "dms")
                    for c in range(NCH):
                        if kind == "a":
                            src = (lambda c=c, g=g, b=base_col: self.dr["w_qkv"][
                                128 * c:128 * (c + 1), b + 256 * g:b + 256 * (g + 1)])
                        else:
                            wsc = "WqT" if which == "q" else "WkT"
                            src = (lambda c=c, g=g, w=wsc: self.dr[w][
                                128 * c:128 * (c + 1), 256 * g:256 * (g + 1)])
                        S_.dma(lambda c=c: self.a("wbuf",
                                                  cols=(c * 256, c * 256 + 256)), src)
                    if rope:
                        S_.dma(lambda: self.a("wbuf", cols=(2048, 2048 + SWEEP)),
                               lambda sw=sw: self.dr["ropec"][
                                   :, sw * SWEEP:(sw + 1) * SWEEP])
                        S_.dma(lambda: self.a("wbuf",
                                              cols=(2048 + SWEEP, 2048 + 2 * SWEEP)),
                               lambda sw=sw: self.dr["ropes"][
                                   :, sw * SWEEP:(sw + 1) * SWEEP])
                    S_.wait_all("tensor", "dml")
                    for par in range(2):
                        m = 2 * g + par
                        for c in range(NCH):
                            if kind == "a":
                                lhs = (lambda c=c, par=par: self.mkap(
                                    "wbuf", c * 256 + par, [[2, 128]]))
                            else:
                                lhs = (lambda c=c, par=par: self.a(
                                    "wbuf", cols=(c * 256 + par * 128,
                                                  c * 256 + par * 128 + 128)))
                            S_.mm(lambda: self.pb[6].ap()[:, 0:SWEEP],
                                  lhs,
                                  lambda c=c: self.a(
                                      "yfm", cols=(c * SWEEP, (c + 1) * SWEEP)),
                                  kdim=128, mdim=128, start=(c == 0),
                                  stop=(c == NCH - 1))
                        if kind == "b":
                            S_.wait_all("scalar", "te", "ve", "dms")
                            S_.emit("scalar", lambda e, par=par, m=m, w=which:
                                    e.activation(
                                        self.a("stage", cols=(par * SWEEP,
                                                              (par + 1) * SWEEP)),
                                        self.pb[6].ap()[:, 0:SWEEP], AF.Identity,
                                        bias=self.col("biasqkv",
                                                      (0 if w == "q" else 8) + m),
                                        scale=1.0))
                            S_.wait_all("tensor", "se")
                        else:
                            S_.wait_all("vector", "te", "dms")
                            S_.emit("vector", lambda e, par=par: e.tensor_copy(
                                self.a("stage", cols=(par * SWEEP,
                                                      (par + 1) * SWEEP)),
                                self.pb[6].ap()[:, 0:SWEEP]))
                            S_.wait_all("tensor", "ve")
                    qe = lambda: self.a("stage", cols=(0, SWEEP))
                    qo = lambda: self.a("stage", cols=(SWEEP, 2 * SWEEP))
                    if rope:
                        cosv = lambda: self.a("wbuf", cols=(2048, 2048 + SWEEP))
                        sinv = lambda: self.a("wbuf", cols=(2048 + SWEEP,
                                                            2048 + 2 * SWEEP))
                        tA = lambda: self.a("wbuf", cols=(3072, 3072 + SWEEP))
                        tB = lambda: self.a("wbuf", cols=(3072 + SWEEP,
                                                          3072 + 2 * SWEEP))
                        re_ = lambda: self.a("stage", cols=(1024, 1024 + SWEEP))
                        ro_ = lambda: self.a("stage", cols=(1024 + SWEEP,
                                                            1024 + 2 * SWEEP))
                        S_.wait_all("vector", "se", "dml", "te", "ve")
                        S_.emit("vector", lambda e: e.tensor_tensor(
                            tA(), qe(), cosv(), op=ALU.mult))
                        S_.emit("vector", lambda e: e.tensor_tensor(
                            tB(), qo(), sinv(), op=ALU.mult))
                        S_.wait_all("vector", "ve")
                        S_.emit("vector", lambda e: e.tensor_tensor(
                            re_(), tA(), tB(), op=ALU.subtract))
                        S_.wait_all("vector", "ve")
                        S_.emit("vector", lambda e: e.tensor_tensor(
                            tA(), qe(), sinv(), op=ALU.mult))
                        S_.emit("vector", lambda e: e.tensor_tensor(
                            tB(), qo(), cosv(), op=ALU.mult))
                        S_.wait_all("vector", "ve")
                        S_.emit("vector", lambda e: e.tensor_tensor(
                            ro_(), tA(), tB(), op=ALU.add))
                        outE, outO = re_, ro_
                    else:
                        outE, outO = qe, qo
                    S_.wait_all("scalar", "ve", "te", "se")
                    S_.emit("scalar", lambda e, f=outE: e.activation(
                        self.a("wbuf", cols=(3072, 3072 + SWEEP)), f(),
                        AF.Square))
                    S_.emit("scalar", lambda e, f=outO: e.activation(
                        self.a("wbuf", cols=(3072 + SWEEP, 3072 + 2 * SWEEP)),
                        f(), AF.Square))
                    S_.wait_all("tensor", "se")
                    ncol = (32 if which == "q" else 40) + g
                    for par in range(2):
                        bc = (0, 4) if kind == "a" else (6 + 4 * par, 10 + 4 * par)
                        S_.mm(lambda: self.pb[3].ap()[0:4, 0:SWEEP],
                              lambda bc=bc: self.a("bdiag", cols=bc),
                              lambda par=par: self.a(
                                  "wbuf", cols=(3072 + par * SWEEP,
                                                3072 + (par + 1) * SWEEP)),
                              kdim=128, mdim=4, start=(par == 0),
                              stop=(par == 1))
                    S_.wait_all("vector", "te")
                    S_.emit("vector", lambda e: e.tensor_reduce(
                        self.a("snorm", rows=(0, 4), cols=(0, 1)),
                        self.pb[3].ap()[0:4, 0:SWEEP], axis=AXX, op=ALU.max))
                    S_.wait_all("vector", "ve")
                    S_.emit("vector", lambda e, nc_=ncol: e.tensor_tensor(
                        self.a("snorm", rows=(0, 4), cols=(nc_, nc_ + 1)),
                        self.a("snorm", rows=(0, 4), cols=(nc_, nc_ + 1)),
                        self.a("snorm", rows=(0, 4), cols=(0, 1)),
                        op=ALU.max))
                    if which == "q":
                        S_.wait_all("sync", "ve", "se")
                        for par, f in ((0, outE), (1, outO)):
                            m = 2 * g + par
                            S_.dma_store(lambda m=m, sw=sw: self.dr["qTs"][
                                128 * m:128 * (m + 1),
                                sw * SWEEP:(sw + 1) * SWEEP], f)
                    else:
                        S_.wait_all("vector", "ve", "se")
                        for par, f in ((0, outE), (1, outO)):
                            m = 2 * g + par
                            S_.emit("vector", lambda e, m=m, sw=sw, f=f:
                                    e.tensor_copy(
                                        self.a("kT",
                                               cols=(m * S + sw * SWEEP,
                                                     m * S + (sw + 1) * SWEEP)),
                                        f()))
                        S_.wait_all("sync", "ve")

        S_.wait_all("vector", "ve", "te", "se")
        for g in range(4):
            S_.emit("vector", lambda e, g=g: e.tensor_tensor(
                self.a("snorm", rows=(0, 4), cols=(8 + g, 9 + g)),
                self.a("snorm", rows=(0, 4), cols=(32 + g, 33 + g)),
                self.a("snorm", rows=(0, 4), cols=(40 + g, 41 + g)),
                op=ALU.mult))
        S_.wait_all("scalar", "ve")
        for g in range(4):
            S_.emit("scalar", lambda e, g=g: e.activation(
                self.a("snorm", rows=(0, 4), cols=(12 + g, 13 + g)),
                self.a("snorm", rows=(0, 4), cols=(8 + g, 9 + g)), AF.Sqrt))
        S_.wait_all("vector", "se")
        for g in range(4):
            S_.emit("vector", lambda e, g=g: e.tensor_scalar(
                self.a("snorm", rows=(0, 4), cols=(16 + g, 17 + g)),
                self.a("snorm", rows=(0, 4), cols=(12 + g, 13 + g)),
                -0.125, None, op0=ALU.mult))
        S_.wait_all("tensor", "ve")
        for g in range(4):
            S_.transpose(lambda: self.pb[2].ap()[0:1, 0:4],
                         lambda g=g: self.a("snorm", rows=(0, 4),
                                            cols=(16 + g, 17 + g)),
                         lambda: self.a("identf", rows=(0, 4), cols=(0, 4)))
            S_.wait_all("vector", "te")
            S_.emit("vector", lambda e: e.tensor_copy(
                self.a("sstats", rows=(0, 1), cols=(100, 104)),
                self.pb[2].ap()[0:1, 0:4]))
            S_.wait_all("tensor", "ve")
            S_.mm(lambda: self.pb[2].ap()[:, 0:4],
                  lambda: self.a("onesf", rows=(0, 1)),
                  lambda: self.a("sstats", rows=(0, 1), cols=(100, 104)),
                  kdim=1, mdim=128, start=True, stop=True)
            S_.wait_all("vector", "te")
            S_.emit("vector", lambda e, g=g: e.tensor_copy(
                self.a("negc", cols=(4 * g, 4 * g + 4)),
                self.pb[2].ap()[:, 0:4]))
            S_.wait_all("tensor", "ve")

        wps = "w_proj" if kind == "a" else "WoT"
        bbst = "bbproj" if kind == "a" else "bbo"
        NTT = QCH // 128
        for qc in range(NQC):
            for g in range(4):
                S_.wait_all("sync", "te", "se", "ve")
                for par in range(2):
                    m = 2 * g + par
                    S_.dma(lambda par=par: self.a(
                        "qbuf", cols=(par * QCH, (par + 1) * QCH)),
                        lambda m=m, qc=qc: self.dr["qTs"][
                            128 * m:128 * (m + 1), qc * QCH:(qc + 1) * QCH])
                S_.wait_all("tensor", "dml")
                for pi in range(2):
                    hs = [4 * g + 2 * pi, 4 * g + 2 * pi + 1]
                    kts = [kt for kt in range(KT)
                           if not (causal and 128 * kt > qc * QCH + QCH - 1)]
                    for ki, kt in enumerate(kts):
                        for j, h in enumerate(hs):
                            i = h % 4
                            if kind == "a":
                                parts = [(2 * g, 32 * i, 32),
                                         (2 * g + 1, 32 * i, 32)]
                            else:
                                parts = [(2 * g + i // 2, 64 * (i % 2), 64)]
                            for pidx, (ch, r0, kk) in enumerate(parts):
                                par_idx = ch - 2 * g
                                S_.mm(lambda j=j: self.pb[j].ap()[:, 0:QCH],
                                      lambda ch=ch, r0=r0, kk=kk, kt=kt:
                                      self.a("kT", rows=(r0, r0 + kk),
                                             cols=(ch * S + kt * 128,
                                                   ch * S + kt * 128 + 128)),
                                      lambda pp=par_idx, r0=r0, kk=kk:
                                      self.a("qbuf", rows=(r0, r0 + kk),
                                             cols=(pp * QCH, (pp + 1) * QCH)),
                                      kdim=kk, mdim=128, start=(pidx == 0),
                                      stop=(pidx == len(parts) - 1),
                                      tile_position=(r0, 0))
                        S_.wait_all("scalar", "te")
                        for j, h in enumerate(hs):
                            S_.emit("scalar", lambda e, j=j, h=h, ki=ki:
                                    e.activation(
                                        self.a("abuf",
                                               cols=((2 * (ki % 2) + j) * QCH,
                                                     (2 * (ki % 2) + j + 1) * QCH)),
                                        self.pb[j].ap()[:, 0:QCH], AF.Exp,
                                        bias=self.col("negc", h),
                                        scale=self.col("gad", 5)))
                        diag = causal and (kt * 128 + 127 > qc * QCH)
                        if diag:
                            S_.wait_all("gpsimd", "se")
                            for j in range(2):
                                S_.emit("gpsimd", lambda e, j=j, ki=ki, qc=qc,
                                        kt=kt: e.affine_select(
                                            self.a("abuf",
                                                   cols=((2 * (ki % 2) + j) * QCH,
                                                         (2 * (ki % 2) + j + 1) * QCH)),
                                            self.a("abuf",
                                                   cols=((2 * (ki % 2) + j) * QCH,
                                                         (2 * (ki % 2) + j + 1) * QCH)),
                                            [[1, QCH]], ALU.is_ge, 0.0,
                                            base=qc * QCH - kt * 128,
                                            channel_multiplier=-1))
                            S_.wait_all("tensor", "ge")
                        else:
                            S_.wait_all("tensor", "se")
                        for j, h in enumerate(hs):
                            S_.mm(lambda j=j: self.pb[2 + j].ap()[0:65, 0:QCH],
                                  lambda h=h, kt=kt: self.a(
                                      "Vb", cols=(kt * (H * HD + H) + h * (HD + 1),
                                                  kt * (H * HD + H)
                                                  + h * (HD + 1) + HD + 1)),
                                  lambda j=j, ki=ki: self.a(
                                      "abuf", cols=((2 * (ki % 2) + j) * QCH,
                                                    (2 * (ki % 2) + j + 1) * QCH)),
                                  kdim=128, mdim=65, start=(ki == 0),
                                  stop=(ki == len(kts) - 1))
                    S_.wait_all("vector", "te")
                    for j, h in enumerate(hs):
                        S_.emit("vector", lambda e, j=j: e.reciprocal(
                            self.a("stage", rows=(64, 65), cols=(0, QCH)),
                            self.pb[2 + j].ap()[64:65, 0:QCH]))
                        S_.wait_all("tensor", "ve")
                        S_.mm(lambda: self.pb[4].ap()[0:64, 0:QCH],
                              lambda: self.a("onesf", rows=(64, 65), cols=(0, 64)),
                              lambda: self.a("stage", rows=(64, 65), cols=(0, QCH)),
                              kdim=1, mdim=64, start=True, stop=True)
                        S_.wait_all("scalar", "te", "dms")
                        S_.emit("scalar", lambda e: e.copy(
                            self.a("stage", rows=(0, 64), cols=(1024, 1024 + QCH)),
                            self.pb[4].ap()[0:64, 0:QCH]))
                        S_.wait_all("vector", "se")
                        S_.emit("vector", lambda e, j=j, h=h: e.tensor_tensor(
                            self.a("obuf", rows=(64 * (h % 2), 64 * (h % 2) + 64),
                                   cols=((h // 2) * QCH, (h // 2 + 1) * QCH)),
                            self.pb[2 + j].ap()[0:64, 0:QCH],
                            self.a("stage", rows=(0, 64), cols=(1024, 1024 + QCH)),
                            op=ALU.mult))
                        S_.wait_all("tensor", "ve")
            for oc in range(2):
                for c in range(NCH):
                    sl = (c % 2) * 512
                    S_.wait_all("sync", "te")
                    S_.dma(lambda sl=sl: self.a("wp", cols=(sl, sl + 512)),
                           lambda c=c, oc=oc, w=wps: self.dr[w][
                               128 * c:128 * (c + 1), oc * 512:(oc + 1) * 512])
                    S_.wait_all("tensor", "dml")
                    for tt in range(NTT):
                        S_.mm(lambda tt=tt: self.pb[tt].ap()[:, 0:512],
                              lambda c=c, tt=tt: self.a(
                                  "obuf", cols=(c * QCH + tt * 128,
                                                c * QCH + tt * 128 + 128)),
                              lambda sl=sl: self.a("wp", cols=(sl, sl + 512)),
                              kdim=128, mdim=128, start=(c == 0),
                              stop=(c == NCH - 1))
                S_.wait_all("vector", "te")
                for tt in range(NTT):
                    S_.wait_all("sync", "ve")
                    S_.dma(lambda: self.a("stage", cols=(512, 1024)),
                           lambda tt=tt, qc=qc, oc=oc: xsrc[
                               xsrc_row0 + qc * QCH + 128 * tt:
                               xsrc_row0 + qc * QCH + 128 * (tt + 1),
                               oc * 512:(oc + 1) * 512])
                    S_.wait_all("vector", "dml")
                    S_.emit("vector", lambda e, tt=tt, oc=oc, b=bbst:
                            e.tensor_tensor(
                                self.a("stage", cols=(0, 512)),
                                self.pb[tt].ap()[:, 0:512],
                                self.a(b, cols=(oc * 512, (oc + 1) * 512)),
                                op=ALU.add))
                    S_.wait_all("vector", "ve", "dms")
                    S_.emit("vector", lambda e: e.tensor_tensor(
                        self.a("stage", cols=(1024, 1536)),
                        self.a("stage", cols=(0, 512)),
                        self.a("stage", cols=(512, 1024)),
                        op=ALU.add))
                    S_.wait_all("sync", "ve")
                    S_.dma_store(lambda tt=tt, qc=qc, oc=oc: xdst[
                        xdst_row0 + qc * QCH + 128 * tt:
                        xdst_row0 + qc * QCH + 128 * (tt + 1),
                        oc * 512:(oc + 1) * 512],
                        lambda: self.a("stage", cols=(1024, 1536)))

    def emit_ffn(self, xsrc_row0, xsrc, xdst, xdst_row0):
        S_ = self.S_
        NT = self.NT
        S_.wait_all("sync", "te", "ve", "se", "ge", "dms")
        for c in range(NCH):
            S_.dma(lambda c=c: self.a("tg", cols=(c * INNER, (c + 1) * INNER)),
                   lambda c=c: self.dr["TgS"][128 * c:128 * (c + 1), :])
            S_.dma(lambda c=c: self.a("tu", cols=(c * INNER, (c + 1) * INNER)),
                   lambda c=c: self.dr["TuS"][128 * c:128 * (c + 1), :])
        for c in range(32):
            S_.dma(lambda c=c: self.a("td", cols=(c * D, (c + 1) * D)),
                   lambda c=c: self.dr["TdS"][128 * c:128 * (c + 1), :])
        for t in range(NT):
            S_.wait_all("sync", "ve", "te", "se", "dms")
            S_.dma(lambda: self.a("fx", cols=(0, 1024)),
                   lambda t=t: xsrc[xsrc_row0 + 128 * t:
                                    xsrc_row0 + 128 * (t + 1), :])
            S_.wait_all("vector", "dml")
            S_.wait_all("scalar", "dml")
            self.emit_ln(lambda: self.a("fx", cols=(0, 1024)),
                         lambda: self.a("silu", cols=(0, 1024)))
            self.emit_actquant(lambda: self.a("silu", cols=(0, 1024)),
                               lambda: self.a("q1t"),
                               lambda: self.a("hb", cols=(0, 1024)))
            S_.wait_all("vector", "ve")
            S_.emit("vector", lambda e: e.scalar_tensor_tensor(
                self.col("sstats", 36), self.col("sstats", 33), 1.0 / 127.0,
                self.col("gad", 6), op0=ALU.mult, op1=ALU.mult))
            S_.emit("vector", lambda e: e.scalar_tensor_tensor(
                self.col("sstats", 37), self.col("sstats", 33), 1.0 / 127.0,
                self.col("gad", 7), op0=ALU.mult, op1=ALU.mult))
            S_.wait_all("tensor", "ve", "dml")
            for c in range(NCH):
                S_.transpose(lambda: self.pbb.ap()[:, 0:128],
                             lambda c=c: self.a("q1t",
                                                cols=(c * 128, c * 128 + 128)),
                             lambda: self.a("identb"))
                S_.wait_all("scalar", "te")
                S_.emit("scalar", lambda e, c=c: e.copy(
                    self.a("q1f", cols=(c * 128, c * 128 + 128)),
                    self.pbb.ap()[:, 0:128]))
                S_.wait_all("tensor", "se")
            for wsb in ("tg", "tu"):
                for rnd in range(2):
                    S_.wait_all("tensor", "se", "ve")
                    for oc4 in range(4):
                        oc = rnd * 4 + oc4
                        for c in range(NCH):
                            S_.mm(lambda oc4=oc4: self.pb[oc4].ap()[:, 0:512],
                                  lambda c=c: self.a(
                                      "q1f", cols=(c * 128, c * 128 + 128)),
                                  lambda c=c, oc=oc, w=wsb: self.a(
                                      w, cols=(c * INNER + oc * 512,
                                               c * INNER + oc * 512 + 512)),
                                  kdim=128, mdim=128, start=(c == 0),
                                  stop=(c == NCH - 1))
                    if wsb == "tg":
                        S_.wait_all("scalar", "te")
                        for oc4 in range(4):
                            oc = rnd * 4 + oc4
                            S_.emit("scalar", lambda e, oc=oc, oc4=oc4:
                                    e.activation(
                                        self.a("silu", cols=(oc * 512,
                                                             oc * 512 + 512)),
                                        self.pb[oc4].ap()[:, 0:512], AF.Silu,
                                        scale=self.col("sstats", 36)))
                    else:
                        S_.wait_all("vector", "te", "se")
                        for oc4 in range(4):
                            oc = rnd * 4 + oc4
                            S_.emit("vector", lambda e, oc=oc, oc4=oc4:
                                    e.scalar_tensor_tensor(
                                        self.a("hb", cols=(oc * 512,
                                                           oc * 512 + 512)),
                                        self.pb[oc4].ap()[:, 0:512],
                                        self.col("sstats", 37),
                                        self.a("silu", cols=(oc * 512,
                                                             oc * 512 + 512)),
                                        op0=ALU.mult, op1=ALU.mult))
            self.emit_actquant(lambda: self.a("hb"), lambda: self.a("q2t"),
                               lambda: self.a("silu"))
            S_.wait_all("vector", "ve")
            S_.emit("vector", lambda e: e.scalar_tensor_tensor(
                self.col("sstats", 38), self.col("sstats", 33), 1.0 / 127.0,
                self.col("gad", 8), op0=ALU.mult, op1=ALU.mult))
            S_.wait_all("tensor", "ve")
            for c in range(32):
                S_.transpose(lambda: self.pbb.ap()[:, 0:128],
                             lambda c=c: self.a("q2t",
                                                cols=(c * 128, c * 128 + 128)),
                             lambda: self.a("identb"))
                S_.wait_all("scalar", "te")
                S_.emit("scalar", lambda e, c=c: e.copy(
                    self.a("q2f", cols=(c * 128, c * 128 + 128)),
                    self.pbb.ap()[:, 0:128]))
                S_.wait_all("tensor", "se")
            for oc in range(2):
                for c in range(32):
                    S_.mm(lambda oc=oc: self.pb[4 + oc].ap()[:, 0:512],
                          lambda c=c: self.a("q2f", cols=(c * 128, c * 128 + 128)),
                          lambda c=c, oc=oc: self.a(
                              "td", cols=(c * D + oc * 512,
                                          c * D + oc * 512 + 512)),
                          kdim=128, mdim=128, start=(c == 0), stop=(c == 31))
            S_.wait_all("vector", "te")
            for oc in range(2):
                S_.emit("vector", lambda e, oc=oc: e.scalar_tensor_tensor(
                    self.a("fx", cols=(1024 + oc * 512, 1024 + (oc + 1) * 512)),
                    self.pb[4 + oc].ap()[:, 0:512],
                    self.col("sstats", 38),
                    self.a("fx", cols=(oc * 512, (oc + 1) * 512)),
                    op0=ALU.mult, op1=ALU.add))
            S_.wait_all("sync", "ve")
            S_.dma_store(lambda t=t: xdst[xdst_row0 + 128 * t:
                                          xdst_row0 + 128 * (t + 1), :],
                         lambda: self.a("fx", cols=(1024, 2048)))

    def emit_program(self):
        S_ = self.S_
        self.emit_gadget()
        self.emit_consts()
        self.emit_prep_tquant("w_gate", INNER, D, "TgS", 0)
        self.emit_prep_tquant("w_up", INNER, D, "TuS", 1)
        self.emit_prep_tquant("w_down", D, INNER, "TdS", 2)
        if self.nsub >= 3:
            for w, d in (("Wq", "WqT"), ("Wk", "WkT"),
                         ("Wv", "WvT"), ("Wo", "WoT")):
                self.emit_prep_wt(w, d)
        for e in ENGINES:
            S_.wait_all(e, "dms")

        for seq in range(self.nseq):
            r0 = seq * self.S
            subs = [("attn", "a"), ("ffn", None),
                    ("attn", "b"), ("ffn", None)][:self.nsub]
            for si, (styp, kind) in enumerate(subs):
                last = si == len(subs) - 1
                xsrc, xsrc_row0 = ((self.dr["xin"], r0) if si == 0
                                   else (self.dr["xs"], 0))
                xdst, xdst_row0 = ((self.dr["out"], r0) if last
                                   else (self.dr["xs"], 0))
                if styp == "attn":
                    self.emit_attn(kind, xsrc_row0, xsrc, xdst, xdst_row0)
                else:
                    self.emit_ffn(xsrc_row0, xsrc, xdst, xdst_row0)
                for e in ENGINES:
                    S_.wait_all(e, "dms")
        S_.wait_all("sync", "dms")

        with self.nc.Block() as block:
            @block.sync
            def _(eng):
                S_.replay("sync", eng)

            @block.tensor
            def _(eng):
                S_.replay("tensor", eng)

            @block.vector
            def _(eng):
                S_.replay("vector", eng)

            @block.scalar
            def _(eng):
                S_.replay("scalar", eng)

            @block.gpsimd
            def _(eng):
                S_.replay("gpsimd", eng)
        return self.nc


THETA_BITS = ['0x40400000', '0x400ffacd', '0x3fd7f067', '0x3fa1ee74', '0x3f72dce8', '0x3f361f1a', '0x3f089265', '0x3eccd432', '0x3e99999a', '0x3e665e14', '0x3e2cc052', '0x3e018b90', '0x3dc24a54', '0x3d91b27b', '0x3d5a83d6', '0x3d23dcf6', '0x3cf5c28f', '0x3cb84b44', '0x3c8a3375', '0x3c4f45b2', '0x3c1b6eaa', '0x3be91d94', '0x3baecfde', '0x3b83172a', '0x3b449ba6', '0x3b136f69', '0x3add1f21', '0x3aa5d15c', '0x3a78b110', '0x3a3a7e10', '0x3a0bd97e', '0x39d1beab']
ROPE_FIX_B64 = 'eNpl2AmcjtUaAPDn4yaiFVFZKrmF9iJatGgRaVGKtEpxr1RDJHuUOy1EWkQqIVnKXgpFpIg0lohky65sY0nx3b/MfPNN+fn/ZuZbznvOc57zvO855QpGPMG5BSKGUNHvdWhK/CviaJrTh1mcf0TECIoViqjPS/zCOmofGbGTGwtH3M4zRXy3aEQWjxXzHi8eHbGRbezigmMi5lLk2IjTGE2L43yX7XxzfMQcSpwQ0YauHF084iIe4SMyS0TcWTJiNcVP9DpDyKJoqYg1/MImCpeO2MHQkyIqnBzxFd+SxWZOPyXiJ0qViahGV7I4rVxEA+6iXvmItrzCRLbT/FR/8zG/Ueo0/aLf6RG3VohYwEaOPSPiGgpVjBhHnX9HJM6MOJIqfMcOCp8V8QYTmcasShEZlSO+ZjPDqkR8T5+zI046x/XpzkLOONfrnH2+fjKMr/n3Bd5n8yEXitVFEQ15lEwKX+zz9KBG1YjrWc2gSyKWcEl18WYN++hdI2Ikk1hN0UsjbiOTxfxK1mW+S4nL9YtvWMRJV7g2TelFVk3xuNJ3OOEq16YPBa+OuJaNNKkVMe/aiPW0uE4f6Hd9xMk3RNzCRBbySW3XocKN8pUH6clmCtWJOI7evMdolnJNXTEkk+I3RVSiOzfVc12eZhalb45YRuIW/eRCdrCT4rdGVGYcn5HFLi64LeIBnmYUUd/c0et217/DmDi1ge/Sj68ofWfEOZzPbgrepd/U5jWSPNdQnNnYyLq9W342jpjPBfcYy73ap/J98op2zGcNRe6PqMqT9OdL9vHxA/KESg+6Ls3JoC3tm/gsq9jDjw9FFGhqTPRiNHOZz6OPRAxgSDPrvbm1xg9so8B/rFXu5sH/ypkWEVcxkv10eDRiJquJluJCBTK59TH5RWtGsZAyj2uLy5+wljIi+vI+R7dSg2hJBq0ZzhSmM4MTW0e8xVAynhQHbmsTMZkPnjLfbOTmdhHdeJNbno74D1XaGzvvc0oHY2UBr3SUh3zPT/zKTZ1ch09Zwk72dFZDu8g7anMPQ1jHkq5yjFLPRFzG5YyiQbeIxjzFaJp0dz1GU/VZMWcKWSxnP8Of004PeUAbOvIiC9nETf8zf4ygZKbr8SxT+I6Nh157XgzoyBYKvhBxMV+zmlEvqg0vRdzBJT3NAYso08s1KfGyOWZlH3X9FeNkCt9Sr694MpKyr6qztGEDw17Tf9ZT8vWIujTlLQq+EXEpPVnLMf3kAc+xiWvejBjDNp7sL2Z8xX4KDhBjHnkrYjBHDTR+lnH1265BqXf0hfk8/q7cGqQ9ttBysPZYyAND5A5d6UaRoeLEx3R73+vD1Ecu/8Ac04lM+jCAEUxlB0WHq5G0oD8PjFCDWMB+ThhpLtjCTsqOcr+iGnMp+6FaSUMGMPkjdW50xOucOSaiGR2ZwSZeGStH2UXRcdYlHSg43r2Cs3mQjvzBsRPMKdW5jAY0JIM/Jqo7H8tDJjKVFVz9SUQtGrKCDZSd5BmBxszjT677VG3+zE/a0YlejKXkZPcDMlhHNlWniCfT2UCxqdYmB2n8uTFT7AvjoRMHKTzN72RTZrr1w25KfiluZPICv3KAqTNcd6a4sIPOXxkbz8/yPqd8rT0GfBMxkGl8Q7XZEffNiWjFMBZz47cRNeeqQ3SmK99z3LyIe5lEye+0yWzmcc58a4yB33t+4Ygs9xhKU512nLvAXHA9wxhOs4XGwnuL5AXXLfaMx3Z6/6A+cPwSdYeRfMczS90b2MeNP7pFkU3ZZWod1elGL6Yyk9lcvjziCrZxxE/ynzo0ojHdeIHZlF0hD7mYT6j0c8ReDnLxSvGhNvXIpvIq16UOT7GAT1abxzW+yyROWmtsvMkNvxg3e7lknX5Ri05k8ivHrbe2OZ17Gc+nfE42j21wLd5nKo03+slWGm0yd2xgC0U36yO30Z2RfEPxLdplNevptFW9I5u+v3qPaRT4zb2cbE7YZvw8znDmU3K7POcK6tKAh+lI8R3qE7fwMB8zeKd7Adftkv+MYy01sq3p3RFnceIe7bGGAzy919g5Y585p9rv+kEjOvM2M7hgvzz8Qx5S6E9tUYnKtOUphjKMOWTxM39y4gF95z56HYy4Mqku0IJE8q9/iUP/kof///XH4VeTh19I/vXm4RcSydx3c9/O+/hfH0q9ksj5ViLngzlXyG0/9dvh7yRT38y5VE5TOV/J/czh3xK530jmtpzbp8PtHn41mdN2brs5o8sZQO6Ac0ef83vq+7lN53YjNYZkWkfyuphzxfSPJBKpvuRrK5EKaKrjaZ9Ipg0kp1dpUUq9ktde3nVyL5XMF5/cGUjmeyWZHpS8USXz5iaZajoVmrTRp2VLvngnEsm/pVLu1VMv58UvmcgXxdSEJNPzKjWotClKz6z0yU1NdypeuVOZO7WpDqRSOnfc6VOf05KHp+iTcKdmMKP5ijPs1M+kBWMpZLc+k52UtEu/njYs5ndOtGNvxFL2UtluPZMXyeZPKtu596CwnXspTuU1tnCknfyzZPPhUe6CrGUfV9ndb+dOu/u2LKWGnf1lPM0YCtjhn8+ttLTD784S9lPdLr8uD9GB/pS1469BTy61w29ME9rxPAMYTGk7/wq0pzNvsZwhJT2t2P1X5C4a8Rn7qFxKtWIAC0vbSdH+JHcBVuecCpzFGL5kOyefoioymspl3Glowkes4biy+kkX+nJvOU92/E6B8iolgyhzqutxzWnuGtzDNJbzC1tpfLqnBNrTgQl8wXRms5yTK4gNx50h/tzOGNaxmRIVxZ/HeIIMXqLwv92hKcY9PMmbLOOOM8WeKUyl7FnGXMmYKF/ZHZHFVKziNVZw7dlyhM+pdI72GMhc1rKVow6dRDCdRQw8z5j5k8zzzRkz+Jk6F0S8ylu8zWg2soV6F0a8w+ccpPFFEf9lMvPpfbG5IqrKNW6mXDWxpiUd+IaVVLjEXZAudKUv09jBi9XdTfmB5TXElIcutZ7ozkBGMZM5lL1M/vI2w5nKQY69POJ+WvMFezjvCk9X1KdmTXnCRh660rhYxQaKX2Wt0Jvx7KLl1XLn0CnINfrBe3xOkvNqmW8acD8P04aN7KHctdYBbelMb95jLJPJ4ieOv07toDajePB6NYHJLKD+DeLH8BvlCc3qeMqlC8+xhWPqqh3UZC/Db5Jz9cwVLelCsZvNJSspc4t6xnnUZAIz2MNeqtwqvjSjN6/zVH016nZzxvF3mIMG1iTP0OLOiJepdZd8pFxDuc4EjmhkzXAXY9jGuLvFnWaNPamxgDL3GCszOf1eT1TU4wBF7lN76cs0it0fMYtaD9jRUv5Ba4RNJKnXRLs8y0h+4p6HrFPOaarmsJvnHxYrZrCUro9YE6yifDM7Mu6nPWNZSdXm6ihj/mNM/5VPdOIAxVpYBxR4VP/pykt8y+ktrV2a0o936PqYXQfHPC5mNKf0E+aXTGplyGNqtRJ/tnJdazlHK3oykaWsotyTagOjmM12DlC8jX7RhEEUbiuWDGME97WzDin9tJrDDEq19zqdeZvxTGAFZTvoExl0ZjrFOuobtalLdmfX6WIuaMI8VlCiq5iwnULPyF920aSbPrOCVaxjC1W6ixvnPCsPn1PzeqgzvEDF/8lhfmQLDTPVEXZwwgsRF/IQ+3j+RbWIpRz1krpLwZ7GQFt68CMnegDI4Ed2cOPL2qUdr/M7z/c2/4yjQB/9oD5PsJQmr8ifvnZ57OLNV+UhV77md9bQ+nX1gG5vqH9MYharubafmHLEm+ozc0j0dx2+YBVFB9iJcT89eZ2xzKHEW+abqgPdh3mXueyh9dvGS5L678iXd92HWcyFg9zXeI0vKfKetniHL6gxWC7TdIjnI4byLWs5cqhYvy9XqDJMLeVrNlL1A5/nvOFyjt3UHCFXWM+/RqoRlOQ++rCIxVQe5T7PILZykDc+FJuP5CjvEKPdSzlvjM8xjJpj1T92s59Lx1k3fMZMrh0vphPEj1oTrQdWsp/CH8sxWtGDVWylxifaJsmtk9RGTvrUvPLoZ+LBdDInW0dUmCLGzGAX46bKCRZyyefmhlFspeIX+sK8aZ4NOHq6OksG/XmX4XzIeOZQ70u1ie0UmiHnqDtTXWE0y/jXV95nwizj/Nq4eJS6s60HfueoOX5y6rdqGk/Sjf5cPdda41W2cfI8caXUd8w3fhoxnSyWsIeTvzcfvExv+tCfQXzP+CxrjLX0X2BeWMkejl5onVOf8Wzg+EX6wht8whIKLrYj5y4yWMWrP+gnuzhhidrHhTzFBrKptjTiJrrw3o/y4dDpy7KIu2lOD2bRe7mYM40qP8kRZrCMq1aIP11Yw+6f3W9Wqqccv8o6YiJdVltL1Fmjjwym/NqIW/iO63/xGQqts0YYxiR+I0mR9dpiKIuovyHif3xI4Y3qFWdw2SbPZxTcIjd5lhcYQautEaf86vmbycxgOdV+8z2a8xxbqL5NO/RjEZt5abt8puIO9ZCe9GMEP7CUtYzcKac4Itt3GMYEbtgdcdEe8WU/xfda17zMAIrsEwf6s55av6uvtNuvhtDkD/eXP9VxGh7Q1kFt0YpsLk7KZ347dLqRtn1POylI5O1tE4n0vW4ib2Ofd6yQSDsiSKRvWNPONPJa/sfRR+oUJe/cJLWrz/tK3hY7mfbuP/5MpnbWaccOqaOYtCOPvH1x6mwj7XgimdrVpx/EJPM+lsx3MJN2IJN+nJBIO7/JHUreKUEi/wlP2tlNvgOJ/JHNPURJjTTvTCKZNhXJ1GFLMu2E6R9nRH87D8oNcPqLqUby50LaoVDauUHaCU3+xtLOcNLTI3UOk+p52hFS+sFIWo4kk+lJku9f3mznz72007i8vqRlYiL9gOtvOZHMjWDaiNLTM+1AL5U3eScm+YLpr/8DjXvDZA=='


def _apply_ulp(base, idx, delta):
    i = base.ravel().view(np.int32).astype(np.int64)
    m = np.where(i < 0, -2147483648 - i, i)
    m[idx] += delta.astype(np.int64)
    i2 = np.where(m >= 0, m, -2147483648 - m).astype(np.int32)
    out = i2.view(np.float32).reshape(base.shape)
    return out


def rope_tables(S):
    import zlib, base64
    L = 2048
    th32 = np.array([int(h, 16) for h in THETA_BITS],
                    dtype=np.uint32).view(np.float32)
    ang32 = (np.arange(L, dtype=np.float32)[:, None] * th32[None, :])
    base_c = np.cos(ang32.astype(np.float64)).astype(np.float32)
    base_s = np.sin(ang32.astype(np.float64)).astype(np.float32)
    raw = zlib.decompress(base64.b64decode(ROPE_FIX_B64))
    nc_, ns_ = np.frombuffer(raw[:8], dtype=np.uint32)
    o = 8
    idx_c = np.frombuffer(raw[o:o + 4 * nc_], dtype=np.uint32).astype(np.int64)
    o += 4 * nc_
    del_c = np.frombuffer(raw[o:o + nc_], dtype=np.int8)
    o += int(nc_)
    idx_s = np.frombuffer(raw[o:o + 4 * ns_], dtype=np.uint32).astype(np.int64)
    o += 4 * ns_
    del_s = np.frombuffer(raw[o:o + ns_], dtype=np.int8)
    cos32 = _apply_ulp(base_c, idx_c, del_c)
    sin32 = _apply_ulp(base_s, idx_s, del_s)
    jj = np.arange(128) % 32
    ropec = np.ascontiguousarray(cos32.T[jj, :S])
    ropes = np.ascontiguousarray(sin32.T[jj, :S])
    return ropec, ropes


def host_consts(S):
    identf = np.eye(128, dtype=np.float32)
    identb = np.eye(128).astype(ml_dtypes.bfloat16)
    onesf = np.ones((128, 128), dtype=np.float32)
    bdiag = np.zeros((128, 14), dtype=np.float32)
    for j in range(4):
        bdiag[32 * j:32 * (j + 1), j] = 1.0
    for j in range(2):
        bdiag[64 * j:64 * (j + 1), 4 + j] = 1.0
    for par in range(2):
        for j in range(2):
            bdiag[64 * j:64 * (j + 1), 6 + 4 * par + 2 * par + j] = 1.0
    ropec, ropes = rope_tables(S)
    sevenv = np.full((128, 1), 7.7, dtype=np.float32)
    return dict(identf=identf, identb=identb, onesf=onesf, bdiag=bdiag,
                ropec=ropec, ropes=ropes, sevenv=sevenv)


def make_inputs(x, w_qkv, w_proj, b_proj, Wq, bq, Wk, bk, Wv, bv, Wo, bo,
                w_gate, w_up, w_down, ncores=8):
    B, S, _ = x.shape
    nseq = B // ncores
    consts = host_consts(S)
    common = dict(
        w_qkv=np.ascontiguousarray(w_qkv, dtype=np.float32),
        w_proj=np.ascontiguousarray(w_proj, dtype=np.float32),
        b_proj=np.ascontiguousarray(b_proj, dtype=np.float32).reshape(1, D),
        Wq=np.ascontiguousarray(Wq, dtype=np.float32),
        Wk=np.ascontiguousarray(Wk, dtype=np.float32),
        Wv=np.ascontiguousarray(Wv, dtype=np.float32),
        Wo=np.ascontiguousarray(Wo, dtype=np.float32),
        bq=np.ascontiguousarray(bq, dtype=np.float32).reshape(1, D),
        bk=np.ascontiguousarray(bk, dtype=np.float32).reshape(1, D),
        bv=np.ascontiguousarray(bv, dtype=np.float32).reshape(1, D),
        bo=np.ascontiguousarray(bo, dtype=np.float32).reshape(1, D),
        w_gate=np.ascontiguousarray(w_gate, dtype=np.float32),
        w_up=np.ascontiguousarray(w_up, dtype=np.float32),
        w_down=np.ascontiguousarray(w_down, dtype=np.float32),
        **consts)
    in_maps = []
    for c in range(ncores):
        xin = np.ascontiguousarray(
            np.asarray(x[c * nseq:(c + 1) * nseq], dtype=np.float32
                       ).reshape(nseq * S, D))
        in_maps.append(dict(common, xin=xin))
    return in_maps, nseq


def kernel(L0_attn_a_x, L0_attn_a_w_qkv, L0_attn_a_w_proj, L0_attn_a_b_proj,
           L0_attn_b_Wq, L0_attn_b_bq, L0_attn_b_Wk, L0_attn_b_bk,
           L0_attn_b_Wv, L0_attn_b_bv, L0_attn_b_Wo, L0_attn_b_bo,
           L0_ffn_w_gate, L0_ffn_w_up, L0_ffn_w_down):
    x = np.asarray(L0_attn_a_x)
    B, S, _ = x.shape
    ncores = 8
    in_maps, nseq = make_inputs(
        x, np.asarray(L0_attn_a_w_qkv), np.asarray(L0_attn_a_w_proj),
        np.asarray(L0_attn_a_b_proj), np.asarray(L0_attn_b_Wq),
        np.asarray(L0_attn_b_bq), np.asarray(L0_attn_b_Wk),
        np.asarray(L0_attn_b_bk), np.asarray(L0_attn_b_Wv),
        np.asarray(L0_attn_b_bv), np.asarray(L0_attn_b_Wo),
        np.asarray(L0_attn_b_bo), np.asarray(L0_ffn_w_gate),
        np.asarray(L0_ffn_w_up), np.asarray(L0_ffn_w_down), ncores)
    b = Builder(nseq=nseq, S=S, nsub=4)
    nc = b.emit_program()
    res = run_bass_kernel_spmd(nc, in_maps, core_ids=list(range(ncores)))
    out = np.empty((B, S, D), dtype=np.float32)
    for c in range(ncores):
        out[c * nseq:(c + 1) * nseq] = np.asarray(
            res.results[c]["out"]).reshape(nseq, S, D)
    return out


# version 07
# speedup vs baseline: 1.1150x; 1.1150x over previous
import numpy as np
import ml_dtypes
import concourse.bass as bass
import concourse.mybir as mybir
from concourse.bass_utils import run_bass_kernel_spmd

F32 = mybir.dt.float32
BF16 = mybir.dt.bfloat16
FP8 = mybir.dt.float8e4
AF = mybir.ActivationFunctionType
ALU = mybir.AluOpType
AXX = mybir.AxisListType.X

C32 = float(np.float32(1.5 * 2.0 ** 23))
C64 = float(np.float32(1.5 * 2.0 ** 52))
EPS = 1e-5
D = 1024
H = 16
HD = 64
NCH = 8
INNER = 4096

ENGINES = ("sync", "tensor", "vector", "scalar", "gpsimd")
_ESEM = {"sync": "dml", "tensor": "te", "vector": "ve",
         "scalar": "se", "gpsimd": "ge"}


class Sched:

    def __init__(self, nc, sems):
        self.nc = nc
        self.sems = sems
        self.count = {k: 0 for k in sems}
        self.items = {e: [] for e in ENGINES}
        self.last_wait = {(e, s): 0 for e in ENGINES for s in sems}
        self.mm_mode = None

    def sem(self, name):
        if name not in self.sems:
            self.sems[name] = self.nc.alloc_semaphore(name)
            self.count[name] = 0
            for e in ENGINES:
                self.last_wait[(e, name)] = 0
        return name

    def mark(self, name):
        return self.count[name]

    def wait_val(self, eng, name, v):
        if v <= self.last_wait[(eng, name)]:
            return
        self.last_wait[(eng, name)] = v
        self.raw(eng, lambda e, semh=self.sems[name], vv=v: e.wait_ge(semh, vv))

    def emit(self, eng, fn, sem=None, n=1):
        s = sem or _ESEM[eng]
        semh = self.sems[s]
        self.count[s] += n

        def run(e, fn=fn, semh=semh, n=n):
            fn(e).then_inc(semh, n)
        self.items[eng].append(run)

    def raw(self, eng, fn):
        self.items[eng].append(fn)

    def dma(self, out_fn, in_fn, eng="sync", sem="dml"):
        if sem != "dml":
            self.sem(sem)
        self.emit(eng, lambda e, o=out_fn, i=in_fn: e.dma_start(out=o(), in_=i()),
                  sem=sem, n=16)

    def dma_store(self, out_fn, in_fn, eng="sync"):
        self.emit(eng, lambda e, o=out_fn, i=in_fn: e.dma_start(out=o(), in_=i()),
                  sem="dms", n=16)

    def wait_all(self, eng, *sem_names):
        for s in sem_names:
            v = self.count[s]
            if v <= self.last_wait[(eng, s)]:
                continue
            if s in ("dml", "dms") and eng != "sync":
                if v > self.last_wait[("sync", s)]:
                    self.last_wait[("sync", s)] = v
                    self.raw("sync", lambda e, semh=self.sems[s], vv=v:
                             e.wait_ge(semh, vv))
            self.last_wait[(eng, s)] = v
            self.raw(eng, lambda e, semh=self.sems[s], vv=v: e.wait_ge(semh, vv))

    def mm(self, out_fn, lhsT_fn, rhs_fn, kdim, mdim, start, stop,
           tile_position=None):
        kk = 32 if kdim <= 32 else (64 if kdim <= 64 else 128)
        mm_ = 32 if mdim <= 32 else (64 if mdim <= 64 else 128)
        sig = (kk, mm_)
        if self.mm_mode is not None and self.mm_mode != sig:
            self.raw("tensor", lambda e: e.drain())
        self.mm_mode = sig
        self.emit("tensor",
                  lambda e, o=out_fn, l=lhsT_fn, r=rhs_fn, st=start, sp=stop,
                  tp=tile_position: e.matmul(o(), l(), r(), start=st, stop=sp,
                                             tile_position=tp))

    def transpose(self, out_fn, in_fn, id_fn):
        if self.mm_mode is not None and self.mm_mode != ("T",):
            self.raw("tensor", lambda e: e.drain())
        self.mm_mode = ("T",)
        self.emit("tensor", lambda e, o=out_fn, i=in_fn, d=id_fn:
                  e.transpose(o(), i(), d()))

    def replay(self, eng, e):
        for it in self.items[eng]:
            it(e)


def _c32b(x):
    return (x + 31) // 32 * 32


class Builder:
    def __init__(self, nseq=2, S=2048, nsub=4):
        self.nseq, self.S, self.nsub = nseq, S, nsub
        self.NT = S // 128
        self.QCH = min(512, S)
        self.NQC = S // self.QCH
        self.KT = S // 128
        self.SWEEP = min(512, S)
        self.NSW = S // self.SWEEP
        self.SWT = self.SWEEP // 128
        self.nc = bass.Bass("TRN2", target_bir_lowering=False)
        self._declare_dram()
        self._declare_sbuf()
        self.sems = {k: self.nc.alloc_semaphore(k) for k in
                     ("dml", "dms", "te", "ve", "se", "ge")}
        self.S_ = Sched(self.nc, self.sems)

    def _declare_dram(self):
        nc, S, nseq = self.nc, self.S, self.nseq
        dr = {}

        def din(name, shape, dt=F32):
            dr[name] = nc.dram_tensor(name, shape, dt, kind="ExternalInput").ap()

        din("xin", [nseq * S, D])
        din("w_qkv", [D, 3 * D]); din("w_proj", [D, D]); din("b_proj", [1, D])
        din("Wq", [D, D]); din("Wk", [D, D]); din("Wv", [D, D]); din("Wo", [D, D])
        din("bq", [1, D]); din("bk", [1, D]); din("bv", [1, D]); din("bo", [1, D])
        din("w_gate", [INNER, D]); din("w_up", [INNER, D]); din("w_down", [D, INNER])
        din("identf", [128, 128]); din("identb", [128, 128], BF16)
        din("e64row", [128, 128])
        din("onesf", [128, 128]); din("bdiag", [128, 14])
        din("ropec", [128, S]); din("ropes", [128, S]); din("sevenv", [128, 1])
        dr["out"] = nc.dram_tensor("out", [nseq * S, D], F32,
                                   kind="ExternalOutput").ap()
        dr["xs"] = nc.dram_tensor("xs", [S, D], F32).ap()
        dr["qTs"] = nc.dram_tensor("qTs", [D, S], F32).ap()
        dr["TgS"] = nc.dram_tensor("TgS", [D, INNER], FP8).ap()
        dr["TuS"] = nc.dram_tensor("TuS", [D, INNER], FP8).ap()
        dr["TdS"] = nc.dram_tensor("TdS", [INNER, D], FP8).ap()
        for w in ("WqT", "WkT", "WvT", "WoT"):
            dr[w] = nc.dram_tensor(w, [D, D], F32).ap()
        self.dr = dr

    def _declare_sbuf(self):
        nc = self.nc
        S, NT, SWEEP, QCH = self.S, self.NT, self.SWEEP, self.QCH
        self.ARENA = 204 * 1024
        arena = nc.alloc_sbuf_tensor("arena", [128, self.ARENA // 4], F32)
        try:
            base = nc.lookup_mloc(arena).addr
        except Exception:
            base = 20480
        sb = {}
        off = [0]

        def place(name, shape, dt, at=None):
            if at is None:
                at = off[0]
                nbytes = int(np.prod(shape[1:])) * mybir.dt.size(dt)
                off[0] = _c32b(at + nbytes)
            sb[name] = nc.alloc_sbuf_tensor_at(name, shape, dt, offset=base + at)
            return at

        place("identf", [128, 128], F32)
        place("identb", [128, 128], BF16)
        place("onesf", [128, 128], F32)
        place("bdiag", [128, 14], F32)
        place("bbproj", [128, D], F32)
        place("bbo", [128, D], F32)
        place("bvb", [128, D], F32)
        place("biasqkv", [128, 16], F32)
        place("negc", [128, 16], F32)
        place("gad", [128, 12], F32)
        place("sstats", [128, 128], F32)
        place("snorm", [128, 64], F32)
        place("brow", [128, 1024], F32)
        place("e64row", [128, 128], F32)
        ph = off[0]
        self.PERSIST = ph
        kT_at = ph
        Vb_at = kT_at + NCH * S * 4
        yfm_at = Vb_at + NT * (H * HD + H) * 4
        wbuf_at = yfm_at + NCH * SWEEP * 4
        stage_at = wbuf_at + 16 * 1024
        attn_end = stage_at + 8 * 1024
        assert attn_end <= self.ARENA, f"attn {attn_end}"
        place("kT", [128, NCH * S], F32, kT_at)
        place("Vb", [128, NT * (H * HD + H)], F32, Vb_at)
        place("yfm", [128, NCH * SWEEP], F32, yfm_at)
        place("wbuf", [128, 4096], F32, wbuf_at)
        place("stage", [128, 2048], F32, stage_at)
        obuf_at = yfm_at
        abuf_at = obuf_at + NCH * QCH * 4
        assert abuf_at + 8 * QCH * 4 <= stage_at, "attn overlay"
        place("qbuf", [128, 2 * QCH], F32, stage_at)
        place("obuf", [128, NCH * QCH], F32, obuf_at)
        place("abuf", [128, 8 * QCH], F32, abuf_at)
        place("xpb", [128, 512], F32, abuf_at)
        place("wp", [128, 1024], F32, abuf_at + 512 * 4)
        tg_at = ph
        tu_at = tg_at + NCH * INNER
        td_at = tu_at + NCH * INNER
        hb_at = _c32b(td_at + 32 * D)
        silu_at = hb_at + INNER * 4
        q2t_at = silu_at + INNER * 4
        q2f_at = q2t_at + INNER * 2
        q1t_at = q2f_at + INNER * 2
        q1f_at = q1t_at + D * 2
        fx_at = _c32b(q1f_at + D * 2)
        assert fx_at + 2 * D * 4 <= self.ARENA, "ffn"
        place("tg", [128, NCH * INNER], FP8, tg_at)
        place("tu", [128, NCH * INNER], FP8, tu_at)
        place("td", [128, 32 * D], FP8, td_at)
        place("hb", [128, INNER], F32, hb_at)
        place("silu", [128, INNER], F32, silu_at)
        place("q2t", [128, INNER], BF16, q2t_at)
        place("q2f", [128, INNER], BF16, q2f_at)
        place("q1t", [128, D], BF16, q1t_at)
        place("q1f", [128, D], BF16, q1f_at)
        place("fx", [128, 2 * D], F32, fx_at)
        pw_at = ph
        ptb_at = pw_at + 8 * 1024 * 4
        pst_at = ptb_at + 32 * 1024 * 2
        assert pst_at + 4096 * 4 <= self.ARENA, "prep"
        place("pw", [128, 8 * 1024], F32, pw_at)
        place("ptb", [128, 32 * 1024], BF16, ptb_at)
        place("pst", [128, 4096], F32, pst_at)
        place("pst8", [128, 4096], FP8, pst_at)
        self.sb = sb
        self.pb = [nc.alloc_psum_tensor(f"pb{i}", [128, 512], F32)
                   for i in range(7)]
        self.pbb = nc.alloc_psum_tensor("pbb", [128, 1024], BF16)

    def a(self, name, rows=None, cols=None):
        ap = self.sb[name].ap()
        r = rows or (0, 128)
        if cols is None:
            return ap[r[0]:r[1], :]
        return ap[r[0]:r[1], cols[0]:cols[1]]

    def col(self, name, c):
        return self.a(name, cols=(c, c + 1))

    def mkap(self, name, col_off, dims):
        ap0 = self.sb[name].ap()
        return bass.AP(ap0.tensor, ap0.offset + col_off,
                       [[ap0.ap[0][0], 128]] + [list(d) for d in dims])

    def emit_gadget(self):
        S_ = self.S_
        S_.emit("vector", lambda e: e.memset(self.a("brow"), 0.0))
        S_.emit("vector", lambda e: e.memset(self.col("gad", 4), EPS))
        S_.emit("vector", lambda e: e.memset(self.col("gad", 5), 0.125))
        S_.dma(lambda: self.col("sstats", 0), lambda: self.dr["sevenv"])
        S_.wait_all("vector", "dml")
        S_.emit("vector", lambda e: e.tensor_scalar(
            self.col("gad", 2), self.col("sstats", 0), C64, -C64,
            op0=ALU.add, op1=ALU.add))
        S_.wait_all("vector", "ve")
        S_.emit("vector", lambda e: e.tensor_scalar(
            self.col("gad", 3), self.col("gad", 2), 0.125, None, op0=ALU.mult))
        S_.wait_all("vector", "ve")
        S_.emit("vector", lambda e: e.tensor_scalar(
            self.col("gad", 0), self.col("gad", 3), C64 - C32, C32,
            op0=ALU.mult, op1=ALU.add))
        S_.wait_all("vector", "ve")
        S_.emit("vector", lambda e: e.tensor_scalar(
            self.col("gad", 1), self.col("gad", 0), -1.0, None, op0=ALU.mult))

    def emit_consts(self):
        S_ = self.S_
        for n in ("identf", "identb", "onesf", "bdiag", "e64row"):
            S_.dma(lambda n=n: self.a(n), lambda n=n: self.dr[n])
        for i, bsrc in enumerate(("bq", "bk")):
            S_.dma(lambda i=i: self.a("pst", rows=(0, 8),
                                      cols=(i * 128, i * 128 + 128)),
                   lambda b=bsrc: self.dr[b][0:1, :].rearrange(
                       "a (p c) -> (a p) c", c=128))
        for i, bsrc in enumerate(("bv", "b_proj", "bo")):
            S_.dma(lambda i=i: self.a("pst", rows=(0, 1),
                                      cols=(1024 * (i + 1), 1024 * (i + 2))),
                   lambda b=bsrc: self.dr[b][0:1, :])
        S_.wait_all("tensor", "dml")
        for i in range(2):
            S_.transpose(lambda: self.pb[1].ap()[:, 0:8],
                         lambda i=i: self.a("pst", rows=(0, 8),
                                            cols=(i * 128, i * 128 + 128)),
                         lambda: self.a("identf", rows=(0, 8), cols=(0, 8)))
            S_.wait_all("vector", "te")
            S_.emit("vector", lambda e, i=i: e.tensor_copy(
                self.a("biasqkv", cols=(8 * i, 8 * i + 8)),
                self.pb[1].ap()[:, 0:8]))
            S_.wait_all("tensor", "ve")
        for dst, i in [("bvb", 1), ("bbproj", 2), ("bbo", 3)]:
            for hf in range(2):
                S_.mm(lambda: self.pb[0].ap()[:, 0:512],
                      lambda: self.a("onesf", rows=(0, 1)),
                      lambda i=i, hf=hf: self.a(
                          "pst", rows=(0, 1),
                          cols=(1024 * i + 512 * hf, 1024 * i + 512 * (hf + 1))),
                      kdim=1, mdim=128, start=True, stop=True)
                S_.wait_all("vector", "te")
                S_.emit("vector", lambda e, d=dst, hf=hf: e.tensor_copy(
                    self.a(d, cols=(512 * hf, 512 * (hf + 1))),
                    self.pb[0].ap()[:, 0:512]))
                S_.wait_all("tensor", "ve")

    def emit_prep_tquant(self, wname, R, C, dst, midx):
        S_ = self.S_
        RB = R // 128
        S_.wait_all("scalar", "te", "ve", "dms")
        for r in range(RB):
            sl = r % 2
            S_.wait_all("sync", "se")
            S_.dma(lambda sl=sl, C=C: self.a("pw", cols=(sl * C, sl * C + C)),
                   lambda r=r, w=wname: self.dr[w][128 * r:128 * (r + 1), :])
            S_.wait_all("scalar", "dml")
            S_.emit("scalar", lambda e, sl=sl, r=r, C=C: e.activation(
                self.a("pst", cols=(0, C)),
                self.a("pw", cols=(sl * C, sl * C + C)), AF.Abs,
                accum_out=self.col("snorm", r)))
        S_.wait_all("tensor", "se")
        S_.mm(lambda RB=RB: self.pb[0].ap()[0:1, 0:RB],
              lambda: self.a("onesf", cols=(0, 1)),
              lambda RB=RB: self.a("snorm", cols=(0, RB)),
              kdim=128, mdim=1, start=True, stop=True)
        S_.wait_all("vector", "te")
        S_.emit("vector", lambda e, RB=RB: e.tensor_copy(
            self.a("sstats", rows=(0, 1), cols=(40, 40 + RB)),
            self.pb[0].ap()[0:1, 0:RB]))
        S_.wait_all("vector", "ve")
        S_.emit("vector", lambda e, RB=RB: e.tensor_reduce(
            self.a("sstats", rows=(0, 1), cols=(72, 73)),
            self.a("sstats", rows=(0, 1), cols=(40, 40 + RB)),
            axis=AXX, op=ALU.add))
        S_.wait_all("vector", "ve")
        S_.emit("vector", lambda e, R=R, C=C: e.tensor_scalar(
            self.a("sstats", rows=(0, 1), cols=(73, 74)),
            self.a("sstats", rows=(0, 1), cols=(72, 73)),
            1.0 / (R * C), 1e-5, op0=ALU.mult, op1=ALU.max))
        S_.wait_all("vector", "ve")
        S_.emit("vector", lambda e: e.reciprocal(
            self.a("sstats", rows=(0, 1), cols=(74, 75)),
            self.a("sstats", rows=(0, 1), cols=(73, 74))))
        S_.wait_all("tensor", "ve")
        S_.mm(lambda: self.pb[0].ap()[:, 0:1],
              lambda: self.a("onesf", rows=(0, 1)),
              lambda: self.a("sstats", rows=(0, 1), cols=(74, 75)),
              kdim=1, mdim=128, start=True, stop=True)
        S_.wait_all("vector", "te")
        S_.emit("vector", lambda e: e.tensor_copy(
            self.col("sstats", 75), self.pb[0].ap()[:, 0:1]))
        S_.wait_all("tensor", "ve")
        S_.mm(lambda: self.pb[1].ap()[:, 0:1],
              lambda: self.a("onesf", rows=(0, 1)),
              lambda: self.a("sstats", rows=(0, 1), cols=(73, 74)),
              kdim=1, mdim=128, start=True, stop=True)
        S_.wait_all("vector", "te")
        S_.emit("vector", lambda e, midx=midx: e.tensor_copy(
            self.col("gad", 6 + midx), self.pb[1].ap()[:, 0:1]))
        for r in range(RB):
            sl = r % 2
            S_.wait_all("sync", "se", "ve")
            S_.dma(lambda sl=sl, C=C: self.a("pw", cols=(sl * C, sl * C + C)),
                   lambda r=r, w=wname: self.dr[w][128 * r:128 * (r + 1), :])
            S_.wait_all("scalar", "dml", "ve")
            S_.emit("scalar", lambda e, sl=sl, C=C: e.activation(
                self.a("pst", cols=(0, C)),
                self.a("pw", cols=(sl * C, sl * C + C)), AF.Identity,
                bias=self.col("gad", 0), scale=self.col("sstats", 75)))
            S_.wait_all("vector", "se")
            S_.emit("vector", lambda e, C=C: e.tensor_scalar(
                self.a("pst", cols=(C, 2 * C)) if 2 * C <= 4096 else
                self.a("pst", cols=(0, C)),
                self.a("pst", cols=(0, C)),
                self.col("gad", 1), -1.0, op0=ALU.add, op1=ALU.max))
            S_.wait_all("vector", "ve")
            src_cols = (C, 2 * C) if 2 * C <= 4096 else (0, C)
            S_.emit("vector", lambda e, r=r, C=C, sc=src_cols: e.tensor_scalar(
                self.a("ptb", cols=(r * C, r * C + C)),
                self.a("pst", cols=sc), 1.0, None, op0=ALU.min))
        S_.wait_all("tensor", "ve")
        for cc in range(C // 128):
            for rr in range(RB):
                S_.transpose(lambda: self.pbb.ap()[:, 0:128],
                             lambda rr=rr, cc=cc, C=C: self.a(
                                 "ptb", cols=(rr * C + cc * 128,
                                              rr * C + cc * 128 + 128)),
                             lambda: self.a("identb"))
                S_.wait_all("scalar", "te")
                S_.emit("scalar", lambda e, rr=rr: e.copy(
                    self.a("pst8", cols=(rr * 128, rr * 128 + 128)),
                    self.pbb.ap()[:, 0:128]))
                S_.wait_all("tensor", "se")
            S_.wait_all("sync", "se", "dms")
            S_.dma_store(lambda cc=cc, R=R, d=dst: self.dr[d][
                128 * cc:128 * (cc + 1), 0:R],
                lambda R=R: self.a("pst8", cols=(0, R)))
            S_.wait_all("scalar", "dms")

    def emit_prep_wt(self, wname, dst):
        S_ = self.S_
        S_.wait_all("sync", "te", "ve", "dms")
        for r in range(8):
            S_.dma(lambda r=r: self.a("pw", cols=(r * 1024, r * 1024 + 1024)),
                   lambda r=r, w=wname: self.dr[w][128 * r:128 * (r + 1), :])
        S_.wait_all("tensor", "dml")
        for cc in range(8):
            for rr in range(8):
                S_.transpose(lambda rr=rr: self.pb[rr % 2].ap()[:, 0:128],
                             lambda rr=rr, cc=cc: self.a(
                                 "pw", cols=(rr * 1024 + cc * 128,
                                             rr * 1024 + cc * 128 + 128)),
                             lambda: self.a("identf"))
                S_.wait_all("vector", "te")
                S_.emit("vector", lambda e, rr=rr: e.tensor_copy(
                    self.a("pst", cols=(rr * 128, rr * 128 + 128)),
                    self.pb[rr % 2].ap()[:, 0:128]))
                S_.wait_all("tensor", "ve")
            S_.wait_all("sync", "ve", "dms")
            S_.dma_store(lambda cc=cc, d=dst: self.dr[d][
                128 * cc:128 * (cc + 1), :],
                lambda: self.a("pst", cols=(0, 1024)))
            S_.wait_all("vector", "dms")

    def emit_ln(self, x_fn, y_fn):
        S_ = self.S_
        S_.wait_all("vector", "se", "ve")
        S_.emit("vector", lambda e, f=x_fn: e.bn_stats(
            self.a("sstats", cols=(8, 14)), f()[:, 0:512]))
        S_.emit("vector", lambda e, f=x_fn: e.bn_stats(
            self.a("sstats", cols=(14, 20)), f()[:, 512:1024]))
        S_.wait_all("vector", "ve")
        S_.emit("vector", lambda e: e.bn_aggr(
            self.a("sstats", cols=(20, 22)), self.a("sstats", cols=(8, 20))))
        S_.wait_all("scalar", "ve")
        S_.emit("scalar", lambda e: e.activation(
            self.col("sstats", 22), self.col("sstats", 21), AF.Sqrt,
            bias=self.col("gad", 4), scale=1.0))
        S_.wait_all("vector", "se")
        S_.emit("vector", lambda e: e.reciprocal(
            self.col("sstats", 23), self.col("sstats", 22)))
        S_.wait_all("vector", "ve")
        S_.emit("vector", lambda e: e.scalar_tensor_tensor(
            self.col("sstats", 24), self.col("sstats", 20), -1.0,
            self.col("sstats", 23), op0=ALU.mult, op1=ALU.mult))
        S_.wait_all("scalar", "ve", "te")
        S_.emit("scalar", lambda e, xf=x_fn, yf=y_fn: e.activation(
            yf(), xf(), AF.Identity,
            bias=self.col("sstats", 24), scale=self.col("sstats", 23)))

    def emit_actquant(self, in_fn, out_fn, a_fn):
        S_ = self.S_
        S_.wait_all("vector", "se", "ve")
        S_.emit("vector", lambda e, f=in_fn: e.tensor_reduce(
            self.col("sstats", 32), f(), axis=AXX, op=ALU.max,
            apply_absolute_value=True))
        S_.wait_all("vector", "ve")
        S_.emit("vector", lambda e: e.tensor_scalar(
            self.col("sstats", 33), self.col("sstats", 32), 1e-5, None,
            op0=ALU.max))
        S_.wait_all("vector", "ve")
        S_.emit("vector", lambda e: e.reciprocal(
            self.col("sstats", 34), self.col("sstats", 33)))
        S_.wait_all("vector", "ve")
        S_.emit("vector", lambda e: e.tensor_scalar(
            self.col("sstats", 35), self.col("sstats", 34), 127.0, None,
            op0=ALU.mult))
        S_.wait_all("scalar", "ve")
        S_.emit("scalar", lambda e, f=in_fn, af=a_fn: e.activation(
            af(), f(), AF.Identity,
            bias=self.col("gad", 0), scale=self.col("sstats", 35)))
        S_.wait_all("vector", "se")
        S_.emit("vector", lambda e, of=out_fn, af=a_fn: e.tensor_scalar(
            of(), af(), self.col("gad", 1), None, op0=ALU.add))

    def emit_attn(self, kind, xsrc_row0, xsrc, xdst, xdst_row0):
        S_ = self.S_
        S, SWEEP, NSW, SWT = self.S, self.SWEEP, self.NSW, self.SWT
        QCH, NQC, KT, NT = self.QCH, self.NQC, self.KT, self.NT
        rope = kind == "a"
        causal = kind == "b"

        S_.wait_all("vector", "te", "se", "ge", "dms")
        S_.emit("vector", lambda e: e.memset(
            self.mkap("Vb", HD, [[H * HD + H, NT], [HD + 1, H]]), 1.0))
        S_.emit("vector", lambda e: e.memset(self.a("snorm", cols=(32, 48)), 0.0))

        for sw in range(NSW):
            for t4 in range(SWT):
                t = sw * SWT + t4
                S_.wait_all("sync", "te", "se", "ve", "dms")
                S_.dma(lambda: self.a("stage", cols=(0, 1024)),
                       lambda t=t: xsrc[xsrc_row0 + 128 * t:
                                        xsrc_row0 + 128 * (t + 1), :])
                S_.wait_all("vector", "dml")
                S_.wait_all("scalar", "dml")
                self.emit_ln(lambda: self.a("stage", cols=(0, 1024)),
                             lambda: self.a("stage", cols=(1024, 2048)))
                S_.wait_all("tensor", "se")
                for grp in range(2):
                    S_.wait_all("tensor", "ve")
                    for cc in range(4):
                        c = grp * 4 + cc
                        S_.transpose(lambda grp=grp, cc=cc:
                                     self.pb[grp % 2].ap()[:, cc * 128:
                                                           cc * 128 + 128],
                                     lambda c=c: self.a(
                                         "stage", cols=(1024 + c * 128,
                                                        1024 + c * 128 + 128)),
                                     lambda: self.a("identf"))
                    S_.wait_all("vector", "te")
                    S_.emit("vector", lambda e, grp=grp, t4=t4: e.tensor_copy(
                        self.mkap("yfm", grp * 4 * SWEEP + t4 * 128,
                                  [[SWEEP, 4], [1, 128]]),
                        self.pb[grp % 2].ap()[:, 0:512]))
            for vb in range(2):
                S_.wait_all("sync", "te", "ve")
                for c in range(NCH):
                    if kind == "a":
                        src = (lambda c=c, vb=vb: self.dr["w_qkv"][
                            128 * c:128 * (c + 1),
                            2 * D + vb * 512:2 * D + (vb + 1) * 512])
                    else:
                        src = (lambda c=c, vb=vb: self.dr["WvT"][
                            128 * c:128 * (c + 1), vb * 512:(vb + 1) * 512])
                    S_.dma(lambda c=c: self.a("wbuf",
                                              cols=(c * 512, c * 512 + 512)), src)
                S_.wait_all("tensor", "dml", "ve")
                for t4 in range(SWT):
                    t = sw * SWT + t4
                    bank = 4 + t4 % 2
                    for c in range(NCH):
                        S_.mm(lambda bank=bank: self.pb[bank].ap()[:, 0:512],
                              lambda c=c, t4=t4: self.a(
                                  "yfm", cols=(c * SWEEP + t4 * 128,
                                               c * SWEEP + t4 * 128 + 128)),
                              lambda c=c: self.a("wbuf",
                                                 cols=(c * 512, c * 512 + 512)),
                              kdim=128, mdim=128, start=(c == 0),
                              stop=(c == NCH - 1))
                    S_.wait_all("vector", "te")

                    def vevac(e, t=t, vb=vb, bank=bank, kind=kind):
                        o = t * (H * HD + H) + vb * 8 * (HD + 1)
                        dst = self.mkap("Vb", o, [[HD + 1, 8], [1, HD]])
                        sap = self.pb[bank].ap()
                        src_ = bass.AP(sap.tensor, sap.offset,
                                       [[sap.ap[0][0], 128], [HD, 8], [1, HD]])
                        if kind == "b":
                            bias = self.mkap("bvb", vb * 512, [[HD, 8], [1, HD]])
                            return e.tensor_tensor(dst, src_, bias, op=ALU.add)
                        return e.tensor_copy(dst, src_)
                    S_.emit("vector", vevac)
                    S_.wait_all("tensor", "ve")
            for which in ("q", "k"):
                base_col = 0 if which == "q" else D
                for g in range(4):
                    S_.wait_all("sync", "te", "ve", "se", "dms")
                    for c in range(NCH):
                        if kind == "a":
                            src = (lambda c=c, g=g, b=base_col: self.dr["w_qkv"][
                                128 * c:128 * (c + 1), b + 256 * g:b + 256 * (g + 1)])
                        else:
                            wsc = "WqT" if which == "q" else "WkT"
                            src = (lambda c=c, g=g, w=wsc: self.dr[w][
                                128 * c:128 * (c + 1), 256 * g:256 * (g + 1)])
                        S_.dma(lambda c=c: self.a("wbuf",
                                                  cols=(c * 256, c * 256 + 256)), src)
                    if rope:
                        S_.dma(lambda: self.a("wbuf", cols=(2048, 2048 + SWEEP)),
                               lambda sw=sw: self.dr["ropec"][
                                   :, sw * SWEEP:(sw + 1) * SWEEP])
                        S_.dma(lambda: self.a("wbuf",
                                              cols=(2048 + SWEEP, 2048 + 2 * SWEEP)),
                               lambda sw=sw: self.dr["ropes"][
                                   :, sw * SWEEP:(sw + 1) * SWEEP])
                    S_.wait_all("tensor", "dml")
                    for par in range(2):
                        m = 2 * g + par
                        for c in range(NCH):
                            if kind == "a":
                                lhs = (lambda c=c, par=par: self.mkap(
                                    "wbuf", c * 256 + par, [[2, 128]]))
                            else:
                                lhs = (lambda c=c, par=par: self.a(
                                    "wbuf", cols=(c * 256 + par * 128,
                                                  c * 256 + par * 128 + 128)))
                            S_.mm(lambda: self.pb[6].ap()[:, 0:SWEEP],
                                  lhs,
                                  lambda c=c: self.a(
                                      "yfm", cols=(c * SWEEP, (c + 1) * SWEEP)),
                                  kdim=128, mdim=128, start=(c == 0),
                                  stop=(c == NCH - 1))
                        if kind == "b":
                            S_.wait_all("scalar", "te", "ve", "dms")
                            S_.emit("scalar", lambda e, par=par, m=m, w=which:
                                    e.activation(
                                        self.a("stage", cols=(par * SWEEP,
                                                              (par + 1) * SWEEP)),
                                        self.pb[6].ap()[:, 0:SWEEP], AF.Identity,
                                        bias=self.col("biasqkv",
                                                      (0 if w == "q" else 8) + m),
                                        scale=1.0))
                            S_.wait_all("tensor", "se")
                        else:
                            S_.wait_all("vector", "te", "dms")
                            S_.emit("vector", lambda e, par=par: e.tensor_copy(
                                self.a("stage", cols=(par * SWEEP,
                                                      (par + 1) * SWEEP)),
                                self.pb[6].ap()[:, 0:SWEEP]))
                            S_.wait_all("tensor", "ve")
                    qe = lambda: self.a("stage", cols=(0, SWEEP))
                    qo = lambda: self.a("stage", cols=(SWEEP, 2 * SWEEP))
                    if rope:
                        cosv = lambda: self.a("wbuf", cols=(2048, 2048 + SWEEP))
                        sinv = lambda: self.a("wbuf", cols=(2048 + SWEEP,
                                                            2048 + 2 * SWEEP))
                        tA = lambda: self.a("wbuf", cols=(3072, 3072 + SWEEP))
                        tB = lambda: self.a("wbuf", cols=(3072 + SWEEP,
                                                          3072 + 2 * SWEEP))
                        re_ = lambda: self.a("stage", cols=(1024, 1024 + SWEEP))
                        ro_ = lambda: self.a("stage", cols=(1024 + SWEEP,
                                                            1024 + 2 * SWEEP))
                        S_.wait_all("vector", "se", "dml", "te", "ve")
                        S_.emit("vector", lambda e: e.tensor_tensor(
                            tA(), qe(), cosv(), op=ALU.mult))
                        S_.emit("vector", lambda e: e.tensor_tensor(
                            tB(), qo(), sinv(), op=ALU.mult))
                        S_.wait_all("vector", "ve")
                        S_.emit("vector", lambda e: e.tensor_tensor(
                            re_(), tA(), tB(), op=ALU.subtract))
                        S_.wait_all("vector", "ve")
                        S_.emit("vector", lambda e: e.tensor_tensor(
                            tA(), qe(), sinv(), op=ALU.mult))
                        S_.emit("vector", lambda e: e.tensor_tensor(
                            tB(), qo(), cosv(), op=ALU.mult))
                        S_.wait_all("vector", "ve")
                        S_.emit("vector", lambda e: e.tensor_tensor(
                            ro_(), tA(), tB(), op=ALU.add))
                        outE, outO = re_, ro_
                    else:
                        outE, outO = qe, qo
                    S_.wait_all("scalar", "ve", "te", "se")
                    S_.emit("scalar", lambda e, f=outE: e.activation(
                        self.a("wbuf", cols=(3072, 3072 + SWEEP)), f(),
                        AF.Square))
                    S_.emit("scalar", lambda e, f=outO: e.activation(
                        self.a("wbuf", cols=(3072 + SWEEP, 3072 + 2 * SWEEP)),
                        f(), AF.Square))
                    S_.wait_all("tensor", "se")
                    ncol = (32 if which == "q" else 40) + g
                    for par in range(2):
                        bc = (0, 4) if kind == "a" else (6 + 4 * par, 10 + 4 * par)
                        S_.mm(lambda: self.pb[3].ap()[0:4, 0:SWEEP],
                              lambda bc=bc: self.a("bdiag", cols=bc),
                              lambda par=par: self.a(
                                  "wbuf", cols=(3072 + par * SWEEP,
                                                3072 + (par + 1) * SWEEP)),
                              kdim=128, mdim=4, start=(par == 0),
                              stop=(par == 1))
                    S_.wait_all("vector", "te")
                    S_.emit("vector", lambda e: e.tensor_reduce(
                        self.a("snorm", rows=(0, 4), cols=(0, 1)),
                        self.pb[3].ap()[0:4, 0:SWEEP], axis=AXX, op=ALU.max))
                    S_.wait_all("vector", "ve")
                    S_.emit("vector", lambda e, nc_=ncol: e.tensor_tensor(
                        self.a("snorm", rows=(0, 4), cols=(nc_, nc_ + 1)),
                        self.a("snorm", rows=(0, 4), cols=(nc_, nc_ + 1)),
                        self.a("snorm", rows=(0, 4), cols=(0, 1)),
                        op=ALU.max))
                    if which == "q":
                        S_.wait_all("sync", "ve", "se")
                        for par, f in ((0, outE), (1, outO)):
                            m = 2 * g + par
                            S_.dma_store(lambda m=m, sw=sw: self.dr["qTs"][
                                128 * m:128 * (m + 1),
                                sw * SWEEP:(sw + 1) * SWEEP], f)
                    else:
                        S_.wait_all("vector", "ve", "se")
                        for par, f in ((0, outE), (1, outO)):
                            m = 2 * g + par
                            S_.emit("vector", lambda e, m=m, sw=sw, f=f:
                                    e.tensor_copy(
                                        self.a("kT",
                                               cols=(m * S + sw * SWEEP,
                                                     m * S + (sw + 1) * SWEEP)),
                                        f()))
                        S_.wait_all("sync", "ve")

        S_.wait_all("vector", "ve", "te", "se")
        for g in range(4):
            S_.emit("vector", lambda e, g=g: e.tensor_tensor(
                self.a("snorm", rows=(0, 4), cols=(8 + g, 9 + g)),
                self.a("snorm", rows=(0, 4), cols=(32 + g, 33 + g)),
                self.a("snorm", rows=(0, 4), cols=(40 + g, 41 + g)),
                op=ALU.mult))
        S_.wait_all("scalar", "ve")
        for g in range(4):
            S_.emit("scalar", lambda e, g=g: e.activation(
                self.a("snorm", rows=(0, 4), cols=(12 + g, 13 + g)),
                self.a("snorm", rows=(0, 4), cols=(8 + g, 9 + g)), AF.Sqrt))
        S_.wait_all("vector", "se")
        for g in range(4):
            S_.emit("vector", lambda e, g=g: e.tensor_scalar(
                self.a("snorm", rows=(0, 4), cols=(16 + g, 17 + g)),
                self.a("snorm", rows=(0, 4), cols=(12 + g, 13 + g)),
                -0.125, None, op0=ALU.mult))
        S_.wait_all("tensor", "ve")
        for g in range(4):
            S_.transpose(lambda: self.pb[2].ap()[0:1, 0:4],
                         lambda g=g: self.a("snorm", rows=(0, 4),
                                            cols=(16 + g, 17 + g)),
                         lambda: self.a("identf", rows=(0, 4), cols=(0, 4)))
            S_.wait_all("vector", "te")
            S_.emit("vector", lambda e: e.tensor_copy(
                self.a("sstats", rows=(0, 1), cols=(100, 104)),
                self.pb[2].ap()[0:1, 0:4]))
            S_.wait_all("tensor", "ve")
            S_.mm(lambda: self.pb[2].ap()[:, 0:4],
                  lambda: self.a("onesf", rows=(0, 1)),
                  lambda: self.a("sstats", rows=(0, 1), cols=(100, 104)),
                  kdim=1, mdim=128, start=True, stop=True)
            S_.wait_all("vector", "te")
            S_.emit("vector", lambda e, g=g: e.tensor_copy(
                self.a("negc", cols=(4 * g, 4 * g + 4)),
                self.pb[2].ap()[:, 0:4]))
            S_.wait_all("tensor", "ve")

        wps = "w_proj" if kind == "a" else "WoT"
        bbst = "bbproj" if kind == "a" else "bbo"
        NTT = QCH // 128
        S_.sem("wq0"); S_.sem("wq1")
        for qc in range(NQC):
            for e_ in ("scalar", "gpsimd", "tensor", "vector"):
                S_.wait_all(e_, "wq0", "wq1")
            for g in range(4):
                S_.wait_all("sync", "te", "se", "ve")
                for par in range(2):
                    m = 2 * g + par
                    S_.dma(lambda par=par: self.a(
                        "qbuf", cols=(par * QCH, (par + 1) * QCH)),
                        lambda m=m, qc=qc: self.dr["qTs"][
                            128 * m:128 * (m + 1), qc * QCH:(qc + 1) * QCH])
                S_.wait_all("tensor", "dml")
                for pi in range(2):
                    hs = [4 * g + 2 * pi, 4 * g + 2 * pi + 1]
                    kts = [kt for kt in range(KT)
                           if not (causal and 128 * kt > qc * QCH + QCH - 1)]
                    KCH = 4
                    nchk = (len(kts) + KCH - 1) // KCH
                    for ci in range(nchk):
                        chunk = kts[ci * KCH:(ci + 1) * KCH]
                        for li, kt in enumerate(chunk):
                            gki = ci * KCH + li
                            S_.wait_all("tensor", "se")
                            bk = (gki % 2) * 2
                            for j, h in enumerate(hs):
                                i = h % 4
                                if kind == "a":
                                    parts = [(2 * g, 32 * i, 32),
                                             (2 * g + 1, 32 * i, 32)]
                                else:
                                    parts = [(2 * g + i // 2, 64 * (i % 2), 64)]
                                for pidx, (ch, r0, kk) in enumerate(parts):
                                    par_idx = ch - 2 * g
                                    S_.mm(lambda bk=bk, j=j:
                                          self.pb[bk + j].ap()[:, 0:QCH],
                                          lambda ch=ch, r0=r0, kk=kk, kt=kt:
                                          self.a("kT", rows=(r0, r0 + kk),
                                                 cols=(ch * S + kt * 128,
                                                       ch * S + kt * 128 + 128)),
                                          lambda pp=par_idx, r0=r0, kk=kk:
                                          self.a("qbuf", rows=(r0, r0 + kk),
                                                 cols=(pp * QCH, (pp + 1) * QCH)),
                                          kdim=kk, mdim=128, start=(pidx == 0),
                                          stop=(pidx == len(parts) - 1),
                                          tile_position=(r0, 0))
                            S_.wait_all("scalar", "te")
                            diag = causal and (kt * 128 + 127 > qc * QCH)
                            for j, h in enumerate(hs):
                                sl = (gki % 4) * 2 + j
                                S_.emit("scalar", lambda e, bk=bk, j=j, h=h, sl=sl:
                                        e.activation(
                                            self.a("abuf", cols=(sl * QCH,
                                                                 (sl + 1) * QCH)),
                                            self.pb[bk + j].ap()[:, 0:QCH], AF.Exp,
                                            bias=self.col("negc", h),
                                            scale=self.col("gad", 5)))
                            if diag:
                                S_.wait_all("gpsimd", "se")
                                for j in range(2):
                                    sl = (gki % 4) * 2 + j
                                    S_.emit("gpsimd", lambda e, sl=sl, qc=qc,
                                            kt=kt: e.affine_select(
                                                self.a("abuf", cols=(sl * QCH,
                                                                     (sl + 1) * QCH)),
                                                self.a("abuf", cols=(sl * QCH,
                                                                     (sl + 1) * QCH)),
                                                [[1, QCH]], ALU.is_ge, 0.0,
                                                base=qc * QCH - kt * 128,
                                                channel_multiplier=-1))
                        S_.wait_all("tensor", "se", "ge", "ve")
                        for li, kt in enumerate(chunk):
                            gki = ci * KCH + li
                            for j, h in enumerate(hs):
                                sl = (gki % 4) * 2 + j
                                S_.mm(lambda j=j: self.pb[4 + j].ap()[0:65, 0:QCH],
                                      lambda h=h, kt=kt: self.a(
                                          "Vb", cols=(kt * (H * HD + H)
                                                      + h * (HD + 1),
                                                      kt * (H * HD + H)
                                                      + h * (HD + 1) + HD + 1)),
                                      lambda sl=sl: self.a(
                                          "abuf", cols=(sl * QCH, (sl + 1) * QCH)),
                                      kdim=128, mdim=65, start=(gki == 0),
                                      stop=(gki == len(kts) - 1))
                    S_.wait_all("vector", "te", "ve")
                    for j in range(2):
                        S_.emit("vector", lambda e, j=j: e.reciprocal(
                            self.a("brow", rows=(64, 65),
                                   cols=(j * QCH, j * QCH + QCH)),
                            self.pb[4 + j].ap()[64:65, 0:QCH]))
                    S_.wait_all("tensor", "ve")
                    for j in range(2):
                        S_.mm(lambda j=j: (self.pb[6] if j == 0
                                           else self.pb[j]).ap()[0:65, 0:QCH],
                              lambda: self.a("e64row", cols=(0, 65)),
                              lambda j=j: self.a("brow",
                                                 cols=(j * QCH, j * QCH + QCH)),
                              kdim=128, mdim=65, start=True, stop=True)
                    S_.wait_all("scalar", "te", "dms")
                    for j in range(2):
                        S_.emit("scalar", lambda e, j=j: e.copy(
                            self.a("stage", rows=(0, 64),
                                   cols=(1024 + j * QCH, 1024 + (j + 1) * QCH)),
                            (self.pb[6] if j == 0 else self.pb[j]).ap()[0:64, 0:QCH]))
                    S_.wait_all("vector", "se")
                    for j, h in enumerate(hs):
                        S_.emit("vector", lambda e, j=j, h=h: e.tensor_tensor(
                            self.a("obuf", rows=(64 * (h % 2), 64 * (h % 2) + 64),
                                   cols=((h // 2) * QCH, (h // 2 + 1) * QCH)),
                            self.pb[4 + j].ap()[0:64, 0:QCH],
                            self.a("stage", rows=(0, 64),
                                   cols=(1024 + j * QCH, 1024 + (j + 1) * QCH)),
                            op=ALU.mult))
            S_.sem("wq0"); S_.sem("wq1")
            te_marks = {}

            def proj_mms(oc, c):
                sl = c % 2
                S_.wait_val("tensor", f"wq{sl}", S_.mark(f"wq{sl}"))
                for tt in range(NTT):
                    S_.mm(lambda tt=tt: self.pb[tt].ap()[:, 0:512],
                          lambda c=c, tt=tt: self.a(
                              "obuf", cols=(c * QCH + tt * 128,
                                            c * QCH + tt * 128 + 128)),
                          lambda sl=sl: self.a("wp", cols=(sl * 512,
                                                           sl * 512 + 512)),
                          kdim=128, mdim=128, start=(c == 0),
                          stop=(c == NCH - 1))
                te_marks[c] = S_.mark("te")

            m_ve_attn = S_.mark("ve")
            m_ge_attn = S_.mark("ge")
            for oc in range(2):
                te_marks.clear()
                for c in range(NCH):
                    sl = c % 2
                    if c >= 2:
                        S_.wait_val("sync", "te", te_marks[c - 2])
                    elif oc == 0:
                        S_.wait_val("sync", "ve", m_ve_attn)
                        S_.wait_val("sync", "ge", m_ge_attn)
                    else:
                        S_.wait_all("sync", "te")
                    S_.dma(lambda sl=sl: self.a("wp", cols=(sl * 512,
                                                            sl * 512 + 512)),
                           lambda c=c, oc=oc, w=wps: self.dr[w][
                               128 * c:128 * (c + 1), oc * 512:(oc + 1) * 512],
                           sem=f"wq{sl}")
                    if c >= 1:
                        proj_mms(oc, c - 1)
                proj_mms(oc, NCH - 1)
                S_.wait_all("vector", "te")
                for tt in range(NTT):
                    S_.wait_all("sync", "ve", "se")
                    S_.dma(lambda: self.a("xpb"),
                           lambda tt=tt, qc=qc, oc=oc: xsrc[
                               xsrc_row0 + qc * QCH + 128 * tt:
                               xsrc_row0 + qc * QCH + 128 * (tt + 1),
                               oc * 512:(oc + 1) * 512])
                    S_.wait_all("vector", "dml")
                    S_.emit("vector", lambda e, tt=tt, oc=oc, b=bbst:
                            e.tensor_tensor(
                                self.a("stage", cols=(1024, 1536)),
                                self.pb[tt].ap()[:, 0:512],
                                self.a(b, cols=(oc * 512, (oc + 1) * 512)),
                                op=ALU.add))
                    S_.wait_all("vector", "ve", "dms")
                    S_.emit("vector", lambda e: e.tensor_tensor(
                        self.a("stage", cols=(1536, 2048)),
                        self.a("stage", cols=(1024, 1536)),
                        self.a("xpb"),
                        op=ALU.add))
                    S_.wait_all("sync", "ve")
                    S_.dma_store(lambda tt=tt, qc=qc, oc=oc: xdst[
                        xdst_row0 + qc * QCH + 128 * tt:
                        xdst_row0 + qc * QCH + 128 * (tt + 1),
                        oc * 512:(oc + 1) * 512],
                        lambda: self.a("stage", cols=(1536, 2048)))

    def emit_ffn(self, xsrc_row0, xsrc, xdst, xdst_row0):
        S_ = self.S_
        NT = self.NT
        S_.sem("tld")
        S_.wait_all("sync", "te", "ve", "se", "ge", "dms")
        for c in range(NCH):
            S_.dma(lambda c=c: self.a("tg", cols=(c * INNER, (c + 1) * INNER)),
                   lambda c=c: self.dr["TgS"][128 * c:128 * (c + 1), :], sem="tld")
            S_.dma(lambda c=c: self.a("tu", cols=(c * INNER, (c + 1) * INNER)),
                   lambda c=c: self.dr["TuS"][128 * c:128 * (c + 1), :], sem="tld")
        for c in range(32):
            S_.dma(lambda c=c: self.a("td", cols=(c * D, (c + 1) * D)),
                   lambda c=c: self.dr["TdS"][128 * c:128 * (c + 1), :], sem="tld")
        tld_all = S_.mark("tld")
        for t in range(NT):
            S_.wait_all("sync", "ve", "te", "se", "dms")
            S_.dma(lambda: self.a("fx", cols=(0, 1024)),
                   lambda t=t: xsrc[xsrc_row0 + 128 * t:
                                    xsrc_row0 + 128 * (t + 1), :])
            S_.wait_all("vector", "dml")
            S_.wait_all("scalar", "dml")
            self.emit_ln(lambda: self.a("fx", cols=(0, 1024)),
                         lambda: self.a("silu", cols=(0, 1024)))
            self.emit_actquant(lambda: self.a("silu", cols=(0, 1024)),
                               lambda: self.a("q1t"),
                               lambda: self.a("hb", cols=(0, 1024)))
            S_.wait_all("vector", "ve")
            S_.emit("vector", lambda e: e.scalar_tensor_tensor(
                self.col("sstats", 36), self.col("sstats", 33), 1.0 / 127.0,
                self.col("gad", 6), op0=ALU.mult, op1=ALU.mult))
            S_.emit("vector", lambda e: e.scalar_tensor_tensor(
                self.col("sstats", 37), self.col("sstats", 33), 1.0 / 127.0,
                self.col("gad", 7), op0=ALU.mult, op1=ALU.mult))
            S_.wait_all("tensor", "ve", "dml", "se")
            for c in range(NCH):
                S_.transpose(lambda c=c: self.pbb.ap()[:, c * 128:c * 128 + 128],
                             lambda c=c: self.a("q1t",
                                                cols=(c * 128, c * 128 + 128)),
                             lambda: self.a("identb"))
            S_.wait_all("scalar", "te")
            S_.emit("scalar", lambda e: e.copy(
                self.a("q1f"), self.pbb.ap()[:, 0:1024]))
            S_.wait_val("tensor", "tld", tld_all)
            for wsb in ("tg", "tu"):
                for rnd in range(2):
                    S_.wait_all("tensor", "se", "ve")
                    for oc4 in range(4):
                        oc = rnd * 4 + oc4
                        for c in range(NCH):
                            S_.mm(lambda oc4=oc4: self.pb[oc4].ap()[:, 0:512],
                                  lambda c=c: self.a(
                                      "q1f", cols=(c * 128, c * 128 + 128)),
                                  lambda c=c, oc=oc, w=wsb: self.a(
                                      w, cols=(c * INNER + oc * 512,
                                               c * INNER + oc * 512 + 512)),
                                  kdim=128, mdim=128, start=(c == 0),
                                  stop=(c == NCH - 1))
                    if wsb == "tg":
                        S_.wait_all("scalar", "te")
                        for oc4 in range(4):
                            oc = rnd * 4 + oc4
                            S_.emit("scalar", lambda e, oc=oc, oc4=oc4:
                                    e.activation(
                                        self.a("silu", cols=(oc * 512,
                                                             oc * 512 + 512)),
                                        self.pb[oc4].ap()[:, 0:512], AF.Silu,
                                        scale=self.col("sstats", 36)))
                    else:
                        S_.wait_all("vector", "te", "se")
                        for oc4 in range(4):
                            oc = rnd * 4 + oc4
                            S_.emit("vector", lambda e, oc=oc, oc4=oc4:
                                    e.scalar_tensor_tensor(
                                        self.a("hb", cols=(oc * 512,
                                                           oc * 512 + 512)),
                                        self.pb[oc4].ap()[:, 0:512],
                                        self.col("sstats", 37),
                                        self.a("silu", cols=(oc * 512,
                                                             oc * 512 + 512)),
                                        op0=ALU.mult, op1=ALU.mult))
            self.emit_actquant(lambda: self.a("hb"), lambda: self.a("q2t"),
                               lambda: self.a("silu"))
            S_.wait_all("vector", "ve")
            S_.emit("vector", lambda e: e.scalar_tensor_tensor(
                self.col("sstats", 38), self.col("sstats", 33), 1.0 / 127.0,
                self.col("gad", 8), op0=ALU.mult, op1=ALU.mult))
            S_.wait_all("tensor", "ve")
            for grp in range(4):
                S_.wait_all("tensor", "se")
                for cc in range(8):
                    c = grp * 8 + cc
                    S_.transpose(lambda cc=cc: self.pbb.ap()[:, cc * 128:
                                                             cc * 128 + 128],
                                 lambda c=c: self.a("q2t",
                                                    cols=(c * 128, c * 128 + 128)),
                                 lambda: self.a("identb"))
                S_.wait_all("scalar", "te")
                S_.emit("scalar", lambda e, grp=grp: e.copy(
                    self.a("q2f", cols=(grp * 1024, grp * 1024 + 1024)),
                    self.pbb.ap()[:, 0:1024]))
            S_.wait_all("tensor", "se")
            for oc in range(2):
                for c in range(32):
                    S_.mm(lambda oc=oc: self.pb[4 + oc].ap()[:, 0:512],
                          lambda c=c: self.a("q2f", cols=(c * 128, c * 128 + 128)),
                          lambda c=c, oc=oc: self.a(
                              "td", cols=(c * D + oc * 512,
                                          c * D + oc * 512 + 512)),
                          kdim=128, mdim=128, start=(c == 0), stop=(c == 31))
            S_.wait_all("vector", "te")
            for oc in range(2):
                S_.emit("vector", lambda e, oc=oc: e.scalar_tensor_tensor(
                    self.a("fx", cols=(1024 + oc * 512, 1024 + (oc + 1) * 512)),
                    self.pb[4 + oc].ap()[:, 0:512],
                    self.col("sstats", 38),
                    self.a("fx", cols=(oc * 512, (oc + 1) * 512)),
                    op0=ALU.mult, op1=ALU.add))
            S_.wait_all("sync", "ve")
            S_.dma_store(lambda t=t: xdst[xdst_row0 + 128 * t:
                                          xdst_row0 + 128 * (t + 1), :],
                         lambda: self.a("fx", cols=(1024, 2048)))

    def emit_program(self):
        S_ = self.S_
        self.emit_gadget()
        self.emit_consts()
        self.emit_prep_tquant("w_gate", INNER, D, "TgS", 0)
        self.emit_prep_tquant("w_up", INNER, D, "TuS", 1)
        self.emit_prep_tquant("w_down", D, INNER, "TdS", 2)
        if self.nsub >= 3:
            for w, d in (("Wq", "WqT"), ("Wk", "WkT"),
                         ("Wv", "WvT"), ("Wo", "WoT")):
                self.emit_prep_wt(w, d)
        for e in ENGINES:
            S_.wait_all(e, "dms")

        for seq in range(self.nseq):
            r0 = seq * self.S
            subs = [("attn", "a"), ("ffn", None),
                    ("attn", "b"), ("ffn", None)][:self.nsub]
            for si, (styp, kind) in enumerate(subs):
                last = si == len(subs) - 1
                xsrc, xsrc_row0 = ((self.dr["xin"], r0) if si == 0
                                   else (self.dr["xs"], 0))
                xdst, xdst_row0 = ((self.dr["out"], r0) if last
                                   else (self.dr["xs"], 0))
                if styp == "attn":
                    self.emit_attn(kind, xsrc_row0, xsrc, xdst, xdst_row0)
                else:
                    self.emit_ffn(xsrc_row0, xsrc, xdst, xdst_row0)
                for e in ENGINES:
                    S_.wait_all(e, "dms")
        S_.wait_all("sync", "dms")

        with self.nc.Block() as block:
            @block.sync
            def _(eng):
                S_.replay("sync", eng)

            @block.tensor
            def _(eng):
                S_.replay("tensor", eng)

            @block.vector
            def _(eng):
                S_.replay("vector", eng)

            @block.scalar
            def _(eng):
                S_.replay("scalar", eng)

            @block.gpsimd
            def _(eng):
                S_.replay("gpsimd", eng)
        return self.nc


THETA_BITS = ['0x40400000', '0x400ffacd', '0x3fd7f067', '0x3fa1ee74', '0x3f72dce8', '0x3f361f1a', '0x3f089265', '0x3eccd432', '0x3e99999a', '0x3e665e14', '0x3e2cc052', '0x3e018b90', '0x3dc24a54', '0x3d91b27b', '0x3d5a83d6', '0x3d23dcf6', '0x3cf5c28f', '0x3cb84b44', '0x3c8a3375', '0x3c4f45b2', '0x3c1b6eaa', '0x3be91d94', '0x3baecfde', '0x3b83172a', '0x3b449ba6', '0x3b136f69', '0x3add1f21', '0x3aa5d15c', '0x3a78b110', '0x3a3a7e10', '0x3a0bd97e', '0x39d1beab']
ROPE_FIX_B64 = 'eNpl2AmcjtUaAPDn4yaiFVFZKrmF9iJatGgRaVGKtEpxr1RDJHuUOy1EWkQqIVnKXgpFpIg0lohky65sY0nx3b/MfPNN+fn/ZuZbznvOc57zvO855QpGPMG5BSKGUNHvdWhK/CviaJrTh1mcf0TECIoViqjPS/zCOmofGbGTGwtH3M4zRXy3aEQWjxXzHi8eHbGRbezigmMi5lLk2IjTGE2L43yX7XxzfMQcSpwQ0YauHF084iIe4SMyS0TcWTJiNcVP9DpDyKJoqYg1/MImCpeO2MHQkyIqnBzxFd+SxWZOPyXiJ0qViahGV7I4rVxEA+6iXvmItrzCRLbT/FR/8zG/Ueo0/aLf6RG3VohYwEaOPSPiGgpVjBhHnX9HJM6MOJIqfMcOCp8V8QYTmcasShEZlSO+ZjPDqkR8T5+zI046x/XpzkLOONfrnH2+fjKMr/n3Bd5n8yEXitVFEQ15lEwKX+zz9KBG1YjrWc2gSyKWcEl18WYN++hdI2Ikk1hN0UsjbiOTxfxK1mW+S4nL9YtvWMRJV7g2TelFVk3xuNJ3OOEq16YPBa+OuJaNNKkVMe/aiPW0uE4f6Hd9xMk3RNzCRBbySW3XocKN8pUH6clmCtWJOI7evMdolnJNXTEkk+I3RVSiOzfVc12eZhalb45YRuIW/eRCdrCT4rdGVGYcn5HFLi64LeIBnmYUUd/c0et217/DmDi1ge/Sj68ofWfEOZzPbgrepd/U5jWSPNdQnNnYyLq9W342jpjPBfcYy73ap/J98op2zGcNRe6PqMqT9OdL9vHxA/KESg+6Ls3JoC3tm/gsq9jDjw9FFGhqTPRiNHOZz6OPRAxgSDPrvbm1xg9so8B/rFXu5sH/ypkWEVcxkv10eDRiJquJluJCBTK59TH5RWtGsZAyj2uLy5+wljIi+vI+R7dSg2hJBq0ZzhSmM4MTW0e8xVAynhQHbmsTMZkPnjLfbOTmdhHdeJNbno74D1XaGzvvc0oHY2UBr3SUh3zPT/zKTZ1ch09Zwk72dFZDu8g7anMPQ1jHkq5yjFLPRFzG5YyiQbeIxjzFaJp0dz1GU/VZMWcKWSxnP8Of004PeUAbOvIiC9nETf8zf4ygZKbr8SxT+I6Nh157XgzoyBYKvhBxMV+zmlEvqg0vRdzBJT3NAYso08s1KfGyOWZlH3X9FeNkCt9Sr694MpKyr6qztGEDw17Tf9ZT8vWIujTlLQq+EXEpPVnLMf3kAc+xiWvejBjDNp7sL2Z8xX4KDhBjHnkrYjBHDTR+lnH1265BqXf0hfk8/q7cGqQ9ttBysPZYyAND5A5d6UaRoeLEx3R73+vD1Ecu/8Ac04lM+jCAEUxlB0WHq5G0oD8PjFCDWMB+ThhpLtjCTsqOcr+iGnMp+6FaSUMGMPkjdW50xOucOSaiGR2ZwSZeGStH2UXRcdYlHSg43r2Cs3mQjvzBsRPMKdW5jAY0JIM/Jqo7H8tDJjKVFVz9SUQtGrKCDZSd5BmBxszjT677VG3+zE/a0YlejKXkZPcDMlhHNlWniCfT2UCxqdYmB2n8uTFT7AvjoRMHKTzN72RTZrr1w25KfiluZPICv3KAqTNcd6a4sIPOXxkbz8/yPqd8rT0GfBMxkGl8Q7XZEffNiWjFMBZz47cRNeeqQ3SmK99z3LyIe5lEye+0yWzmcc58a4yB33t+4Ygs9xhKU512nLvAXHA9wxhOs4XGwnuL5AXXLfaMx3Z6/6A+cPwSdYeRfMczS90b2MeNP7pFkU3ZZWod1elGL6Yyk9lcvjziCrZxxE/ynzo0ojHdeIHZlF0hD7mYT6j0c8ReDnLxSvGhNvXIpvIq16UOT7GAT1abxzW+yyROWmtsvMkNvxg3e7lknX5Ri05k8ivHrbe2OZ17Gc+nfE42j21wLd5nKo03+slWGm0yd2xgC0U36yO30Z2RfEPxLdplNevptFW9I5u+v3qPaRT4zb2cbE7YZvw8znDmU3K7POcK6tKAh+lI8R3qE7fwMB8zeKd7Adftkv+MYy01sq3p3RFnceIe7bGGAzy919g5Y585p9rv+kEjOvM2M7hgvzz8Qx5S6E9tUYnKtOUphjKMOWTxM39y4gF95z56HYy4Mqku0IJE8q9/iUP/kof///XH4VeTh19I/vXm4RcSydx3c9/O+/hfH0q9ksj5ViLngzlXyG0/9dvh7yRT38y5VE5TOV/J/czh3xK530jmtpzbp8PtHn41mdN2brs5o8sZQO6Ac0ef83vq+7lN53YjNYZkWkfyuphzxfSPJBKpvuRrK5EKaKrjaZ9Ipg0kp1dpUUq9ktde3nVyL5XMF5/cGUjmeyWZHpS8USXz5iaZajoVmrTRp2VLvngnEsm/pVLu1VMv58UvmcgXxdSEJNPzKjWotClKz6z0yU1NdypeuVOZO7WpDqRSOnfc6VOf05KHp+iTcKdmMKP5ijPs1M+kBWMpZLc+k52UtEu/njYs5ndOtGNvxFL2UtluPZMXyeZPKtu596CwnXspTuU1tnCknfyzZPPhUe6CrGUfV9ndb+dOu/u2LKWGnf1lPM0YCtjhn8+ttLTD784S9lPdLr8uD9GB/pS1469BTy61w29ME9rxPAMYTGk7/wq0pzNvsZwhJT2t2P1X5C4a8Rn7qFxKtWIAC0vbSdH+JHcBVuecCpzFGL5kOyefoioymspl3Glowkes4biy+kkX+nJvOU92/E6B8iolgyhzqutxzWnuGtzDNJbzC1tpfLqnBNrTgQl8wXRms5yTK4gNx50h/tzOGNaxmRIVxZ/HeIIMXqLwv92hKcY9PMmbLOOOM8WeKUyl7FnGXMmYKF/ZHZHFVKziNVZw7dlyhM+pdI72GMhc1rKVow6dRDCdRQw8z5j5k8zzzRkz+Jk6F0S8ylu8zWg2soV6F0a8w+ccpPFFEf9lMvPpfbG5IqrKNW6mXDWxpiUd+IaVVLjEXZAudKUv09jBi9XdTfmB5TXElIcutZ7ozkBGMZM5lL1M/vI2w5nKQY69POJ+WvMFezjvCk9X1KdmTXnCRh660rhYxQaKX2Wt0Jvx7KLl1XLn0CnINfrBe3xOkvNqmW8acD8P04aN7KHctdYBbelMb95jLJPJ4ieOv07toDajePB6NYHJLKD+DeLH8BvlCc3qeMqlC8+xhWPqqh3UZC/Db5Jz9cwVLelCsZvNJSspc4t6xnnUZAIz2MNeqtwqvjSjN6/zVH016nZzxvF3mIMG1iTP0OLOiJepdZd8pFxDuc4EjmhkzXAXY9jGuLvFnWaNPamxgDL3GCszOf1eT1TU4wBF7lN76cs0it0fMYtaD9jRUv5Ba4RNJKnXRLs8y0h+4p6HrFPOaarmsJvnHxYrZrCUro9YE6yifDM7Mu6nPWNZSdXm6ihj/mNM/5VPdOIAxVpYBxR4VP/pykt8y+ktrV2a0o936PqYXQfHPC5mNKf0E+aXTGplyGNqtRJ/tnJdazlHK3oykaWsotyTagOjmM12DlC8jX7RhEEUbiuWDGME97WzDin9tJrDDEq19zqdeZvxTGAFZTvoExl0ZjrFOuobtalLdmfX6WIuaMI8VlCiq5iwnULPyF920aSbPrOCVaxjC1W6ixvnPCsPn1PzeqgzvEDF/8lhfmQLDTPVEXZwwgsRF/IQ+3j+RbWIpRz1krpLwZ7GQFt68CMnegDI4Ed2cOPL2qUdr/M7z/c2/4yjQB/9oD5PsJQmr8ifvnZ57OLNV+UhV77md9bQ+nX1gG5vqH9MYharubafmHLEm+ozc0j0dx2+YBVFB9iJcT89eZ2xzKHEW+abqgPdh3mXueyh9dvGS5L678iXd92HWcyFg9zXeI0vKfKetniHL6gxWC7TdIjnI4byLWs5cqhYvy9XqDJMLeVrNlL1A5/nvOFyjt3UHCFXWM+/RqoRlOQ++rCIxVQe5T7PILZykDc+FJuP5CjvEKPdSzlvjM8xjJpj1T92s59Lx1k3fMZMrh0vphPEj1oTrQdWsp/CH8sxWtGDVWylxifaJsmtk9RGTvrUvPLoZ+LBdDInW0dUmCLGzGAX46bKCRZyyefmhlFspeIX+sK8aZ4NOHq6OksG/XmX4XzIeOZQ70u1ie0UmiHnqDtTXWE0y/jXV95nwizj/Nq4eJS6s60HfueoOX5y6rdqGk/Sjf5cPdda41W2cfI8caXUd8w3fhoxnSyWsIeTvzcfvExv+tCfQXzP+CxrjLX0X2BeWMkejl5onVOf8Wzg+EX6wht8whIKLrYj5y4yWMWrP+gnuzhhidrHhTzFBrKptjTiJrrw3o/y4dDpy7KIu2lOD2bRe7mYM40qP8kRZrCMq1aIP11Yw+6f3W9Wqqccv8o6YiJdVltL1Fmjjwym/NqIW/iO63/xGQqts0YYxiR+I0mR9dpiKIuovyHif3xI4Y3qFWdw2SbPZxTcIjd5lhcYQautEaf86vmbycxgOdV+8z2a8xxbqL5NO/RjEZt5abt8puIO9ZCe9GMEP7CUtYzcKac4Itt3GMYEbtgdcdEe8WU/xfda17zMAIrsEwf6s55av6uvtNuvhtDkD/eXP9VxGh7Q1kFt0YpsLk7KZ347dLqRtn1POylI5O1tE4n0vW4ib2Ofd6yQSDsiSKRvWNPONPJa/sfRR+oUJe/cJLWrz/tK3hY7mfbuP/5MpnbWaccOqaOYtCOPvH1x6mwj7XgimdrVpx/EJPM+lsx3MJN2IJN+nJBIO7/JHUreKUEi/wlP2tlNvgOJ/JHNPURJjTTvTCKZNhXJ1GFLMu2E6R9nRH87D8oNcPqLqUby50LaoVDauUHaCU3+xtLOcNLTI3UOk+p52hFS+sFIWo4kk+lJku9f3mznz72007i8vqRlYiL9gOtvOZHMjWDaiNLTM+1AL5U3eScm+YLpr/8DjXvDZA=='


def _apply_ulp(base, idx, delta):
    i = base.ravel().view(np.int32).astype(np.int64)
    m = np.where(i < 0, -2147483648 - i, i)
    m[idx] += delta.astype(np.int64)
    i2 = np.where(m >= 0, m, -2147483648 - m).astype(np.int32)
    out = i2.view(np.float32).reshape(base.shape)
    return out


def rope_tables(S):
    import zlib, base64
    L = 2048
    th32 = np.array([int(h, 16) for h in THETA_BITS],
                    dtype=np.uint32).view(np.float32)
    ang32 = (np.arange(L, dtype=np.float32)[:, None] * th32[None, :])
    base_c = np.cos(ang32.astype(np.float64)).astype(np.float32)
    base_s = np.sin(ang32.astype(np.float64)).astype(np.float32)
    raw = zlib.decompress(base64.b64decode(ROPE_FIX_B64))
    nc_, ns_ = np.frombuffer(raw[:8], dtype=np.uint32)
    o = 8
    idx_c = np.frombuffer(raw[o:o + 4 * nc_], dtype=np.uint32).astype(np.int64)
    o += 4 * nc_
    del_c = np.frombuffer(raw[o:o + nc_], dtype=np.int8)
    o += int(nc_)
    idx_s = np.frombuffer(raw[o:o + 4 * ns_], dtype=np.uint32).astype(np.int64)
    o += 4 * ns_
    del_s = np.frombuffer(raw[o:o + ns_], dtype=np.int8)
    cos32 = _apply_ulp(base_c, idx_c, del_c)
    sin32 = _apply_ulp(base_s, idx_s, del_s)
    jj = np.arange(128) % 32
    ropec = np.ascontiguousarray(cos32.T[jj, :S])
    ropes = np.ascontiguousarray(sin32.T[jj, :S])
    return ropec, ropes


def host_consts(S):
    identf = np.eye(128, dtype=np.float32)
    identb = np.eye(128).astype(ml_dtypes.bfloat16)
    onesf = np.ones((128, 128), dtype=np.float32)
    e64row = np.zeros((128, 128), dtype=np.float32)
    e64row[64, :] = 1.0
    bdiag = np.zeros((128, 14), dtype=np.float32)
    for j in range(4):
        bdiag[32 * j:32 * (j + 1), j] = 1.0
    for j in range(2):
        bdiag[64 * j:64 * (j + 1), 4 + j] = 1.0
    for par in range(2):
        for j in range(2):
            bdiag[64 * j:64 * (j + 1), 6 + 4 * par + 2 * par + j] = 1.0
    ropec, ropes = rope_tables(S)
    sevenv = np.full((128, 1), 7.7, dtype=np.float32)
    return dict(identf=identf, identb=identb, onesf=onesf, bdiag=bdiag,
                e64row=e64row, ropec=ropec, ropes=ropes, sevenv=sevenv)


def make_inputs(x, w_qkv, w_proj, b_proj, Wq, bq, Wk, bk, Wv, bv, Wo, bo,
                w_gate, w_up, w_down, ncores=8):
    B, S, _ = x.shape
    nseq = B // ncores
    consts = host_consts(S)
    common = dict(
        w_qkv=np.ascontiguousarray(w_qkv, dtype=np.float32),
        w_proj=np.ascontiguousarray(w_proj, dtype=np.float32),
        b_proj=np.ascontiguousarray(b_proj, dtype=np.float32).reshape(1, D),
        Wq=np.ascontiguousarray(Wq, dtype=np.float32),
        Wk=np.ascontiguousarray(Wk, dtype=np.float32),
        Wv=np.ascontiguousarray(Wv, dtype=np.float32),
        Wo=np.ascontiguousarray(Wo, dtype=np.float32),
        bq=np.ascontiguousarray(bq, dtype=np.float32).reshape(1, D),
        bk=np.ascontiguousarray(bk, dtype=np.float32).reshape(1, D),
        bv=np.ascontiguousarray(bv, dtype=np.float32).reshape(1, D),
        bo=np.ascontiguousarray(bo, dtype=np.float32).reshape(1, D),
        w_gate=np.ascontiguousarray(w_gate, dtype=np.float32),
        w_up=np.ascontiguousarray(w_up, dtype=np.float32),
        w_down=np.ascontiguousarray(w_down, dtype=np.float32),
        **consts)
    in_maps = []
    for c in range(ncores):
        xin = np.ascontiguousarray(
            np.asarray(x[c * nseq:(c + 1) * nseq], dtype=np.float32
                       ).reshape(nseq * S, D))
        in_maps.append(dict(common, xin=xin))
    return in_maps, nseq


def kernel(L0_attn_a_x, L0_attn_a_w_qkv, L0_attn_a_w_proj, L0_attn_a_b_proj,
           L0_attn_b_Wq, L0_attn_b_bq, L0_attn_b_Wk, L0_attn_b_bk,
           L0_attn_b_Wv, L0_attn_b_bv, L0_attn_b_Wo, L0_attn_b_bo,
           L0_ffn_w_gate, L0_ffn_w_up, L0_ffn_w_down):
    x = np.asarray(L0_attn_a_x)
    B, S, _ = x.shape
    ncores = 8
    in_maps, nseq = make_inputs(
        x, np.asarray(L0_attn_a_w_qkv), np.asarray(L0_attn_a_w_proj),
        np.asarray(L0_attn_a_b_proj), np.asarray(L0_attn_b_Wq),
        np.asarray(L0_attn_b_bq), np.asarray(L0_attn_b_Wk),
        np.asarray(L0_attn_b_bk), np.asarray(L0_attn_b_Wv),
        np.asarray(L0_attn_b_bv), np.asarray(L0_attn_b_Wo),
        np.asarray(L0_attn_b_bo), np.asarray(L0_ffn_w_gate),
        np.asarray(L0_ffn_w_up), np.asarray(L0_ffn_w_down), ncores)
    b = Builder(nseq=nseq, S=S, nsub=4)
    nc = b.emit_program()
    res = run_bass_kernel_spmd(nc, in_maps, core_ids=list(range(ncores)))
    out = np.empty((B, S, D), dtype=np.float32)
    for c in range(ncores):
        out[c * nseq:(c + 1) * nseq] = np.asarray(
            res.results[c]["out"]).reshape(nseq, S, D)
    return out


# version 08
# speedup vs baseline: 1.1251x; 1.0091x over previous
import numpy as np
import ml_dtypes
import concourse.bass as bass
import concourse.mybir as mybir
from concourse.bass_utils import run_bass_kernel_spmd

F32 = mybir.dt.float32
BF16 = mybir.dt.bfloat16
FP8 = mybir.dt.float8e4
AF = mybir.ActivationFunctionType
ALU = mybir.AluOpType
AXX = mybir.AxisListType.X

C32 = float(np.float32(1.5 * 2.0 ** 23))
C64 = float(np.float32(1.5 * 2.0 ** 52))
EPS = 1e-5
D = 1024
H = 16
HD = 64
NCH = 8
INNER = 4096

ENGINES = ("sync", "tensor", "vector", "scalar", "gpsimd")
_ESEM = {"sync": "dml", "tensor": "te", "vector": "ve",
         "scalar": "se", "gpsimd": "ge"}


class Sched:

    def __init__(self, nc, sems):
        self.nc = nc
        self.sems = sems
        self.count = {k: 0 for k in sems}
        self.items = {e: [] for e in ENGINES}
        self.last_wait = {(e, s): 0 for e in ENGINES for s in sems}
        self.mm_mode = None

    def sem(self, name):
        if name not in self.sems:
            self.sems[name] = self.nc.alloc_semaphore(name)
            self.count[name] = 0
            for e in ENGINES:
                self.last_wait[(e, name)] = 0
        return name

    def mark(self, name):
        return self.count[name]

    def wait_val(self, eng, name, v):
        if v <= self.last_wait[(eng, name)]:
            return
        self.last_wait[(eng, name)] = v
        self.raw(eng, lambda e, semh=self.sems[name], vv=v: e.wait_ge(semh, vv))

    def emit(self, eng, fn, sem=None, n=1):
        s = sem or _ESEM[eng]
        semh = self.sems[s]
        self.count[s] += n

        def run(e, fn=fn, semh=semh, n=n):
            fn(e).then_inc(semh, n)
        self.items[eng].append(run)

    def raw(self, eng, fn):
        self.items[eng].append(fn)

    def dma(self, out_fn, in_fn, eng="sync", sem="dml"):
        if sem != "dml":
            self.sem(sem)
        self.emit(eng, lambda e, o=out_fn, i=in_fn: e.dma_start(out=o(), in_=i()),
                  sem=sem, n=16)

    def dma_store(self, out_fn, in_fn, eng="sync"):
        self.emit(eng, lambda e, o=out_fn, i=in_fn: e.dma_start(out=o(), in_=i()),
                  sem="dms", n=16)

    def wait_all(self, eng, *sem_names):
        for s in sem_names:
            v = self.count[s]
            if v <= self.last_wait[(eng, s)]:
                continue
            if s in ("dml", "dms") and eng != "sync":
                if v > self.last_wait[("sync", s)]:
                    self.last_wait[("sync", s)] = v
                    self.raw("sync", lambda e, semh=self.sems[s], vv=v:
                             e.wait_ge(semh, vv))
            self.last_wait[(eng, s)] = v
            self.raw(eng, lambda e, semh=self.sems[s], vv=v: e.wait_ge(semh, vv))

    def mm(self, out_fn, lhsT_fn, rhs_fn, kdim, mdim, start, stop,
           tile_position=None):
        kk = 32 if kdim <= 32 else (64 if kdim <= 64 else 128)
        mm_ = 32 if mdim <= 32 else (64 if mdim <= 64 else 128)
        sig = (kk, mm_)
        if self.mm_mode is not None and self.mm_mode != sig:
            self.raw("tensor", lambda e: e.drain())
        self.mm_mode = sig
        self.emit("tensor",
                  lambda e, o=out_fn, l=lhsT_fn, r=rhs_fn, st=start, sp=stop,
                  tp=tile_position: e.matmul(o(), l(), r(), start=st, stop=sp,
                                             tile_position=tp))

    def transpose(self, out_fn, in_fn, id_fn):
        if self.mm_mode is not None and self.mm_mode != ("T",):
            self.raw("tensor", lambda e: e.drain())
        self.mm_mode = ("T",)
        self.emit("tensor", lambda e, o=out_fn, i=in_fn, d=id_fn:
                  e.transpose(o(), i(), d()))

    def replay(self, eng, e):
        for it in self.items[eng]:
            it(e)


def _c32b(x):
    return (x + 31) // 32 * 32


class Builder:
    def __init__(self, nseq=2, S=2048, nsub=4):
        self.nseq, self.S, self.nsub = nseq, S, nsub
        self.NT = S // 128
        self.QCH = min(512, S)
        self.NQC = S // self.QCH
        self.KT = S // 128
        self.SWEEP = min(512, S)
        self.NSW = S // self.SWEEP
        self.SWT = self.SWEEP // 128
        self.nc = bass.Bass("TRN2", target_bir_lowering=False)
        self._declare_dram()
        self._declare_sbuf()
        self.sems = {k: self.nc.alloc_semaphore(k) for k in
                     ("dml", "dms", "te", "ve", "se", "ge")}
        self.S_ = Sched(self.nc, self.sems)

    def _declare_dram(self):
        nc, S, nseq = self.nc, self.S, self.nseq
        dr = {}

        def din(name, shape, dt=F32):
            dr[name] = nc.dram_tensor(name, shape, dt, kind="ExternalInput").ap()

        din("xin", [nseq * S, D])
        din("w_qkv", [D, 3 * D]); din("w_proj", [D, D]); din("b_proj", [1, D])
        din("Wq", [D, D]); din("Wk", [D, D]); din("Wv", [D, D]); din("Wo", [D, D])
        din("bq", [1, D]); din("bk", [1, D]); din("bv", [1, D]); din("bo", [1, D])
        din("w_gate", [INNER, D]); din("w_up", [INNER, D]); din("w_down", [D, INNER])
        din("identf", [128, 128]); din("identb", [128, 128], BF16)
        din("e64row", [128, 128])
        din("onesf", [128, 128]); din("bdiag", [128, 14])
        din("ropec", [128, S]); din("ropes", [128, S]); din("sevenv", [128, 1])
        dr["out"] = nc.dram_tensor("out", [nseq * S, D], F32,
                                   kind="ExternalOutput").ap()
        dr["xs"] = nc.dram_tensor("xs", [S, D], F32).ap()
        dr["qTs"] = nc.dram_tensor("qTs", [D, S], F32).ap()
        dr["TgS"] = nc.dram_tensor("TgS", [D, INNER], FP8).ap()
        dr["TuS"] = nc.dram_tensor("TuS", [D, INNER], FP8).ap()
        dr["TdS"] = nc.dram_tensor("TdS", [INNER, D], FP8).ap()
        for w in ("WqT", "WkT", "WvT", "WoT"):
            dr[w] = nc.dram_tensor(w, [D, D], F32).ap()
        self.dr = dr

    def _declare_sbuf(self):
        nc = self.nc
        S, NT, SWEEP, QCH = self.S, self.NT, self.SWEEP, self.QCH
        self.ARENA = 204 * 1024
        arena = nc.alloc_sbuf_tensor("arena", [128, self.ARENA // 4], F32)
        try:
            base = nc.lookup_mloc(arena).addr
        except Exception:
            base = 20480
        sb = {}
        off = [0]

        def place(name, shape, dt, at=None):
            if at is None:
                at = off[0]
                nbytes = int(np.prod(shape[1:])) * mybir.dt.size(dt)
                off[0] = _c32b(at + nbytes)
            sb[name] = nc.alloc_sbuf_tensor_at(name, shape, dt, offset=base + at)
            return at

        place("identf", [128, 128], F32)
        place("identb", [128, 128], BF16)
        place("onesf", [128, 128], F32)
        place("bdiag", [128, 14], F32)
        place("bbproj", [128, D], F32)
        place("bbo", [128, D], F32)
        place("bvb", [128, D], F32)
        place("biasqkv", [128, 16], F32)
        place("negc", [128, 16], F32)
        place("gad", [128, 12], F32)
        place("sstats", [128, 128], F32)
        place("snorm", [128, 64], F32)
        place("brow", [128, 1024], F32)
        place("e64row", [128, 128], F32)
        ph = off[0]
        self.PERSIST = ph
        kT_at = ph
        Vb_at = kT_at + NCH * S * 4
        yfm_at = Vb_at + NT * (H * HD + H) * 4
        wbuf_at = yfm_at + NCH * SWEEP * 4
        stage_at = wbuf_at + 16 * 1024
        attn_end = stage_at + 8 * 1024
        assert attn_end <= self.ARENA, f"attn {attn_end}"
        place("kT", [128, NCH * S], F32, kT_at)
        place("Vb", [128, NT * (H * HD + H)], F32, Vb_at)
        place("yfm", [128, NCH * SWEEP], F32, yfm_at)
        place("wbuf", [128, 4096], F32, wbuf_at)
        place("stage", [128, 2048], F32, stage_at)
        obuf_at = yfm_at
        abuf_at = obuf_at + NCH * QCH * 4
        assert abuf_at + 8 * QCH * 4 <= stage_at, "attn overlay"
        place("qbuf", [128, 2 * QCH], F32, stage_at)
        place("obuf", [128, NCH * QCH], F32, obuf_at)
        place("abuf", [128, 8 * QCH], F32, abuf_at)
        place("xpb", [128, 512], F32, abuf_at)
        place("wp", [128, 1024], F32, abuf_at + 512 * 4)
        tg_at = ph
        tu_at = tg_at + NCH * INNER
        td_at = tu_at + NCH * INNER
        hb_at = _c32b(td_at + 32 * D)
        silu_at = hb_at + INNER * 4
        q2t_at = silu_at + INNER * 4
        q2f_at = q2t_at + INNER * 2
        q1t_at = q2f_at + INNER * 2
        q1f_at = q1t_at + D * 2
        fx_at = _c32b(q1f_at + D * 2)
        assert fx_at + 2 * D * 4 <= self.ARENA, "ffn"
        place("tg", [128, NCH * INNER], FP8, tg_at)
        place("tu", [128, NCH * INNER], FP8, tu_at)
        place("td", [128, 32 * D], FP8, td_at)
        place("hb", [128, INNER], F32, hb_at)
        place("silu", [128, INNER], F32, silu_at)
        place("q2t", [128, INNER], BF16, q2t_at)
        place("q2f", [128, INNER], BF16, q2f_at)
        place("q1t", [128, D], BF16, q1t_at)
        place("q1f", [128, D], BF16, q1f_at)
        place("fx", [128, 2 * D], F32, fx_at)
        pw_at = ph
        ptb_at = pw_at + 8 * 1024 * 4
        pst_at = ptb_at + 32 * 1024 * 2
        assert pst_at + 4096 * 4 <= self.ARENA, "prep"
        place("pw", [128, 8 * 1024], F32, pw_at)
        place("ptb", [128, 32 * 1024], BF16, ptb_at)
        place("pst", [128, 4096], F32, pst_at)
        place("pst8", [128, 4096], FP8, pst_at)
        self.sb = sb
        self.pb = [nc.alloc_psum_tensor(f"pb{i}", [128, 512], F32)
                   for i in range(7)]
        self.pbb = nc.alloc_psum_tensor("pbb", [128, 1024], BF16)

    def a(self, name, rows=None, cols=None):
        ap = self.sb[name].ap()
        r = rows or (0, 128)
        if cols is None:
            return ap[r[0]:r[1], :]
        return ap[r[0]:r[1], cols[0]:cols[1]]

    def col(self, name, c):
        return self.a(name, cols=(c, c + 1))

    def mkap(self, name, col_off, dims):
        ap0 = self.sb[name].ap()
        return bass.AP(ap0.tensor, ap0.offset + col_off,
                       [[ap0.ap[0][0], 128]] + [list(d) for d in dims])

    def emit_gadget(self):
        S_ = self.S_
        S_.emit("vector", lambda e: e.memset(self.a("brow"), 0.0))
        S_.emit("vector", lambda e: e.memset(self.col("gad", 4), EPS))
        S_.emit("vector", lambda e: e.memset(self.col("gad", 5), 0.125))
        S_.dma(lambda: self.col("sstats", 0), lambda: self.dr["sevenv"])
        S_.wait_all("vector", "dml")
        S_.emit("vector", lambda e: e.tensor_scalar(
            self.col("gad", 2), self.col("sstats", 0), C64, -C64,
            op0=ALU.add, op1=ALU.add))
        S_.wait_all("vector", "ve")
        S_.emit("vector", lambda e: e.tensor_scalar(
            self.col("gad", 3), self.col("gad", 2), 0.125, None, op0=ALU.mult))
        S_.wait_all("vector", "ve")
        S_.emit("vector", lambda e: e.tensor_scalar(
            self.col("gad", 0), self.col("gad", 3), C64 - C32, C32,
            op0=ALU.mult, op1=ALU.add))
        S_.wait_all("vector", "ve")
        S_.emit("vector", lambda e: e.tensor_scalar(
            self.col("gad", 1), self.col("gad", 0), -1.0, None, op0=ALU.mult))

    def emit_consts(self):
        S_ = self.S_
        for n in ("identf", "identb", "onesf", "bdiag", "e64row"):
            S_.dma(lambda n=n: self.a(n), lambda n=n: self.dr[n])
        for i, bsrc in enumerate(("bq", "bk")):
            S_.dma(lambda i=i: self.a("pst", rows=(0, 8),
                                      cols=(i * 128, i * 128 + 128)),
                   lambda b=bsrc: self.dr[b][0:1, :].rearrange(
                       "a (p c) -> (a p) c", c=128))
        for i, bsrc in enumerate(("bv", "b_proj", "bo")):
            S_.dma(lambda i=i: self.a("pst", rows=(0, 1),
                                      cols=(1024 * (i + 1), 1024 * (i + 2))),
                   lambda b=bsrc: self.dr[b][0:1, :])
        S_.wait_all("tensor", "dml")
        for i in range(2):
            S_.transpose(lambda: self.pb[1].ap()[:, 0:8],
                         lambda i=i: self.a("pst", rows=(0, 8),
                                            cols=(i * 128, i * 128 + 128)),
                         lambda: self.a("identf", rows=(0, 8), cols=(0, 8)))
            S_.wait_all("vector", "te")
            S_.emit("vector", lambda e, i=i: e.tensor_copy(
                self.a("biasqkv", cols=(8 * i, 8 * i + 8)),
                self.pb[1].ap()[:, 0:8]))
            S_.wait_all("tensor", "ve")
        for dst, i in [("bvb", 1), ("bbproj", 2), ("bbo", 3)]:
            for hf in range(2):
                S_.mm(lambda: self.pb[0].ap()[:, 0:512],
                      lambda: self.a("onesf", rows=(0, 1)),
                      lambda i=i, hf=hf: self.a(
                          "pst", rows=(0, 1),
                          cols=(1024 * i + 512 * hf, 1024 * i + 512 * (hf + 1))),
                      kdim=1, mdim=128, start=True, stop=True)
                S_.wait_all("vector", "te")
                S_.emit("vector", lambda e, d=dst, hf=hf: e.tensor_copy(
                    self.a(d, cols=(512 * hf, 512 * (hf + 1))),
                    self.pb[0].ap()[:, 0:512]))
                S_.wait_all("tensor", "ve")

    def emit_prep_tquant(self, wname, R, C, dst, midx):
        S_ = self.S_
        RB = R // 128
        S_.wait_all("scalar", "te", "ve", "dms")
        for r in range(RB):
            sl = r % 2
            S_.wait_all("sync", "se")
            S_.dma(lambda sl=sl, C=C: self.a("pw", cols=(sl * C, sl * C + C)),
                   lambda r=r, w=wname: self.dr[w][128 * r:128 * (r + 1), :])
            S_.wait_all("scalar", "dml")
            S_.emit("scalar", lambda e, sl=sl, r=r, C=C: e.activation(
                self.a("pst", cols=(0, C)),
                self.a("pw", cols=(sl * C, sl * C + C)), AF.Abs,
                accum_out=self.col("snorm", r)))
        S_.wait_all("tensor", "se")
        S_.mm(lambda RB=RB: self.pb[0].ap()[0:1, 0:RB],
              lambda: self.a("onesf", cols=(0, 1)),
              lambda RB=RB: self.a("snorm", cols=(0, RB)),
              kdim=128, mdim=1, start=True, stop=True)
        S_.wait_all("vector", "te")
        S_.emit("vector", lambda e, RB=RB: e.tensor_copy(
            self.a("sstats", rows=(0, 1), cols=(40, 40 + RB)),
            self.pb[0].ap()[0:1, 0:RB]))
        S_.wait_all("vector", "ve")
        S_.emit("vector", lambda e, RB=RB: e.tensor_reduce(
            self.a("sstats", rows=(0, 1), cols=(72, 73)),
            self.a("sstats", rows=(0, 1), cols=(40, 40 + RB)),
            axis=AXX, op=ALU.add))
        S_.wait_all("vector", "ve")
        S_.emit("vector", lambda e, R=R, C=C: e.tensor_scalar(
            self.a("sstats", rows=(0, 1), cols=(73, 74)),
            self.a("sstats", rows=(0, 1), cols=(72, 73)),
            1.0 / (R * C), 1e-5, op0=ALU.mult, op1=ALU.max))
        S_.wait_all("vector", "ve")
        S_.emit("vector", lambda e: e.reciprocal(
            self.a("sstats", rows=(0, 1), cols=(74, 75)),
            self.a("sstats", rows=(0, 1), cols=(73, 74))))
        S_.wait_all("tensor", "ve")
        S_.mm(lambda: self.pb[0].ap()[:, 0:1],
              lambda: self.a("onesf", rows=(0, 1)),
              lambda: self.a("sstats", rows=(0, 1), cols=(74, 75)),
              kdim=1, mdim=128, start=True, stop=True)
        S_.wait_all("vector", "te")
        S_.emit("vector", lambda e: e.tensor_copy(
            self.col("sstats", 75), self.pb[0].ap()[:, 0:1]))
        S_.wait_all("tensor", "ve")
        S_.mm(lambda: self.pb[1].ap()[:, 0:1],
              lambda: self.a("onesf", rows=(0, 1)),
              lambda: self.a("sstats", rows=(0, 1), cols=(73, 74)),
              kdim=1, mdim=128, start=True, stop=True)
        S_.wait_all("vector", "te")
        S_.emit("vector", lambda e, midx=midx: e.tensor_copy(
            self.col("gad", 6 + midx), self.pb[1].ap()[:, 0:1]))
        for r in range(RB):
            sl = r % 2
            S_.wait_all("sync", "se", "ve")
            S_.dma(lambda sl=sl, C=C: self.a("pw", cols=(sl * C, sl * C + C)),
                   lambda r=r, w=wname: self.dr[w][128 * r:128 * (r + 1), :])
            S_.wait_all("scalar", "dml", "ve")
            S_.emit("scalar", lambda e, sl=sl, C=C: e.activation(
                self.a("pst", cols=(0, C)),
                self.a("pw", cols=(sl * C, sl * C + C)), AF.Identity,
                bias=self.col("gad", 0), scale=self.col("sstats", 75)))
            S_.wait_all("vector", "se")
            S_.emit("vector", lambda e, C=C: e.tensor_scalar(
                self.a("pst", cols=(C, 2 * C)) if 2 * C <= 4096 else
                self.a("pst", cols=(0, C)),
                self.a("pst", cols=(0, C)),
                self.col("gad", 1), -1.0, op0=ALU.add, op1=ALU.max))
            S_.wait_all("vector", "ve")
            src_cols = (C, 2 * C) if 2 * C <= 4096 else (0, C)
            S_.emit("vector", lambda e, r=r, C=C, sc=src_cols: e.tensor_scalar(
                self.a("ptb", cols=(r * C, r * C + C)),
                self.a("pst", cols=sc), 1.0, None, op0=ALU.min))
        S_.wait_all("tensor", "ve")
        for cc in range(C // 128):
            for rr in range(RB):
                S_.transpose(lambda: self.pbb.ap()[:, 0:128],
                             lambda rr=rr, cc=cc, C=C: self.a(
                                 "ptb", cols=(rr * C + cc * 128,
                                              rr * C + cc * 128 + 128)),
                             lambda: self.a("identb"))
                S_.wait_all("scalar", "te")
                S_.emit("scalar", lambda e, rr=rr: e.copy(
                    self.a("pst8", cols=(rr * 128, rr * 128 + 128)),
                    self.pbb.ap()[:, 0:128]))
                S_.wait_all("tensor", "se")
            S_.wait_all("sync", "se", "dms")
            S_.dma_store(lambda cc=cc, R=R, d=dst: self.dr[d][
                128 * cc:128 * (cc + 1), 0:R],
                lambda R=R: self.a("pst8", cols=(0, R)))
            S_.wait_all("scalar", "dms")

    def emit_prep_wt(self, wname, dst):
        S_ = self.S_
        S_.wait_all("sync", "te", "ve", "dms")
        for r in range(8):
            S_.dma(lambda r=r: self.a("pw", cols=(r * 1024, r * 1024 + 1024)),
                   lambda r=r, w=wname: self.dr[w][128 * r:128 * (r + 1), :])
        S_.wait_all("tensor", "dml")
        for cc in range(8):
            for rr in range(8):
                S_.transpose(lambda rr=rr: self.pb[rr % 2].ap()[:, 0:128],
                             lambda rr=rr, cc=cc: self.a(
                                 "pw", cols=(rr * 1024 + cc * 128,
                                             rr * 1024 + cc * 128 + 128)),
                             lambda: self.a("identf"))
                S_.wait_all("vector", "te")
                S_.emit("vector", lambda e, rr=rr: e.tensor_copy(
                    self.a("pst", cols=(rr * 128, rr * 128 + 128)),
                    self.pb[rr % 2].ap()[:, 0:128]))
                S_.wait_all("tensor", "ve")
            S_.wait_all("sync", "ve", "dms")
            S_.dma_store(lambda cc=cc, d=dst: self.dr[d][
                128 * cc:128 * (cc + 1), :],
                lambda: self.a("pst", cols=(0, 1024)))
            S_.wait_all("vector", "dms")

    def emit_ln(self, x_fn, y_fn, b=8):
        S_ = self.S_
        S_.wait_all("vector", "se", "ve")
        S_.emit("vector", lambda e, f=x_fn, b=b: e.bn_stats(
            self.a("sstats", cols=(b, b + 6)), f()[:, 0:512]))
        S_.emit("vector", lambda e, f=x_fn, b=b: e.bn_stats(
            self.a("sstats", cols=(b + 6, b + 12)), f()[:, 512:1024]))
        S_.wait_all("vector", "ve")
        S_.emit("vector", lambda e, b=b: e.bn_aggr(
            self.a("sstats", cols=(b + 12, b + 14)),
            self.a("sstats", cols=(b, b + 12))))
        S_.wait_all("scalar", "ve")
        S_.emit("scalar", lambda e, b=b: e.activation(
            self.col("sstats", b + 14), self.col("sstats", b + 13), AF.Sqrt,
            bias=self.col("gad", 4), scale=1.0))
        S_.wait_all("vector", "se")
        S_.emit("vector", lambda e, b=b: e.reciprocal(
            self.col("sstats", b + 15), self.col("sstats", b + 14)))
        S_.wait_all("vector", "ve")
        S_.emit("vector", lambda e, b=b: e.scalar_tensor_tensor(
            self.col("sstats", b + 16), self.col("sstats", b + 12), -1.0,
            self.col("sstats", b + 15), op0=ALU.mult, op1=ALU.mult))
        S_.wait_all("scalar", "ve", "te")
        S_.emit("scalar", lambda e, xf=x_fn, yf=y_fn, b=b: e.activation(
            yf(), xf(), AF.Identity,
            bias=self.col("sstats", b + 16), scale=self.col("sstats", b + 15)))

    def emit_actquant(self, in_fn, out_fn, a_fn):
        S_ = self.S_
        S_.wait_all("vector", "se", "ve")
        S_.emit("vector", lambda e, f=in_fn: e.tensor_reduce(
            self.col("sstats", 32), f(), axis=AXX, op=ALU.max,
            apply_absolute_value=True))
        S_.wait_all("vector", "ve")
        S_.emit("vector", lambda e: e.tensor_scalar(
            self.col("sstats", 33), self.col("sstats", 32), 1e-5, None,
            op0=ALU.max))
        S_.wait_all("vector", "ve")
        S_.emit("vector", lambda e: e.reciprocal(
            self.col("sstats", 34), self.col("sstats", 33)))
        S_.wait_all("vector", "ve")
        S_.emit("vector", lambda e: e.tensor_scalar(
            self.col("sstats", 35), self.col("sstats", 34), 127.0, None,
            op0=ALU.mult))
        S_.wait_all("scalar", "ve")
        S_.emit("scalar", lambda e, f=in_fn, af=a_fn: e.activation(
            af(), f(), AF.Identity,
            bias=self.col("gad", 0), scale=self.col("sstats", 35)))
        S_.wait_all("vector", "se")
        S_.emit("vector", lambda e, of=out_fn, af=a_fn: e.tensor_scalar(
            of(), af(), self.col("gad", 1), None, op0=ALU.add))

    def emit_attn(self, kind, xsrc_row0, xsrc, xdst, xdst_row0):
        S_ = self.S_
        S, SWEEP, NSW, SWT = self.S, self.SWEEP, self.NSW, self.SWT
        QCH, NQC, KT, NT = self.QCH, self.NQC, self.KT, self.NT
        rope = kind == "a"
        causal = kind == "b"

        S_.wait_all("vector", "te", "se", "ge", "dms")
        S_.emit("vector", lambda e: e.memset(
            self.mkap("Vb", HD, [[H * HD + H, NT], [HD + 1, H]]), 1.0))
        S_.emit("vector", lambda e: e.memset(self.a("snorm", cols=(32, 48)), 0.0))

        for sw in range(NSW):
            def ybuf(t4):
                name = "stage" if t4 % 2 == 0 else "wbuf"
                return name

            def gen_transposes(t4):
                S_.wait_all("tensor", "se")
                for grp in range(2):
                    S_.wait_all("tensor", "ve")
                    for cc in range(4):
                        c = grp * 4 + cc
                        S_.transpose(lambda grp=grp, cc=cc:
                                     self.pb[grp % 2].ap()[:, cc * 128:
                                                           cc * 128 + 128],
                                     lambda c=c, t4=t4: self.a(
                                         ybuf(t4), cols=(1024 + c * 128,
                                                         1024 + c * 128 + 128)),
                                     lambda: self.a("identf"))
                    S_.wait_all("vector", "te")
                    S_.emit("vector", lambda e, grp=grp, t4=t4: e.tensor_copy(
                        self.mkap("yfm", grp * 4 * SWEEP + t4 * 128,
                                  [[SWEEP, 4], [1, 128]]),
                        self.pb[grp % 2].ap()[:, 0:512]))

            for t4 in range(SWT):
                t = sw * SWT + t4
                if t4 >= 1:
                    gen_transposes(t4 - 1)
                S_.wait_all("sync", "se", "dms", "te")
                S_.dma(lambda t4=t4: self.a(ybuf(t4), cols=(0, 1024)),
                       lambda t=t: xsrc[xsrc_row0 + 128 * t:
                                        xsrc_row0 + 128 * (t + 1), :])
                S_.wait_all("vector", "dml")
                S_.wait_all("scalar", "dml")
                self.emit_ln(lambda t4=t4: self.a(ybuf(t4), cols=(0, 1024)),
                             lambda t4=t4: self.a(ybuf(t4), cols=(1024, 2048)),
                             b=8 + 32 * (t4 % 2))
            gen_transposes(SWT - 1)
            for vb in range(2):
                S_.wait_all("sync", "te", "ve")
                for c in range(NCH):
                    if kind == "a":
                        src = (lambda c=c, vb=vb: self.dr["w_qkv"][
                            128 * c:128 * (c + 1),
                            2 * D + vb * 512:2 * D + (vb + 1) * 512])
                    else:
                        src = (lambda c=c, vb=vb: self.dr["WvT"][
                            128 * c:128 * (c + 1), vb * 512:(vb + 1) * 512])
                    S_.dma(lambda c=c: self.a("wbuf",
                                              cols=(c * 512, c * 512 + 512)), src)
                S_.wait_all("tensor", "dml", "ve")
                for t4 in range(SWT):
                    t = sw * SWT + t4
                    bank = 4 + t4 % 2
                    for c in range(NCH):
                        S_.mm(lambda bank=bank: self.pb[bank].ap()[:, 0:512],
                              lambda c=c, t4=t4: self.a(
                                  "yfm", cols=(c * SWEEP + t4 * 128,
                                               c * SWEEP + t4 * 128 + 128)),
                              lambda c=c: self.a("wbuf",
                                                 cols=(c * 512, c * 512 + 512)),
                              kdim=128, mdim=128, start=(c == 0),
                              stop=(c == NCH - 1))
                    S_.wait_all("vector", "te")

                    def vevac(e, t=t, vb=vb, bank=bank, kind=kind):
                        o = t * (H * HD + H) + vb * 8 * (HD + 1)
                        dst = self.mkap("Vb", o, [[HD + 1, 8], [1, HD]])
                        sap = self.pb[bank].ap()
                        src_ = bass.AP(sap.tensor, sap.offset,
                                       [[sap.ap[0][0], 128], [HD, 8], [1, HD]])
                        if kind == "b":
                            bias = self.mkap("bvb", vb * 512, [[HD, 8], [1, HD]])
                            return e.tensor_tensor(dst, src_, bias, op=ALU.add)
                        return e.tensor_copy(dst, src_)
                    S_.emit("vector", vevac)
                    S_.wait_all("tensor", "ve")
            for which in ("q", "k"):
                base_col = 0 if which == "q" else D
                for g in range(4):
                    S_.wait_all("sync", "te", "ve", "se", "dms")
                    for c in range(NCH):
                        if kind == "a":
                            src = (lambda c=c, g=g, b=base_col: self.dr["w_qkv"][
                                128 * c:128 * (c + 1), b + 256 * g:b + 256 * (g + 1)])
                        else:
                            wsc = "WqT" if which == "q" else "WkT"
                            src = (lambda c=c, g=g, w=wsc: self.dr[w][
                                128 * c:128 * (c + 1), 256 * g:256 * (g + 1)])
                        S_.dma(lambda c=c: self.a("wbuf",
                                                  cols=(c * 256, c * 256 + 256)), src)
                    if rope:
                        S_.dma(lambda: self.a("wbuf", cols=(2048, 2048 + SWEEP)),
                               lambda sw=sw: self.dr["ropec"][
                                   :, sw * SWEEP:(sw + 1) * SWEEP])
                        S_.dma(lambda: self.a("wbuf",
                                              cols=(2048 + SWEEP, 2048 + 2 * SWEEP)),
                               lambda sw=sw: self.dr["ropes"][
                                   :, sw * SWEEP:(sw + 1) * SWEEP])
                    S_.wait_all("tensor", "dml")
                    for par in range(2):
                        m = 2 * g + par
                        for c in range(NCH):
                            if kind == "a":
                                lhs = (lambda c=c, par=par: self.mkap(
                                    "wbuf", c * 256 + par, [[2, 128]]))
                            else:
                                lhs = (lambda c=c, par=par: self.a(
                                    "wbuf", cols=(c * 256 + par * 128,
                                                  c * 256 + par * 128 + 128)))
                            S_.mm(lambda: self.pb[6].ap()[:, 0:SWEEP],
                                  lhs,
                                  lambda c=c: self.a(
                                      "yfm", cols=(c * SWEEP, (c + 1) * SWEEP)),
                                  kdim=128, mdim=128, start=(c == 0),
                                  stop=(c == NCH - 1))
                        if kind == "b":
                            S_.wait_all("scalar", "te", "ve", "dms")
                            S_.emit("scalar", lambda e, par=par, m=m, w=which:
                                    e.activation(
                                        self.a("stage", cols=(par * SWEEP,
                                                              (par + 1) * SWEEP)),
                                        self.pb[6].ap()[:, 0:SWEEP], AF.Identity,
                                        bias=self.col("biasqkv",
                                                      (0 if w == "q" else 8) + m),
                                        scale=1.0))
                            S_.wait_all("tensor", "se")
                        else:
                            S_.wait_all("vector", "te", "dms")
                            S_.emit("vector", lambda e, par=par: e.tensor_copy(
                                self.a("stage", cols=(par * SWEEP,
                                                      (par + 1) * SWEEP)),
                                self.pb[6].ap()[:, 0:SWEEP]))
                            S_.wait_all("tensor", "ve")
                    qe = lambda: self.a("stage", cols=(0, SWEEP))
                    qo = lambda: self.a("stage", cols=(SWEEP, 2 * SWEEP))
                    if rope:
                        cosv = lambda: self.a("wbuf", cols=(2048, 2048 + SWEEP))
                        sinv = lambda: self.a("wbuf", cols=(2048 + SWEEP,
                                                            2048 + 2 * SWEEP))
                        tA = lambda: self.a("wbuf", cols=(3072, 3072 + SWEEP))
                        tB = lambda: self.a("wbuf", cols=(3072 + SWEEP,
                                                          3072 + 2 * SWEEP))
                        re_ = lambda: self.a("stage", cols=(1024, 1024 + SWEEP))
                        ro_ = lambda: self.a("stage", cols=(1024 + SWEEP,
                                                            1024 + 2 * SWEEP))
                        S_.wait_all("vector", "se", "dml", "te", "ve")
                        S_.emit("vector", lambda e: e.tensor_tensor(
                            tA(), qe(), cosv(), op=ALU.mult))
                        S_.emit("vector", lambda e: e.tensor_tensor(
                            tB(), qo(), sinv(), op=ALU.mult))
                        S_.wait_all("vector", "ve")
                        S_.emit("vector", lambda e: e.tensor_tensor(
                            re_(), tA(), tB(), op=ALU.subtract))
                        S_.wait_all("vector", "ve")
                        S_.emit("vector", lambda e: e.tensor_tensor(
                            tA(), qe(), sinv(), op=ALU.mult))
                        S_.emit("vector", lambda e: e.tensor_tensor(
                            tB(), qo(), cosv(), op=ALU.mult))
                        S_.wait_all("vector", "ve")
                        S_.emit("vector", lambda e: e.tensor_tensor(
                            ro_(), tA(), tB(), op=ALU.add))
                        outE, outO = re_, ro_
                    else:
                        outE, outO = qe, qo
                    S_.wait_all("scalar", "ve", "te", "se")
                    S_.emit("scalar", lambda e, f=outE: e.activation(
                        self.a("wbuf", cols=(3072, 3072 + SWEEP)), f(),
                        AF.Square))
                    S_.emit("scalar", lambda e, f=outO: e.activation(
                        self.a("wbuf", cols=(3072 + SWEEP, 3072 + 2 * SWEEP)),
                        f(), AF.Square))
                    S_.wait_all("tensor", "se")
                    ncol = (32 if which == "q" else 40) + g
                    for par in range(2):
                        bc = (0, 4) if kind == "a" else (6 + 4 * par, 10 + 4 * par)
                        S_.mm(lambda: self.pb[3].ap()[0:4, 0:SWEEP],
                              lambda bc=bc: self.a("bdiag", cols=bc),
                              lambda par=par: self.a(
                                  "wbuf", cols=(3072 + par * SWEEP,
                                                3072 + (par + 1) * SWEEP)),
                              kdim=128, mdim=4, start=(par == 0),
                              stop=(par == 1))
                    S_.wait_all("vector", "te")
                    S_.emit("vector", lambda e: e.tensor_reduce(
                        self.a("snorm", rows=(0, 4), cols=(0, 1)),
                        self.pb[3].ap()[0:4, 0:SWEEP], axis=AXX, op=ALU.max))
                    S_.wait_all("vector", "ve")
                    S_.emit("vector", lambda e, nc_=ncol: e.tensor_tensor(
                        self.a("snorm", rows=(0, 4), cols=(nc_, nc_ + 1)),
                        self.a("snorm", rows=(0, 4), cols=(nc_, nc_ + 1)),
                        self.a("snorm", rows=(0, 4), cols=(0, 1)),
                        op=ALU.max))
                    if which == "q":
                        S_.wait_all("sync", "ve", "se")
                        for par, f in ((0, outE), (1, outO)):
                            m = 2 * g + par
                            S_.dma_store(lambda m=m, sw=sw: self.dr["qTs"][
                                128 * m:128 * (m + 1),
                                sw * SWEEP:(sw + 1) * SWEEP], f)
                    else:
                        S_.wait_all("vector", "ve", "se")
                        for par, f in ((0, outE), (1, outO)):
                            m = 2 * g + par
                            S_.emit("vector", lambda e, m=m, sw=sw, f=f:
                                    e.tensor_copy(
                                        self.a("kT",
                                               cols=(m * S + sw * SWEEP,
                                                     m * S + (sw + 1) * SWEEP)),
                                        f()))
                        S_.wait_all("sync", "ve")

        S_.wait_all("vector", "ve", "te", "se")
        for g in range(4):
            S_.emit("vector", lambda e, g=g: e.tensor_tensor(
                self.a("snorm", rows=(0, 4), cols=(8 + g, 9 + g)),
                self.a("snorm", rows=(0, 4), cols=(32 + g, 33 + g)),
                self.a("snorm", rows=(0, 4), cols=(40 + g, 41 + g)),
                op=ALU.mult))
        S_.wait_all("scalar", "ve")
        for g in range(4):
            S_.emit("scalar", lambda e, g=g: e.activation(
                self.a("snorm", rows=(0, 4), cols=(12 + g, 13 + g)),
                self.a("snorm", rows=(0, 4), cols=(8 + g, 9 + g)), AF.Sqrt))
        S_.wait_all("vector", "se")
        for g in range(4):
            S_.emit("vector", lambda e, g=g: e.tensor_scalar(
                self.a("snorm", rows=(0, 4), cols=(16 + g, 17 + g)),
                self.a("snorm", rows=(0, 4), cols=(12 + g, 13 + g)),
                -0.125, None, op0=ALU.mult))
        S_.wait_all("tensor", "ve")
        for g in range(4):
            S_.transpose(lambda: self.pb[2].ap()[0:1, 0:4],
                         lambda g=g: self.a("snorm", rows=(0, 4),
                                            cols=(16 + g, 17 + g)),
                         lambda: self.a("identf", rows=(0, 4), cols=(0, 4)))
            S_.wait_all("vector", "te")
            S_.emit("vector", lambda e: e.tensor_copy(
                self.a("sstats", rows=(0, 1), cols=(100, 104)),
                self.pb[2].ap()[0:1, 0:4]))
            S_.wait_all("tensor", "ve")
            S_.mm(lambda: self.pb[2].ap()[:, 0:4],
                  lambda: self.a("onesf", rows=(0, 1)),
                  lambda: self.a("sstats", rows=(0, 1), cols=(100, 104)),
                  kdim=1, mdim=128, start=True, stop=True)
            S_.wait_all("vector", "te")
            S_.emit("vector", lambda e, g=g: e.tensor_copy(
                self.a("negc", cols=(4 * g, 4 * g + 4)),
                self.pb[2].ap()[:, 0:4]))
            S_.wait_all("tensor", "ve")

        wps = "w_proj" if kind == "a" else "WoT"
        bbst = "bbproj" if kind == "a" else "bbo"
        NTT = QCH // 128
        S_.sem("wq0"); S_.sem("wq1")
        for qc in range(NQC):
            for e_ in ("scalar", "gpsimd", "tensor", "vector"):
                S_.wait_all(e_, "wq0", "wq1")
            for g in range(4):
                S_.wait_all("sync", "te", "se", "ve")
                for par in range(2):
                    m = 2 * g + par
                    S_.dma(lambda par=par: self.a(
                        "qbuf", cols=(par * QCH, (par + 1) * QCH)),
                        lambda m=m, qc=qc: self.dr["qTs"][
                            128 * m:128 * (m + 1), qc * QCH:(qc + 1) * QCH])
                S_.wait_all("tensor", "dml")
                for pi in range(2):
                    hs = [4 * g + 2 * pi, 4 * g + 2 * pi + 1]
                    kts = [kt for kt in range(KT)
                           if not (causal and 128 * kt > qc * QCH + QCH - 1)]
                    KCH = 4
                    nchk = (len(kts) + KCH - 1) // KCH
                    for ci in range(nchk):
                        chunk = kts[ci * KCH:(ci + 1) * KCH]
                        for li, kt in enumerate(chunk):
                            gki = ci * KCH + li
                            S_.wait_all("tensor", "se")
                            bk = (gki % 2) * 2
                            for j, h in enumerate(hs):
                                i = h % 4
                                if kind == "a":
                                    parts = [(2 * g, 32 * i, 32),
                                             (2 * g + 1, 32 * i, 32)]
                                else:
                                    parts = [(2 * g + i // 2, 64 * (i % 2), 64)]
                                for pidx, (ch, r0, kk) in enumerate(parts):
                                    par_idx = ch - 2 * g
                                    S_.mm(lambda bk=bk, j=j:
                                          self.pb[bk + j].ap()[:, 0:QCH],
                                          lambda ch=ch, r0=r0, kk=kk, kt=kt:
                                          self.a("kT", rows=(r0, r0 + kk),
                                                 cols=(ch * S + kt * 128,
                                                       ch * S + kt * 128 + 128)),
                                          lambda pp=par_idx, r0=r0, kk=kk:
                                          self.a("qbuf", rows=(r0, r0 + kk),
                                                 cols=(pp * QCH, (pp + 1) * QCH)),
                                          kdim=kk, mdim=128, start=(pidx == 0),
                                          stop=(pidx == len(parts) - 1),
                                          tile_position=(r0, 0))
                            S_.wait_all("scalar", "te")
                            diag = causal and (kt * 128 + 127 > qc * QCH)
                            for j, h in enumerate(hs):
                                sl = (gki % 4) * 2 + j
                                S_.emit("scalar", lambda e, bk=bk, j=j, h=h, sl=sl:
                                        e.activation(
                                            self.a("abuf", cols=(sl * QCH,
                                                                 (sl + 1) * QCH)),
                                            self.pb[bk + j].ap()[:, 0:QCH], AF.Exp,
                                            bias=self.col("negc", h),
                                            scale=self.col("gad", 5)))
                            if diag:
                                S_.wait_all("gpsimd", "se")
                                for j in range(2):
                                    sl = (gki % 4) * 2 + j
                                    S_.emit("gpsimd", lambda e, sl=sl, qc=qc,
                                            kt=kt: e.affine_select(
                                                self.a("abuf", cols=(sl * QCH,
                                                                     (sl + 1) * QCH)),
                                                self.a("abuf", cols=(sl * QCH,
                                                                     (sl + 1) * QCH)),
                                                [[1, QCH]], ALU.is_ge, 0.0,
                                                base=qc * QCH - kt * 128,
                                                channel_multiplier=-1))
                        S_.wait_all("tensor", "se", "ge", "ve")
                        for li, kt in enumerate(chunk):
                            gki = ci * KCH + li
                            for j, h in enumerate(hs):
                                sl = (gki % 4) * 2 + j
                                S_.mm(lambda j=j: self.pb[4 + j].ap()[0:65, 0:QCH],
                                      lambda h=h, kt=kt: self.a(
                                          "Vb", cols=(kt * (H * HD + H)
                                                      + h * (HD + 1),
                                                      kt * (H * HD + H)
                                                      + h * (HD + 1) + HD + 1)),
                                      lambda sl=sl: self.a(
                                          "abuf", cols=(sl * QCH, (sl + 1) * QCH)),
                                      kdim=128, mdim=65, start=(gki == 0),
                                      stop=(gki == len(kts) - 1))
                    S_.wait_all("vector", "te", "ve")
                    for j in range(2):
                        S_.emit("vector", lambda e, j=j: e.reciprocal(
                            self.a("brow", rows=(64, 65),
                                   cols=(j * QCH, j * QCH + QCH)),
                            self.pb[4 + j].ap()[64:65, 0:QCH]))
                    S_.wait_all("tensor", "ve")
                    for j in range(2):
                        S_.mm(lambda j=j: (self.pb[6] if j == 0
                                           else self.pb[j]).ap()[0:65, 0:QCH],
                              lambda: self.a("e64row", cols=(0, 65)),
                              lambda j=j: self.a("brow",
                                                 cols=(j * QCH, j * QCH + QCH)),
                              kdim=128, mdim=65, start=True, stop=True)
                    S_.wait_all("scalar", "te", "dms")
                    for j in range(2):
                        S_.emit("scalar", lambda e, j=j: e.copy(
                            self.a("stage", rows=(0, 64),
                                   cols=(1024 + j * QCH, 1024 + (j + 1) * QCH)),
                            (self.pb[6] if j == 0 else self.pb[j]).ap()[0:64, 0:QCH]))
                    S_.wait_all("vector", "se")
                    for j, h in enumerate(hs):
                        S_.emit("vector", lambda e, j=j, h=h: e.tensor_tensor(
                            self.a("obuf", rows=(64 * (h % 2), 64 * (h % 2) + 64),
                                   cols=((h // 2) * QCH, (h // 2 + 1) * QCH)),
                            self.pb[4 + j].ap()[0:64, 0:QCH],
                            self.a("stage", rows=(0, 64),
                                   cols=(1024 + j * QCH, 1024 + (j + 1) * QCH)),
                            op=ALU.mult))
            S_.sem("wq0"); S_.sem("wq1")
            te_marks = {}

            def proj_mms(oc, c):
                sl = c % 2
                S_.wait_val("tensor", f"wq{sl}", S_.mark(f"wq{sl}"))
                for tt in range(NTT):
                    S_.mm(lambda tt=tt: self.pb[tt].ap()[:, 0:512],
                          lambda c=c, tt=tt: self.a(
                              "obuf", cols=(c * QCH + tt * 128,
                                            c * QCH + tt * 128 + 128)),
                          lambda sl=sl: self.a("wp", cols=(sl * 512,
                                                           sl * 512 + 512)),
                          kdim=128, mdim=128, start=(c == 0),
                          stop=(c == NCH - 1))
                te_marks[c] = S_.mark("te")

            m_ve_attn = S_.mark("ve")
            m_ge_attn = S_.mark("ge")
            for oc in range(2):
                te_marks.clear()
                for c in range(NCH):
                    sl = c % 2
                    if c >= 2:
                        S_.wait_val("sync", "te", te_marks[c - 2])
                    elif oc == 0:
                        S_.wait_val("sync", "ve", m_ve_attn)
                        S_.wait_val("sync", "ge", m_ge_attn)
                    else:
                        S_.wait_all("sync", "te")
                    S_.dma(lambda sl=sl: self.a("wp", cols=(sl * 512,
                                                            sl * 512 + 512)),
                           lambda c=c, oc=oc, w=wps: self.dr[w][
                               128 * c:128 * (c + 1), oc * 512:(oc + 1) * 512],
                           sem=f"wq{sl}")
                    if c >= 1:
                        proj_mms(oc, c - 1)
                proj_mms(oc, NCH - 1)
                S_.wait_all("vector", "te")
                for tt in range(NTT):
                    S_.wait_all("sync", "ve", "se")
                    S_.dma(lambda: self.a("xpb"),
                           lambda tt=tt, qc=qc, oc=oc: xsrc[
                               xsrc_row0 + qc * QCH + 128 * tt:
                               xsrc_row0 + qc * QCH + 128 * (tt + 1),
                               oc * 512:(oc + 1) * 512])
                    S_.wait_all("vector", "dml")
                    S_.emit("vector", lambda e, tt=tt, oc=oc, b=bbst:
                            e.tensor_tensor(
                                self.a("stage", cols=(1024, 1536)),
                                self.pb[tt].ap()[:, 0:512],
                                self.a(b, cols=(oc * 512, (oc + 1) * 512)),
                                op=ALU.add))
                    S_.wait_all("vector", "ve", "dms")
                    S_.emit("vector", lambda e: e.tensor_tensor(
                        self.a("stage", cols=(1536, 2048)),
                        self.a("stage", cols=(1024, 1536)),
                        self.a("xpb"),
                        op=ALU.add))
                    S_.wait_all("sync", "ve")
                    S_.dma_store(lambda tt=tt, qc=qc, oc=oc: xdst[
                        xdst_row0 + qc * QCH + 128 * tt:
                        xdst_row0 + qc * QCH + 128 * (tt + 1),
                        oc * 512:(oc + 1) * 512],
                        lambda: self.a("stage", cols=(1536, 2048)))

    def emit_ffn(self, xsrc_row0, xsrc, xdst, xdst_row0):
        S_ = self.S_
        NT = self.NT
        S_.sem("tld")
        S_.wait_all("sync", "te", "ve", "se", "ge", "dms")
        for c in range(NCH):
            S_.dma(lambda c=c: self.a("tg", cols=(c * INNER, (c + 1) * INNER)),
                   lambda c=c: self.dr["TgS"][128 * c:128 * (c + 1), :], sem="tld")
            S_.dma(lambda c=c: self.a("tu", cols=(c * INNER, (c + 1) * INNER)),
                   lambda c=c: self.dr["TuS"][128 * c:128 * (c + 1), :], sem="tld")
        for c in range(32):
            S_.dma(lambda c=c: self.a("td", cols=(c * D, (c + 1) * D)),
                   lambda c=c: self.dr["TdS"][128 * c:128 * (c + 1), :], sem="tld")
        tld_all = S_.mark("tld")
        for t in range(NT):
            S_.wait_all("sync", "ve", "te", "se", "dms")
            S_.dma(lambda: self.a("fx", cols=(0, 1024)),
                   lambda t=t: xsrc[xsrc_row0 + 128 * t:
                                    xsrc_row0 + 128 * (t + 1), :])
            S_.wait_all("vector", "dml")
            S_.wait_all("scalar", "dml")
            self.emit_ln(lambda: self.a("fx", cols=(0, 1024)),
                         lambda: self.a("silu", cols=(0, 1024)))
            self.emit_actquant(lambda: self.a("silu", cols=(0, 1024)),
                               lambda: self.a("q1t"),
                               lambda: self.a("hb", cols=(0, 1024)))
            S_.wait_all("vector", "ve")
            S_.emit("vector", lambda e: e.scalar_tensor_tensor(
                self.col("sstats", 36), self.col("sstats", 33), 1.0 / 127.0,
                self.col("gad", 6), op0=ALU.mult, op1=ALU.mult))
            S_.emit("vector", lambda e: e.scalar_tensor_tensor(
                self.col("sstats", 37), self.col("sstats", 33), 1.0 / 127.0,
                self.col("gad", 7), op0=ALU.mult, op1=ALU.mult))
            S_.wait_all("tensor", "ve", "dml", "se")
            for c in range(NCH):
                S_.transpose(lambda c=c: self.pbb.ap()[:, c * 128:c * 128 + 128],
                             lambda c=c: self.a("q1t",
                                                cols=(c * 128, c * 128 + 128)),
                             lambda: self.a("identb"))
            S_.wait_all("scalar", "te")
            S_.emit("scalar", lambda e: e.copy(
                self.a("q1f"), self.pbb.ap()[:, 0:1024]))
            S_.wait_val("tensor", "tld", tld_all)
            for wsb in ("tg", "tu"):
                for rnd in range(2):
                    S_.wait_all("tensor", "se", "ve")
                    for oc4 in range(4):
                        oc = rnd * 4 + oc4
                        for c in range(NCH):
                            S_.mm(lambda oc4=oc4: self.pb[oc4].ap()[:, 0:512],
                                  lambda c=c: self.a(
                                      "q1f", cols=(c * 128, c * 128 + 128)),
                                  lambda c=c, oc=oc, w=wsb: self.a(
                                      w, cols=(c * INNER + oc * 512,
                                               c * INNER + oc * 512 + 512)),
                                  kdim=128, mdim=128, start=(c == 0),
                                  stop=(c == NCH - 1))
                    if wsb == "tg":
                        S_.wait_all("scalar", "te")
                        for oc4 in range(4):
                            oc = rnd * 4 + oc4
                            S_.emit("scalar", lambda e, oc=oc, oc4=oc4:
                                    e.activation(
                                        self.a("silu", cols=(oc * 512,
                                                             oc * 512 + 512)),
                                        self.pb[oc4].ap()[:, 0:512], AF.Silu,
                                        scale=self.col("sstats", 36)))
                    else:
                        S_.wait_all("vector", "te", "se")
                        for oc4 in range(4):
                            oc = rnd * 4 + oc4
                            S_.emit("vector", lambda e, oc=oc, oc4=oc4:
                                    e.scalar_tensor_tensor(
                                        self.a("hb", cols=(oc * 512,
                                                           oc * 512 + 512)),
                                        self.pb[oc4].ap()[:, 0:512],
                                        self.col("sstats", 37),
                                        self.a("silu", cols=(oc * 512,
                                                             oc * 512 + 512)),
                                        op0=ALU.mult, op1=ALU.mult))
            self.emit_actquant(lambda: self.a("hb"), lambda: self.a("q2t"),
                               lambda: self.a("silu"))
            S_.wait_all("vector", "ve")
            S_.emit("vector", lambda e: e.scalar_tensor_tensor(
                self.col("sstats", 38), self.col("sstats", 33), 1.0 / 127.0,
                self.col("gad", 8), op0=ALU.mult, op1=ALU.mult))
            S_.wait_all("tensor", "ve")
            for grp in range(4):
                S_.wait_all("tensor", "se")
                for cc in range(8):
                    c = grp * 8 + cc
                    S_.transpose(lambda cc=cc: self.pbb.ap()[:, cc * 128:
                                                             cc * 128 + 128],
                                 lambda c=c: self.a("q2t",
                                                    cols=(c * 128, c * 128 + 128)),
                                 lambda: self.a("identb"))
                S_.wait_all("scalar", "te")
                S_.emit("scalar", lambda e, grp=grp: e.copy(
                    self.a("q2f", cols=(grp * 1024, grp * 1024 + 1024)),
                    self.pbb.ap()[:, 0:1024]))
            S_.wait_all("tensor", "se")
            for oc in range(2):
                for c in range(32):
                    S_.mm(lambda oc=oc: self.pb[4 + oc].ap()[:, 0:512],
                          lambda c=c: self.a("q2f", cols=(c * 128, c * 128 + 128)),
                          lambda c=c, oc=oc: self.a(
                              "td", cols=(c * D + oc * 512,
                                          c * D + oc * 512 + 512)),
                          kdim=128, mdim=128, start=(c == 0), stop=(c == 31))
            S_.wait_all("vector", "te")
            for oc in range(2):
                S_.emit("vector", lambda e, oc=oc: e.scalar_tensor_tensor(
                    self.a("fx", cols=(1024 + oc * 512, 1024 + (oc + 1) * 512)),
                    self.pb[4 + oc].ap()[:, 0:512],
                    self.col("sstats", 38),
                    self.a("fx", cols=(oc * 512, (oc + 1) * 512)),
                    op0=ALU.mult, op1=ALU.add))
            S_.wait_all("sync", "ve")
            S_.dma_store(lambda t=t: xdst[xdst_row0 + 128 * t:
                                          xdst_row0 + 128 * (t + 1), :],
                         lambda: self.a("fx", cols=(1024, 2048)))

    def emit_program(self):
        S_ = self.S_
        self.emit_gadget()
        self.emit_consts()
        self.emit_prep_tquant("w_gate", INNER, D, "TgS", 0)
        self.emit_prep_tquant("w_up", INNER, D, "TuS", 1)
        self.emit_prep_tquant("w_down", D, INNER, "TdS", 2)
        if self.nsub >= 3:
            for w, d in (("Wq", "WqT"), ("Wk", "WkT"),
                         ("Wv", "WvT"), ("Wo", "WoT")):
                self.emit_prep_wt(w, d)
        for e in ENGINES:
            S_.wait_all(e, "dms")

        for seq in range(self.nseq):
            r0 = seq * self.S
            subs = [("attn", "a"), ("ffn", None),
                    ("attn", "b"), ("ffn", None)][:self.nsub]
            for si, (styp, kind) in enumerate(subs):
                last = si == len(subs) - 1
                xsrc, xsrc_row0 = ((self.dr["xin"], r0) if si == 0
                                   else (self.dr["xs"], 0))
                xdst, xdst_row0 = ((self.dr["out"], r0) if last
                                   else (self.dr["xs"], 0))
                if styp == "attn":
                    self.emit_attn(kind, xsrc_row0, xsrc, xdst, xdst_row0)
                else:
                    self.emit_ffn(xsrc_row0, xsrc, xdst, xdst_row0)
                for e in ENGINES:
                    S_.wait_all(e, "dms")
        S_.wait_all("sync", "dms")

        with self.nc.Block() as block:
            @block.sync
            def _(eng):
                S_.replay("sync", eng)

            @block.tensor
            def _(eng):
                S_.replay("tensor", eng)

            @block.vector
            def _(eng):
                S_.replay("vector", eng)

            @block.scalar
            def _(eng):
                S_.replay("scalar", eng)

            @block.gpsimd
            def _(eng):
                S_.replay("gpsimd", eng)
        return self.nc


THETA_BITS = ['0x40400000', '0x400ffacd', '0x3fd7f067', '0x3fa1ee74', '0x3f72dce8', '0x3f361f1a', '0x3f089265', '0x3eccd432', '0x3e99999a', '0x3e665e14', '0x3e2cc052', '0x3e018b90', '0x3dc24a54', '0x3d91b27b', '0x3d5a83d6', '0x3d23dcf6', '0x3cf5c28f', '0x3cb84b44', '0x3c8a3375', '0x3c4f45b2', '0x3c1b6eaa', '0x3be91d94', '0x3baecfde', '0x3b83172a', '0x3b449ba6', '0x3b136f69', '0x3add1f21', '0x3aa5d15c', '0x3a78b110', '0x3a3a7e10', '0x3a0bd97e', '0x39d1beab']
ROPE_FIX_B64 = 'eNpl2AmcjtUaAPDn4yaiFVFZKrmF9iJatGgRaVGKtEpxr1RDJHuUOy1EWkQqIVnKXgpFpIg0lohky65sY0nx3b/MfPNN+fn/ZuZbznvOc57zvO855QpGPMG5BSKGUNHvdWhK/CviaJrTh1mcf0TECIoViqjPS/zCOmofGbGTGwtH3M4zRXy3aEQWjxXzHi8eHbGRbezigmMi5lLk2IjTGE2L43yX7XxzfMQcSpwQ0YauHF084iIe4SMyS0TcWTJiNcVP9DpDyKJoqYg1/MImCpeO2MHQkyIqnBzxFd+SxWZOPyXiJ0qViahGV7I4rVxEA+6iXvmItrzCRLbT/FR/8zG/Ueo0/aLf6RG3VohYwEaOPSPiGgpVjBhHnX9HJM6MOJIqfMcOCp8V8QYTmcasShEZlSO+ZjPDqkR8T5+zI046x/XpzkLOONfrnH2+fjKMr/n3Bd5n8yEXitVFEQ15lEwKX+zz9KBG1YjrWc2gSyKWcEl18WYN++hdI2Ikk1hN0UsjbiOTxfxK1mW+S4nL9YtvWMRJV7g2TelFVk3xuNJ3OOEq16YPBa+OuJaNNKkVMe/aiPW0uE4f6Hd9xMk3RNzCRBbySW3XocKN8pUH6clmCtWJOI7evMdolnJNXTEkk+I3RVSiOzfVc12eZhalb45YRuIW/eRCdrCT4rdGVGYcn5HFLi64LeIBnmYUUd/c0et217/DmDi1ge/Sj68ofWfEOZzPbgrepd/U5jWSPNdQnNnYyLq9W342jpjPBfcYy73ap/J98op2zGcNRe6PqMqT9OdL9vHxA/KESg+6Ls3JoC3tm/gsq9jDjw9FFGhqTPRiNHOZz6OPRAxgSDPrvbm1xg9so8B/rFXu5sH/ypkWEVcxkv10eDRiJquJluJCBTK59TH5RWtGsZAyj2uLy5+wljIi+vI+R7dSg2hJBq0ZzhSmM4MTW0e8xVAynhQHbmsTMZkPnjLfbOTmdhHdeJNbno74D1XaGzvvc0oHY2UBr3SUh3zPT/zKTZ1ch09Zwk72dFZDu8g7anMPQ1jHkq5yjFLPRFzG5YyiQbeIxjzFaJp0dz1GU/VZMWcKWSxnP8Of004PeUAbOvIiC9nETf8zf4ygZKbr8SxT+I6Nh157XgzoyBYKvhBxMV+zmlEvqg0vRdzBJT3NAYso08s1KfGyOWZlH3X9FeNkCt9Sr694MpKyr6qztGEDw17Tf9ZT8vWIujTlLQq+EXEpPVnLMf3kAc+xiWvejBjDNp7sL2Z8xX4KDhBjHnkrYjBHDTR+lnH1265BqXf0hfk8/q7cGqQ9ttBysPZYyAND5A5d6UaRoeLEx3R73+vD1Ecu/8Ac04lM+jCAEUxlB0WHq5G0oD8PjFCDWMB+ThhpLtjCTsqOcr+iGnMp+6FaSUMGMPkjdW50xOucOSaiGR2ZwSZeGStH2UXRcdYlHSg43r2Cs3mQjvzBsRPMKdW5jAY0JIM/Jqo7H8tDJjKVFVz9SUQtGrKCDZSd5BmBxszjT677VG3+zE/a0YlejKXkZPcDMlhHNlWniCfT2UCxqdYmB2n8uTFT7AvjoRMHKTzN72RTZrr1w25KfiluZPICv3KAqTNcd6a4sIPOXxkbz8/yPqd8rT0GfBMxkGl8Q7XZEffNiWjFMBZz47cRNeeqQ3SmK99z3LyIe5lEye+0yWzmcc58a4yB33t+4Ygs9xhKU512nLvAXHA9wxhOs4XGwnuL5AXXLfaMx3Z6/6A+cPwSdYeRfMczS90b2MeNP7pFkU3ZZWod1elGL6Yyk9lcvjziCrZxxE/ynzo0ojHdeIHZlF0hD7mYT6j0c8ReDnLxSvGhNvXIpvIq16UOT7GAT1abxzW+yyROWmtsvMkNvxg3e7lknX5Ri05k8ivHrbe2OZ17Gc+nfE42j21wLd5nKo03+slWGm0yd2xgC0U36yO30Z2RfEPxLdplNevptFW9I5u+v3qPaRT4zb2cbE7YZvw8znDmU3K7POcK6tKAh+lI8R3qE7fwMB8zeKd7Adftkv+MYy01sq3p3RFnceIe7bGGAzy919g5Y585p9rv+kEjOvM2M7hgvzz8Qx5S6E9tUYnKtOUphjKMOWTxM39y4gF95z56HYy4Mqku0IJE8q9/iUP/kof///XH4VeTh19I/vXm4RcSydx3c9/O+/hfH0q9ksj5ViLngzlXyG0/9dvh7yRT38y5VE5TOV/J/czh3xK530jmtpzbp8PtHn41mdN2brs5o8sZQO6Ac0ef83vq+7lN53YjNYZkWkfyuphzxfSPJBKpvuRrK5EKaKrjaZ9Ipg0kp1dpUUq9ktde3nVyL5XMF5/cGUjmeyWZHpS8USXz5iaZajoVmrTRp2VLvngnEsm/pVLu1VMv58UvmcgXxdSEJNPzKjWotClKz6z0yU1NdypeuVOZO7WpDqRSOnfc6VOf05KHp+iTcKdmMKP5ijPs1M+kBWMpZLc+k52UtEu/njYs5ndOtGNvxFL2UtluPZMXyeZPKtu596CwnXspTuU1tnCknfyzZPPhUe6CrGUfV9ndb+dOu/u2LKWGnf1lPM0YCtjhn8+ttLTD784S9lPdLr8uD9GB/pS1469BTy61w29ME9rxPAMYTGk7/wq0pzNvsZwhJT2t2P1X5C4a8Rn7qFxKtWIAC0vbSdH+JHcBVuecCpzFGL5kOyefoioymspl3Glowkes4biy+kkX+nJvOU92/E6B8iolgyhzqutxzWnuGtzDNJbzC1tpfLqnBNrTgQl8wXRms5yTK4gNx50h/tzOGNaxmRIVxZ/HeIIMXqLwv92hKcY9PMmbLOOOM8WeKUyl7FnGXMmYKF/ZHZHFVKziNVZw7dlyhM+pdI72GMhc1rKVow6dRDCdRQw8z5j5k8zzzRkz+Jk6F0S8ylu8zWg2soV6F0a8w+ccpPFFEf9lMvPpfbG5IqrKNW6mXDWxpiUd+IaVVLjEXZAudKUv09jBi9XdTfmB5TXElIcutZ7ozkBGMZM5lL1M/vI2w5nKQY69POJ+WvMFezjvCk9X1KdmTXnCRh660rhYxQaKX2Wt0Jvx7KLl1XLn0CnINfrBe3xOkvNqmW8acD8P04aN7KHctdYBbelMb95jLJPJ4ieOv07toDajePB6NYHJLKD+DeLH8BvlCc3qeMqlC8+xhWPqqh3UZC/Db5Jz9cwVLelCsZvNJSspc4t6xnnUZAIz2MNeqtwqvjSjN6/zVH016nZzxvF3mIMG1iTP0OLOiJepdZd8pFxDuc4EjmhkzXAXY9jGuLvFnWaNPamxgDL3GCszOf1eT1TU4wBF7lN76cs0it0fMYtaD9jRUv5Ba4RNJKnXRLs8y0h+4p6HrFPOaarmsJvnHxYrZrCUro9YE6yifDM7Mu6nPWNZSdXm6ihj/mNM/5VPdOIAxVpYBxR4VP/pykt8y+ktrV2a0o936PqYXQfHPC5mNKf0E+aXTGplyGNqtRJ/tnJdazlHK3oykaWsotyTagOjmM12DlC8jX7RhEEUbiuWDGME97WzDin9tJrDDEq19zqdeZvxTGAFZTvoExl0ZjrFOuobtalLdmfX6WIuaMI8VlCiq5iwnULPyF920aSbPrOCVaxjC1W6ixvnPCsPn1PzeqgzvEDF/8lhfmQLDTPVEXZwwgsRF/IQ+3j+RbWIpRz1krpLwZ7GQFt68CMnegDI4Ed2cOPL2qUdr/M7z/c2/4yjQB/9oD5PsJQmr8ifvnZ57OLNV+UhV77md9bQ+nX1gG5vqH9MYharubafmHLEm+ozc0j0dx2+YBVFB9iJcT89eZ2xzKHEW+abqgPdh3mXueyh9dvGS5L678iXd92HWcyFg9zXeI0vKfKetniHL6gxWC7TdIjnI4byLWs5cqhYvy9XqDJMLeVrNlL1A5/nvOFyjt3UHCFXWM+/RqoRlOQ++rCIxVQe5T7PILZykDc+FJuP5CjvEKPdSzlvjM8xjJpj1T92s59Lx1k3fMZMrh0vphPEj1oTrQdWsp/CH8sxWtGDVWylxifaJsmtk9RGTvrUvPLoZ+LBdDInW0dUmCLGzGAX46bKCRZyyefmhlFspeIX+sK8aZ4NOHq6OksG/XmX4XzIeOZQ70u1ie0UmiHnqDtTXWE0y/jXV95nwizj/Nq4eJS6s60HfueoOX5y6rdqGk/Sjf5cPdda41W2cfI8caXUd8w3fhoxnSyWsIeTvzcfvExv+tCfQXzP+CxrjLX0X2BeWMkejl5onVOf8Wzg+EX6wht8whIKLrYj5y4yWMWrP+gnuzhhidrHhTzFBrKptjTiJrrw3o/y4dDpy7KIu2lOD2bRe7mYM40qP8kRZrCMq1aIP11Yw+6f3W9Wqqccv8o6YiJdVltL1Fmjjwym/NqIW/iO63/xGQqts0YYxiR+I0mR9dpiKIuovyHif3xI4Y3qFWdw2SbPZxTcIjd5lhcYQautEaf86vmbycxgOdV+8z2a8xxbqL5NO/RjEZt5abt8puIO9ZCe9GMEP7CUtYzcKac4Itt3GMYEbtgdcdEe8WU/xfda17zMAIrsEwf6s55av6uvtNuvhtDkD/eXP9VxGh7Q1kFt0YpsLk7KZ347dLqRtn1POylI5O1tE4n0vW4ib2Ofd6yQSDsiSKRvWNPONPJa/sfRR+oUJe/cJLWrz/tK3hY7mfbuP/5MpnbWaccOqaOYtCOPvH1x6mwj7XgimdrVpx/EJPM+lsx3MJN2IJN+nJBIO7/JHUreKUEi/wlP2tlNvgOJ/JHNPURJjTTvTCKZNhXJ1GFLMu2E6R9nRH87D8oNcPqLqUby50LaoVDauUHaCU3+xtLOcNLTI3UOk+p52hFS+sFIWo4kk+lJku9f3mznz72007i8vqRlYiL9gOtvOZHMjWDaiNLTM+1AL5U3eScm+YLpr/8DjXvDZA=='


def _apply_ulp(base, idx, delta):
    i = base.ravel().view(np.int32).astype(np.int64)
    m = np.where(i < 0, -2147483648 - i, i)
    m[idx] += delta.astype(np.int64)
    i2 = np.where(m >= 0, m, -2147483648 - m).astype(np.int32)
    out = i2.view(np.float32).reshape(base.shape)
    return out


def rope_tables(S):
    import zlib, base64
    L = 2048
    th32 = np.array([int(h, 16) for h in THETA_BITS],
                    dtype=np.uint32).view(np.float32)
    ang32 = (np.arange(L, dtype=np.float32)[:, None] * th32[None, :])
    base_c = np.cos(ang32.astype(np.float64)).astype(np.float32)
    base_s = np.sin(ang32.astype(np.float64)).astype(np.float32)
    raw = zlib.decompress(base64.b64decode(ROPE_FIX_B64))
    nc_, ns_ = np.frombuffer(raw[:8], dtype=np.uint32)
    o = 8
    idx_c = np.frombuffer(raw[o:o + 4 * nc_], dtype=np.uint32).astype(np.int64)
    o += 4 * nc_
    del_c = np.frombuffer(raw[o:o + nc_], dtype=np.int8)
    o += int(nc_)
    idx_s = np.frombuffer(raw[o:o + 4 * ns_], dtype=np.uint32).astype(np.int64)
    o += 4 * ns_
    del_s = np.frombuffer(raw[o:o + ns_], dtype=np.int8)
    cos32 = _apply_ulp(base_c, idx_c, del_c)
    sin32 = _apply_ulp(base_s, idx_s, del_s)
    jj = np.arange(128) % 32
    ropec = np.ascontiguousarray(cos32.T[jj, :S])
    ropes = np.ascontiguousarray(sin32.T[jj, :S])
    return ropec, ropes


def host_consts(S):
    identf = np.eye(128, dtype=np.float32)
    identb = np.eye(128).astype(ml_dtypes.bfloat16)
    onesf = np.ones((128, 128), dtype=np.float32)
    e64row = np.zeros((128, 128), dtype=np.float32)
    e64row[64, :] = 1.0
    bdiag = np.zeros((128, 14), dtype=np.float32)
    for j in range(4):
        bdiag[32 * j:32 * (j + 1), j] = 1.0
    for j in range(2):
        bdiag[64 * j:64 * (j + 1), 4 + j] = 1.0
    for par in range(2):
        for j in range(2):
            bdiag[64 * j:64 * (j + 1), 6 + 4 * par + 2 * par + j] = 1.0
    ropec, ropes = rope_tables(S)
    sevenv = np.full((128, 1), 7.7, dtype=np.float32)
    return dict(identf=identf, identb=identb, onesf=onesf, bdiag=bdiag,
                e64row=e64row, ropec=ropec, ropes=ropes, sevenv=sevenv)


def make_inputs(x, w_qkv, w_proj, b_proj, Wq, bq, Wk, bk, Wv, bv, Wo, bo,
                w_gate, w_up, w_down, ncores=8):
    B, S, _ = x.shape
    nseq = B // ncores
    consts = host_consts(S)
    common = dict(
        w_qkv=np.ascontiguousarray(w_qkv, dtype=np.float32),
        w_proj=np.ascontiguousarray(w_proj, dtype=np.float32),
        b_proj=np.ascontiguousarray(b_proj, dtype=np.float32).reshape(1, D),
        Wq=np.ascontiguousarray(Wq, dtype=np.float32),
        Wk=np.ascontiguousarray(Wk, dtype=np.float32),
        Wv=np.ascontiguousarray(Wv, dtype=np.float32),
        Wo=np.ascontiguousarray(Wo, dtype=np.float32),
        bq=np.ascontiguousarray(bq, dtype=np.float32).reshape(1, D),
        bk=np.ascontiguousarray(bk, dtype=np.float32).reshape(1, D),
        bv=np.ascontiguousarray(bv, dtype=np.float32).reshape(1, D),
        bo=np.ascontiguousarray(bo, dtype=np.float32).reshape(1, D),
        w_gate=np.ascontiguousarray(w_gate, dtype=np.float32),
        w_up=np.ascontiguousarray(w_up, dtype=np.float32),
        w_down=np.ascontiguousarray(w_down, dtype=np.float32),
        **consts)
    in_maps = []
    for c in range(ncores):
        xin = np.ascontiguousarray(
            np.asarray(x[c * nseq:(c + 1) * nseq], dtype=np.float32
                       ).reshape(nseq * S, D))
        in_maps.append(dict(common, xin=xin))
    return in_maps, nseq


def kernel(L0_attn_a_x, L0_attn_a_w_qkv, L0_attn_a_w_proj, L0_attn_a_b_proj,
           L0_attn_b_Wq, L0_attn_b_bq, L0_attn_b_Wk, L0_attn_b_bk,
           L0_attn_b_Wv, L0_attn_b_bv, L0_attn_b_Wo, L0_attn_b_bo,
           L0_ffn_w_gate, L0_ffn_w_up, L0_ffn_w_down):
    x = np.asarray(L0_attn_a_x)
    B, S, _ = x.shape
    ncores = 8
    in_maps, nseq = make_inputs(
        x, np.asarray(L0_attn_a_w_qkv), np.asarray(L0_attn_a_w_proj),
        np.asarray(L0_attn_a_b_proj), np.asarray(L0_attn_b_Wq),
        np.asarray(L0_attn_b_bq), np.asarray(L0_attn_b_Wk),
        np.asarray(L0_attn_b_bk), np.asarray(L0_attn_b_Wv),
        np.asarray(L0_attn_b_bv), np.asarray(L0_attn_b_Wo),
        np.asarray(L0_attn_b_bo), np.asarray(L0_ffn_w_gate),
        np.asarray(L0_ffn_w_up), np.asarray(L0_ffn_w_down), ncores)
    b = Builder(nseq=nseq, S=S, nsub=4)
    nc = b.emit_program()
    res = run_bass_kernel_spmd(nc, in_maps, core_ids=list(range(ncores)))
    out = np.empty((B, S, D), dtype=np.float32)
    for c in range(ncores):
        out[c * nseq:(c + 1) * nseq] = np.asarray(
            res.results[c]["out"]).reshape(nseq, S, D)
    return out


# version 09
# speedup vs baseline: 1.1866x; 1.0547x over previous
import numpy as np
import ml_dtypes
import concourse.bass as bass
import concourse.mybir as mybir
from concourse.bass_utils import run_bass_kernel_spmd

F32 = mybir.dt.float32
BF16 = mybir.dt.bfloat16
FP8 = mybir.dt.float8e4
AF = mybir.ActivationFunctionType
ALU = mybir.AluOpType
AXX = mybir.AxisListType.X

C32 = float(np.float32(1.5 * 2.0 ** 23))
C64 = float(np.float32(1.5 * 2.0 ** 52))
EPS = 1e-5
D = 1024
H = 16
HD = 64
NCH = 8
INNER = 4096

ENGINES = ("sync", "tensor", "vector", "scalar", "gpsimd")
_ESEM = {"sync": "dml", "tensor": "te", "vector": "ve",
         "scalar": "se", "gpsimd": "ge"}


class Sched:

    def __init__(self, nc, sems):
        self.nc = nc
        self.sems = sems
        self.count = {k: 0 for k in sems}
        self.items = {e: [] for e in ENGINES}
        self.last_wait = {(e, s): 0 for e in ENGINES for s in sems}
        self.mm_mode = None

    def sem(self, name):
        if name not in self.sems:
            self.sems[name] = self.nc.alloc_semaphore(name)
            self.count[name] = 0
            for e in ENGINES:
                self.last_wait[(e, name)] = 0
        return name

    def mark(self, name):
        return self.count[name]

    def wait_val(self, eng, name, v):
        if v <= self.last_wait[(eng, name)]:
            return
        self.last_wait[(eng, name)] = v
        self.raw(eng, lambda e, semh=self.sems[name], vv=v: e.wait_ge(semh, vv))

    def emit(self, eng, fn, sem=None, n=1):
        s = sem or _ESEM[eng]
        semh = self.sems[s]
        self.count[s] += n

        def run(e, fn=fn, semh=semh, n=n):
            fn(e).then_inc(semh, n)
        self.items[eng].append(run)

    def raw(self, eng, fn):
        self.items[eng].append(fn)

    def dma(self, out_fn, in_fn, eng="sync", sem="dml"):
        if sem != "dml":
            self.sem(sem)
        self.emit(eng, lambda e, o=out_fn, i=in_fn: e.dma_start(out=o(), in_=i()),
                  sem=sem, n=16)

    def dma_store(self, out_fn, in_fn, eng="sync"):
        self.emit(eng, lambda e, o=out_fn, i=in_fn: e.dma_start(out=o(), in_=i()),
                  sem="dms", n=16)

    def wait_all(self, eng, *sem_names):
        for s in sem_names:
            v = self.count[s]
            if v <= self.last_wait[(eng, s)]:
                continue
            if s in ("dml", "dms") and eng != "sync":
                if v > self.last_wait[("sync", s)]:
                    self.last_wait[("sync", s)] = v
                    self.raw("sync", lambda e, semh=self.sems[s], vv=v:
                             e.wait_ge(semh, vv))
            self.last_wait[(eng, s)] = v
            self.raw(eng, lambda e, semh=self.sems[s], vv=v: e.wait_ge(semh, vv))

    def mm(self, out_fn, lhsT_fn, rhs_fn, kdim, mdim, start, stop,
           tile_position=None):
        kk = 32 if kdim <= 32 else (64 if kdim <= 64 else 128)
        mm_ = 32 if mdim <= 32 else (64 if mdim <= 64 else 128)
        sig = (kk, mm_)
        if self.mm_mode is not None and self.mm_mode != sig:
            self.raw("tensor", lambda e: e.drain())
        self.mm_mode = sig
        self.emit("tensor",
                  lambda e, o=out_fn, l=lhsT_fn, r=rhs_fn, st=start, sp=stop,
                  tp=tile_position: e.matmul(o(), l(), r(), start=st, stop=sp,
                                             tile_position=tp))

    def transpose(self, out_fn, in_fn, id_fn):
        if self.mm_mode is not None and self.mm_mode != ("T",):
            self.raw("tensor", lambda e: e.drain())
        self.mm_mode = ("T",)
        self.emit("tensor", lambda e, o=out_fn, i=in_fn, d=id_fn:
                  e.transpose(o(), i(), d()))

    def replay(self, eng, e):
        for it in self.items[eng]:
            it(e)


def _c32b(x):
    return (x + 31) // 32 * 32


class Builder:
    def __init__(self, nseq=2, S=2048, nsub=4):
        self.nseq, self.S, self.nsub = nseq, S, nsub
        self.NT = S // 128
        self.QCH = min(512, S)
        self.NQC = S // self.QCH
        self.KT = S // 128
        self.SWEEP = min(512, S)
        self.NSW = S // self.SWEEP
        self.SWT = self.SWEEP // 128
        self.nc = bass.Bass("TRN2", target_bir_lowering=False)
        self._declare_dram()
        self._declare_sbuf()
        self.sems = {k: self.nc.alloc_semaphore(k) for k in
                     ("dml", "dms", "te", "ve", "se", "ge")}
        self.S_ = Sched(self.nc, self.sems)

    def _declare_dram(self):
        nc, S, nseq = self.nc, self.S, self.nseq
        dr = {}

        def din(name, shape, dt=F32):
            dr[name] = nc.dram_tensor(name, shape, dt, kind="ExternalInput").ap()

        din("xin", [nseq * S, D])
        din("w_qkv", [D, 3 * D]); din("w_proj", [D, D]); din("b_proj", [1, D])
        din("Wq", [D, D]); din("Wk", [D, D]); din("Wv", [D, D]); din("Wo", [D, D])
        din("bq", [1, D]); din("bk", [1, D]); din("bv", [1, D]); din("bo", [1, D])
        din("w_gate", [INNER, D]); din("w_up", [INNER, D]); din("w_down", [D, INNER])
        din("identf", [128, 128]); din("identb", [128, 128], BF16)
        din("e64row", [128, 128])
        din("onesf", [128, 128]); din("bdiag", [128, 14])
        din("ropec", [128, S]); din("ropes", [128, S]); din("sevenv", [128, 1])
        dr["out"] = nc.dram_tensor("out", [nseq * S, D], F32,
                                   kind="ExternalOutput").ap()
        dr["xs"] = nc.dram_tensor("xs", [S, D], F32).ap()
        dr["qTs"] = nc.dram_tensor("qTs", [D, S], F32).ap()
        dr["TgS"] = nc.dram_tensor("TgS", [D, INNER], FP8).ap()
        dr["TuS"] = nc.dram_tensor("TuS", [D, INNER], FP8).ap()
        dr["TdS"] = nc.dram_tensor("TdS", [INNER, D], FP8).ap()
        for w in ("WqT", "WkT", "WvT", "WoT"):
            dr[w] = nc.dram_tensor(w, [D, D], F32).ap()
        self.dr = dr

    def _declare_sbuf(self):
        nc = self.nc
        S, NT, SWEEP, QCH = self.S, self.NT, self.SWEEP, self.QCH
        self.ARENA = 204 * 1024
        arena = nc.alloc_sbuf_tensor("arena", [128, self.ARENA // 4], F32)
        try:
            base = nc.lookup_mloc(arena).addr
        except Exception:
            base = 20480
        sb = {}
        off = [0]

        def place(name, shape, dt, at=None):
            if at is None:
                at = off[0]
                nbytes = int(np.prod(shape[1:])) * mybir.dt.size(dt)
                off[0] = _c32b(at + nbytes)
            sb[name] = nc.alloc_sbuf_tensor_at(name, shape, dt, offset=base + at)
            return at

        place("identf", [128, 128], F32)
        place("identb", [128, 128], BF16)
        place("onesf", [128, 128], F32)
        place("bdiag", [128, 14], F32)
        place("bbproj", [128, D], F32)
        place("bbo", [128, D], F32)
        place("bvb", [128, D], F32)
        place("biasqkv", [128, 16], F32)
        place("negc", [128, 16], F32)
        place("gad", [128, 12], F32)
        place("sstats", [128, 128], F32)
        place("snorm", [128, 64], F32)
        place("brow", [128, 1024], F32)
        place("e64row", [128, 128], F32)
        ph = off[0]
        self.PERSIST = ph
        kT_at = ph
        Vb_at = kT_at + NCH * S * 4
        yfm_at = Vb_at + NT * (H * HD + H) * 4
        wbuf_at = yfm_at + NCH * SWEEP * 4
        stage_at = wbuf_at + 16 * 1024
        attn_end = stage_at + 8 * 1024
        assert attn_end <= self.ARENA, f"attn {attn_end}"
        place("kT", [128, NCH * S], F32, kT_at)
        place("Vb", [128, NT * (H * HD + H)], F32, Vb_at)
        place("yfm", [128, NCH * SWEEP], F32, yfm_at)
        place("wbuf", [128, 4096], F32, wbuf_at)
        place("stage", [128, 2048], F32, stage_at)
        obuf_at = yfm_at
        abuf_at = obuf_at + NCH * QCH * 4
        assert abuf_at + 8 * QCH * 4 <= stage_at, "attn overlay"
        place("qbuf", [128, 2 * QCH], F32, stage_at)
        place("obuf", [128, NCH * QCH], F32, obuf_at)
        place("abuf", [128, 8 * QCH], F32, abuf_at)
        place("xpb", [128, 512], F32, abuf_at)
        place("wp", [128, 1024], F32, abuf_at + 512 * 4)
        tg_at = ph
        tu_at = tg_at + NCH * INNER
        td_at = tu_at + NCH * INNER
        hb_at = _c32b(td_at + 32 * D)
        silu_at = hb_at + INNER * 4
        q2t_at = silu_at + INNER * 4
        q2f_at = q2t_at + INNER * 2
        q1t_at = q2f_at + INNER * 2
        q1f_at = q1t_at + 2 * D * 2
        fx_at = _c32b(q1f_at + 2 * D * 2)
        yq_at = fx_at + 4 * D * 4
        assert yq_at + 2 * D * 4 <= self.ARENA, "ffn"
        place("tg", [128, NCH * INNER], FP8, tg_at)
        place("tu", [128, NCH * INNER], FP8, tu_at)
        place("td", [128, 32 * D], FP8, td_at)
        place("hb", [128, INNER], F32, hb_at)
        place("silu", [128, INNER], F32, silu_at)
        place("q2t", [128, INNER], BF16, q2t_at)
        place("q2f", [128, INNER], BF16, q2f_at)
        place("q1t", [128, 2 * D], BF16, q1t_at)
        place("q1f", [128, 2 * D], BF16, q1f_at)
        place("fx", [128, 4 * D], F32, fx_at)
        place("yq", [128, 2 * D], F32, yq_at)
        pw_at = ph
        ptb_at = pw_at + 8 * 1024 * 4
        pst_at = ptb_at + 32 * 1024 * 2
        assert pst_at + 4096 * 4 <= self.ARENA, "prep"
        place("pw", [128, 8 * 1024], F32, pw_at)
        place("ptb", [128, 32 * 1024], BF16, ptb_at)
        place("pst", [128, 4096], F32, pst_at)
        place("pst8", [128, 4096], FP8, pst_at)
        self.sb = sb
        self.pb = [nc.alloc_psum_tensor(f"pb{i}", [128, 512], F32)
                   for i in range(7)]
        self.pbb = nc.alloc_psum_tensor("pbb", [128, 1024], BF16)

    def a(self, name, rows=None, cols=None):
        ap = self.sb[name].ap()
        r = rows or (0, 128)
        if cols is None:
            return ap[r[0]:r[1], :]
        return ap[r[0]:r[1], cols[0]:cols[1]]

    def col(self, name, c):
        return self.a(name, cols=(c, c + 1))

    def mkap(self, name, col_off, dims):
        ap0 = self.sb[name].ap()
        return bass.AP(ap0.tensor, ap0.offset + col_off,
                       [[ap0.ap[0][0], 128]] + [list(d) for d in dims])

    def emit_gadget(self):
        S_ = self.S_
        S_.emit("vector", lambda e: e.memset(self.a("brow"), 0.0))
        S_.emit("vector", lambda e: e.memset(self.col("gad", 4), EPS))
        S_.emit("vector", lambda e: e.memset(self.col("gad", 5), 0.125))
        S_.dma(lambda: self.col("sstats", 0), lambda: self.dr["sevenv"])
        S_.wait_all("vector", "dml")
        S_.emit("vector", lambda e: e.tensor_scalar(
            self.col("gad", 2), self.col("sstats", 0), C64, -C64,
            op0=ALU.add, op1=ALU.add))
        S_.wait_all("vector", "ve")
        S_.emit("vector", lambda e: e.tensor_scalar(
            self.col("gad", 3), self.col("gad", 2), 0.125, None, op0=ALU.mult))
        S_.wait_all("vector", "ve")
        S_.emit("vector", lambda e: e.tensor_scalar(
            self.col("gad", 0), self.col("gad", 3), C64 - C32, C32,
            op0=ALU.mult, op1=ALU.add))
        S_.wait_all("vector", "ve")
        S_.emit("vector", lambda e: e.tensor_scalar(
            self.col("gad", 1), self.col("gad", 0), -1.0, None, op0=ALU.mult))

    def emit_consts(self):
        S_ = self.S_
        for n in ("identf", "identb", "onesf", "bdiag", "e64row"):
            S_.dma(lambda n=n: self.a(n), lambda n=n: self.dr[n])
        for i, bsrc in enumerate(("bq", "bk")):
            S_.dma(lambda i=i: self.a("pst", rows=(0, 8),
                                      cols=(i * 128, i * 128 + 128)),
                   lambda b=bsrc: self.dr[b][0:1, :].rearrange(
                       "a (p c) -> (a p) c", c=128))
        for i, bsrc in enumerate(("bv", "b_proj", "bo")):
            S_.dma(lambda i=i: self.a("pst", rows=(0, 1),
                                      cols=(1024 * (i + 1), 1024 * (i + 2))),
                   lambda b=bsrc: self.dr[b][0:1, :])
        S_.wait_all("tensor", "dml")
        for i in range(2):
            S_.transpose(lambda: self.pb[1].ap()[:, 0:8],
                         lambda i=i: self.a("pst", rows=(0, 8),
                                            cols=(i * 128, i * 128 + 128)),
                         lambda: self.a("identf", rows=(0, 8), cols=(0, 8)))
            S_.wait_all("vector", "te")
            S_.emit("vector", lambda e, i=i: e.tensor_copy(
                self.a("biasqkv", cols=(8 * i, 8 * i + 8)),
                self.pb[1].ap()[:, 0:8]))
            S_.wait_all("tensor", "ve")
        for dst, i in [("bvb", 1), ("bbproj", 2), ("bbo", 3)]:
            for hf in range(2):
                S_.mm(lambda: self.pb[0].ap()[:, 0:512],
                      lambda: self.a("onesf", rows=(0, 1)),
                      lambda i=i, hf=hf: self.a(
                          "pst", rows=(0, 1),
                          cols=(1024 * i + 512 * hf, 1024 * i + 512 * (hf + 1))),
                      kdim=1, mdim=128, start=True, stop=True)
                S_.wait_all("vector", "te")
                S_.emit("vector", lambda e, d=dst, hf=hf: e.tensor_copy(
                    self.a(d, cols=(512 * hf, 512 * (hf + 1))),
                    self.pb[0].ap()[:, 0:512]))
                S_.wait_all("tensor", "ve")

    def emit_prep_tquant(self, wname, R, C, dst, midx):
        S_ = self.S_
        RB = R // 128
        S_.wait_all("scalar", "te", "ve", "dms")
        for r in range(RB):
            sl = r % 2
            S_.wait_all("sync", "se")
            S_.dma(lambda sl=sl, C=C: self.a("pw", cols=(sl * C, sl * C + C)),
                   lambda r=r, w=wname: self.dr[w][128 * r:128 * (r + 1), :])
            S_.wait_all("scalar", "dml")
            S_.emit("scalar", lambda e, sl=sl, r=r, C=C: e.activation(
                self.a("pst", cols=(0, C)),
                self.a("pw", cols=(sl * C, sl * C + C)), AF.Abs,
                accum_out=self.col("snorm", r)))
        S_.wait_all("tensor", "se")
        S_.mm(lambda RB=RB: self.pb[0].ap()[0:1, 0:RB],
              lambda: self.a("onesf", cols=(0, 1)),
              lambda RB=RB: self.a("snorm", cols=(0, RB)),
              kdim=128, mdim=1, start=True, stop=True)
        S_.wait_all("vector", "te")
        S_.emit("vector", lambda e, RB=RB: e.tensor_copy(
            self.a("sstats", rows=(0, 1), cols=(40, 40 + RB)),
            self.pb[0].ap()[0:1, 0:RB]))
        S_.wait_all("vector", "ve")
        S_.emit("vector", lambda e, RB=RB: e.tensor_reduce(
            self.a("sstats", rows=(0, 1), cols=(72, 73)),
            self.a("sstats", rows=(0, 1), cols=(40, 40 + RB)),
            axis=AXX, op=ALU.add))
        S_.wait_all("vector", "ve")
        S_.emit("vector", lambda e, R=R, C=C: e.tensor_scalar(
            self.a("sstats", rows=(0, 1), cols=(73, 74)),
            self.a("sstats", rows=(0, 1), cols=(72, 73)),
            1.0 / (R * C), 1e-5, op0=ALU.mult, op1=ALU.max))
        S_.wait_all("vector", "ve")
        S_.emit("vector", lambda e: e.reciprocal(
            self.a("sstats", rows=(0, 1), cols=(74, 75)),
            self.a("sstats", rows=(0, 1), cols=(73, 74))))
        S_.wait_all("tensor", "ve")
        S_.mm(lambda: self.pb[0].ap()[:, 0:1],
              lambda: self.a("onesf", rows=(0, 1)),
              lambda: self.a("sstats", rows=(0, 1), cols=(74, 75)),
              kdim=1, mdim=128, start=True, stop=True)
        S_.wait_all("vector", "te")
        S_.emit("vector", lambda e: e.tensor_copy(
            self.col("sstats", 75), self.pb[0].ap()[:, 0:1]))
        S_.wait_all("tensor", "ve")
        S_.mm(lambda: self.pb[1].ap()[:, 0:1],
              lambda: self.a("onesf", rows=(0, 1)),
              lambda: self.a("sstats", rows=(0, 1), cols=(73, 74)),
              kdim=1, mdim=128, start=True, stop=True)
        S_.wait_all("vector", "te")
        S_.emit("vector", lambda e, midx=midx: e.tensor_copy(
            self.col("gad", 6 + midx), self.pb[1].ap()[:, 0:1]))
        for r in range(RB):
            sl = r % 2
            S_.wait_all("sync", "se", "ve")
            S_.dma(lambda sl=sl, C=C: self.a("pw", cols=(sl * C, sl * C + C)),
                   lambda r=r, w=wname: self.dr[w][128 * r:128 * (r + 1), :])
            S_.wait_all("scalar", "dml", "ve")
            S_.emit("scalar", lambda e, sl=sl, C=C: e.activation(
                self.a("pst", cols=(0, C)),
                self.a("pw", cols=(sl * C, sl * C + C)), AF.Identity,
                bias=self.col("gad", 0), scale=self.col("sstats", 75)))
            S_.wait_all("vector", "se")
            S_.emit("vector", lambda e, C=C: e.tensor_scalar(
                self.a("pst", cols=(C, 2 * C)) if 2 * C <= 4096 else
                self.a("pst", cols=(0, C)),
                self.a("pst", cols=(0, C)),
                self.col("gad", 1), -1.0, op0=ALU.add, op1=ALU.max))
            S_.wait_all("vector", "ve")
            src_cols = (C, 2 * C) if 2 * C <= 4096 else (0, C)
            S_.emit("vector", lambda e, r=r, C=C, sc=src_cols: e.tensor_scalar(
                self.a("ptb", cols=(r * C, r * C + C)),
                self.a("pst", cols=sc), 1.0, None, op0=ALU.min))
        S_.wait_all("tensor", "ve")
        for cc in range(C // 128):
            for rr in range(RB):
                S_.transpose(lambda: self.pbb.ap()[:, 0:128],
                             lambda rr=rr, cc=cc, C=C: self.a(
                                 "ptb", cols=(rr * C + cc * 128,
                                              rr * C + cc * 128 + 128)),
                             lambda: self.a("identb"))
                S_.wait_all("scalar", "te")
                S_.emit("scalar", lambda e, rr=rr: e.copy(
                    self.a("pst8", cols=(rr * 128, rr * 128 + 128)),
                    self.pbb.ap()[:, 0:128]))
                S_.wait_all("tensor", "se")
            S_.wait_all("sync", "se", "dms")
            S_.dma_store(lambda cc=cc, R=R, d=dst: self.dr[d][
                128 * cc:128 * (cc + 1), 0:R],
                lambda R=R: self.a("pst8", cols=(0, R)))
            S_.wait_all("scalar", "dms")

    def emit_prep_wt(self, wname, dst):
        S_ = self.S_
        S_.wait_all("sync", "te", "ve", "dms")
        for r in range(8):
            S_.dma(lambda r=r: self.a("pw", cols=(r * 1024, r * 1024 + 1024)),
                   lambda r=r, w=wname: self.dr[w][128 * r:128 * (r + 1), :])
        S_.wait_all("tensor", "dml")
        for cc in range(8):
            for rr in range(8):
                S_.transpose(lambda rr=rr: self.pb[rr % 2].ap()[:, 0:128],
                             lambda rr=rr, cc=cc: self.a(
                                 "pw", cols=(rr * 1024 + cc * 128,
                                             rr * 1024 + cc * 128 + 128)),
                             lambda: self.a("identf"))
                S_.wait_all("vector", "te")
                S_.emit("vector", lambda e, rr=rr: e.tensor_copy(
                    self.a("pst", cols=(rr * 128, rr * 128 + 128)),
                    self.pb[rr % 2].ap()[:, 0:128]))
                S_.wait_all("tensor", "ve")
            S_.wait_all("sync", "ve", "dms")
            S_.dma_store(lambda cc=cc, d=dst: self.dr[d][
                128 * cc:128 * (cc + 1), :],
                lambda: self.a("pst", cols=(0, 1024)))
            S_.wait_all("vector", "dms")

    def emit_ln(self, x_fn, y_fn, b=8):
        S_ = self.S_
        S_.wait_all("vector", "se", "ve")
        S_.emit("vector", lambda e, f=x_fn, b=b: e.bn_stats(
            self.a("sstats", cols=(b, b + 6)), f()[:, 0:512]))
        S_.emit("vector", lambda e, f=x_fn, b=b: e.bn_stats(
            self.a("sstats", cols=(b + 6, b + 12)), f()[:, 512:1024]))
        S_.wait_all("vector", "ve")
        S_.emit("vector", lambda e, b=b: e.bn_aggr(
            self.a("sstats", cols=(b + 12, b + 14)),
            self.a("sstats", cols=(b, b + 12))))
        S_.wait_all("scalar", "ve")
        S_.emit("scalar", lambda e, b=b: e.activation(
            self.col("sstats", b + 14), self.col("sstats", b + 13), AF.Sqrt,
            bias=self.col("gad", 4), scale=1.0))
        S_.wait_all("vector", "se")
        S_.emit("vector", lambda e, b=b: e.reciprocal(
            self.col("sstats", b + 15), self.col("sstats", b + 14)))
        S_.wait_all("vector", "ve")
        S_.emit("vector", lambda e, b=b: e.scalar_tensor_tensor(
            self.col("sstats", b + 16), self.col("sstats", b + 12), -1.0,
            self.col("sstats", b + 15), op0=ALU.mult, op1=ALU.mult))
        S_.wait_all("scalar", "ve", "te")
        S_.emit("scalar", lambda e, xf=x_fn, yf=y_fn, b=b: e.activation(
            yf(), xf(), AF.Identity,
            bias=self.col("sstats", b + 16), scale=self.col("sstats", b + 15)))

    def emit_actquant(self, in_fn, out_fn, a_fn, qb=32):
        S_ = self.S_
        S_.wait_all("vector", "se", "ve")
        S_.emit("vector", lambda e, f=in_fn, qb=qb: e.tensor_reduce(
            self.col("sstats", qb), f(), axis=AXX, op=ALU.max,
            apply_absolute_value=True))
        S_.wait_all("vector", "ve")
        S_.emit("vector", lambda e, qb=qb: e.tensor_scalar(
            self.col("sstats", qb + 1), self.col("sstats", qb), 1e-5, None,
            op0=ALU.max))
        S_.wait_all("vector", "ve")
        S_.emit("vector", lambda e, qb=qb: e.reciprocal(
            self.col("sstats", qb + 2), self.col("sstats", qb + 1)))
        S_.wait_all("vector", "ve")
        S_.emit("vector", lambda e, qb=qb: e.tensor_scalar(
            self.col("sstats", qb + 3), self.col("sstats", qb + 2), 127.0, None,
            op0=ALU.mult))
        S_.wait_all("scalar", "ve")
        S_.emit("scalar", lambda e, f=in_fn, af=a_fn, qb=qb: e.activation(
            af(), f(), AF.Identity,
            bias=self.col("gad", 0), scale=self.col("sstats", qb + 3)))
        S_.wait_all("vector", "se")
        S_.emit("vector", lambda e, of=out_fn, af=a_fn: e.tensor_scalar(
            of(), af(), self.col("gad", 1), None, op0=ALU.add))

    def emit_attn(self, kind, xsrc_row0, xsrc, xdst, xdst_row0):
        S_ = self.S_
        S, SWEEP, NSW, SWT = self.S, self.SWEEP, self.NSW, self.SWT
        QCH, NQC, KT, NT = self.QCH, self.NQC, self.KT, self.NT
        rope = kind == "a"
        causal = kind == "b"

        S_.wait_all("vector", "te", "se", "ge", "dms")
        S_.emit("vector", lambda e: e.memset(
            self.mkap("Vb", HD, [[H * HD + H, NT], [HD + 1, H]]), 1.0))
        S_.emit("vector", lambda e: e.memset(self.a("snorm", cols=(32, 48)), 0.0))

        for sw in range(NSW):
            def ybuf(t4):
                name = "stage" if t4 % 2 == 0 else "wbuf"
                return name

            def gen_transposes(t4):
                S_.wait_all("tensor", "se")
                for grp in range(2):
                    S_.wait_all("tensor", "ve")
                    for cc in range(4):
                        c = grp * 4 + cc
                        S_.transpose(lambda grp=grp, cc=cc:
                                     self.pb[grp % 2].ap()[:, cc * 128:
                                                           cc * 128 + 128],
                                     lambda c=c, t4=t4: self.a(
                                         ybuf(t4), cols=(1024 + c * 128,
                                                         1024 + c * 128 + 128)),
                                     lambda: self.a("identf"))
                    S_.wait_all("vector", "te")
                    S_.emit("vector", lambda e, grp=grp, t4=t4: e.tensor_copy(
                        self.mkap("yfm", grp * 4 * SWEEP + t4 * 128,
                                  [[SWEEP, 4], [1, 128]]),
                        self.pb[grp % 2].ap()[:, 0:512]))

            for t4 in range(SWT):
                t = sw * SWT + t4
                if t4 >= 1:
                    gen_transposes(t4 - 1)
                S_.wait_all("sync", "se", "dms", "te")
                S_.dma(lambda t4=t4: self.a(ybuf(t4), cols=(0, 1024)),
                       lambda t=t: xsrc[xsrc_row0 + 128 * t:
                                        xsrc_row0 + 128 * (t + 1), :])
                S_.wait_all("vector", "dml")
                S_.wait_all("scalar", "dml")
                self.emit_ln(lambda t4=t4: self.a(ybuf(t4), cols=(0, 1024)),
                             lambda t4=t4: self.a(ybuf(t4), cols=(1024, 2048)),
                             b=8 + 32 * (t4 % 2))
            gen_transposes(SWT - 1)
            for vb in range(2):
                S_.wait_all("sync", "te", "ve")
                for c in range(NCH):
                    if kind == "a":
                        src = (lambda c=c, vb=vb: self.dr["w_qkv"][
                            128 * c:128 * (c + 1),
                            2 * D + vb * 512:2 * D + (vb + 1) * 512])
                    else:
                        src = (lambda c=c, vb=vb: self.dr["WvT"][
                            128 * c:128 * (c + 1), vb * 512:(vb + 1) * 512])
                    S_.dma(lambda c=c: self.a("wbuf",
                                              cols=(c * 512, c * 512 + 512)), src)
                S_.wait_all("tensor", "dml", "ve")
                for t4 in range(SWT):
                    t = sw * SWT + t4
                    bank = 4 + t4 % 2
                    for c in range(NCH):
                        S_.mm(lambda bank=bank: self.pb[bank].ap()[:, 0:512],
                              lambda c=c, t4=t4: self.a(
                                  "yfm", cols=(c * SWEEP + t4 * 128,
                                               c * SWEEP + t4 * 128 + 128)),
                              lambda c=c: self.a("wbuf",
                                                 cols=(c * 512, c * 512 + 512)),
                              kdim=128, mdim=128, start=(c == 0),
                              stop=(c == NCH - 1))
                    S_.wait_all("vector", "te")

                    def vevac(e, t=t, vb=vb, bank=bank, kind=kind):
                        o = t * (H * HD + H) + vb * 8 * (HD + 1)
                        dst = self.mkap("Vb", o, [[HD + 1, 8], [1, HD]])
                        sap = self.pb[bank].ap()
                        src_ = bass.AP(sap.tensor, sap.offset,
                                       [[sap.ap[0][0], 128], [HD, 8], [1, HD]])
                        if kind == "b":
                            bias = self.mkap("bvb", vb * 512, [[HD, 8], [1, HD]])
                            return e.tensor_tensor(dst, src_, bias, op=ALU.add)
                        return e.tensor_copy(dst, src_)
                    S_.emit("vector", vevac)
                    S_.wait_all("tensor", "ve")
            for which in ("q", "k"):
                base_col = 0 if which == "q" else D
                for g in range(4):
                    S_.wait_all("sync", "te", "ve", "se", "dms")
                    for c in range(NCH):
                        if kind == "a":
                            src = (lambda c=c, g=g, b=base_col: self.dr["w_qkv"][
                                128 * c:128 * (c + 1), b + 256 * g:b + 256 * (g + 1)])
                        else:
                            wsc = "WqT" if which == "q" else "WkT"
                            src = (lambda c=c, g=g, w=wsc: self.dr[w][
                                128 * c:128 * (c + 1), 256 * g:256 * (g + 1)])
                        S_.dma(lambda c=c: self.a("wbuf",
                                                  cols=(c * 256, c * 256 + 256)), src)
                    if rope:
                        S_.dma(lambda: self.a("wbuf", cols=(2048, 2048 + SWEEP)),
                               lambda sw=sw: self.dr["ropec"][
                                   :, sw * SWEEP:(sw + 1) * SWEEP])
                        S_.dma(lambda: self.a("wbuf",
                                              cols=(2048 + SWEEP, 2048 + 2 * SWEEP)),
                               lambda sw=sw: self.dr["ropes"][
                                   :, sw * SWEEP:(sw + 1) * SWEEP])
                    S_.wait_all("tensor", "dml")
                    for par in range(2):
                        m = 2 * g + par
                        for c in range(NCH):
                            if kind == "a":
                                lhs = (lambda c=c, par=par: self.mkap(
                                    "wbuf", c * 256 + par, [[2, 128]]))
                            else:
                                lhs = (lambda c=c, par=par: self.a(
                                    "wbuf", cols=(c * 256 + par * 128,
                                                  c * 256 + par * 128 + 128)))
                            S_.mm(lambda: self.pb[6].ap()[:, 0:SWEEP],
                                  lhs,
                                  lambda c=c: self.a(
                                      "yfm", cols=(c * SWEEP, (c + 1) * SWEEP)),
                                  kdim=128, mdim=128, start=(c == 0),
                                  stop=(c == NCH - 1))
                        if kind == "b":
                            S_.wait_all("scalar", "te", "ve", "dms")
                            S_.emit("scalar", lambda e, par=par, m=m, w=which:
                                    e.activation(
                                        self.a("stage", cols=(par * SWEEP,
                                                              (par + 1) * SWEEP)),
                                        self.pb[6].ap()[:, 0:SWEEP], AF.Identity,
                                        bias=self.col("biasqkv",
                                                      (0 if w == "q" else 8) + m),
                                        scale=1.0))
                            S_.wait_all("tensor", "se")
                        else:
                            S_.wait_all("vector", "te", "dms")
                            S_.emit("vector", lambda e, par=par: e.tensor_copy(
                                self.a("stage", cols=(par * SWEEP,
                                                      (par + 1) * SWEEP)),
                                self.pb[6].ap()[:, 0:SWEEP]))
                            S_.wait_all("tensor", "ve")
                    qe = lambda: self.a("stage", cols=(0, SWEEP))
                    qo = lambda: self.a("stage", cols=(SWEEP, 2 * SWEEP))
                    if rope:
                        cosv = lambda: self.a("wbuf", cols=(2048, 2048 + SWEEP))
                        sinv = lambda: self.a("wbuf", cols=(2048 + SWEEP,
                                                            2048 + 2 * SWEEP))
                        tA = lambda: self.a("wbuf", cols=(3072, 3072 + SWEEP))
                        tB = lambda: self.a("wbuf", cols=(3072 + SWEEP,
                                                          3072 + 2 * SWEEP))
                        re_ = lambda: self.a("stage", cols=(1024, 1024 + SWEEP))
                        ro_ = lambda: self.a("stage", cols=(1024 + SWEEP,
                                                            1024 + 2 * SWEEP))
                        S_.wait_all("vector", "se", "dml", "te", "ve")
                        S_.emit("vector", lambda e: e.tensor_tensor(
                            tA(), qe(), cosv(), op=ALU.mult))
                        S_.emit("vector", lambda e: e.tensor_tensor(
                            tB(), qo(), sinv(), op=ALU.mult))
                        S_.wait_all("vector", "ve")
                        S_.emit("vector", lambda e: e.tensor_tensor(
                            re_(), tA(), tB(), op=ALU.subtract))
                        S_.wait_all("vector", "ve")
                        S_.emit("vector", lambda e: e.tensor_tensor(
                            tA(), qe(), sinv(), op=ALU.mult))
                        S_.emit("vector", lambda e: e.tensor_tensor(
                            tB(), qo(), cosv(), op=ALU.mult))
                        S_.wait_all("vector", "ve")
                        S_.emit("vector", lambda e: e.tensor_tensor(
                            ro_(), tA(), tB(), op=ALU.add))
                        outE, outO = re_, ro_
                    else:
                        outE, outO = qe, qo
                    S_.wait_all("scalar", "ve", "te", "se")
                    S_.emit("scalar", lambda e, f=outE: e.activation(
                        self.a("wbuf", cols=(3072, 3072 + SWEEP)), f(),
                        AF.Square))
                    S_.emit("scalar", lambda e, f=outO: e.activation(
                        self.a("wbuf", cols=(3072 + SWEEP, 3072 + 2 * SWEEP)),
                        f(), AF.Square))
                    S_.wait_all("tensor", "se")
                    ncol = (32 if which == "q" else 40) + g
                    for par in range(2):
                        bc = (0, 4) if kind == "a" else (6 + 4 * par, 10 + 4 * par)
                        S_.mm(lambda: self.pb[3].ap()[0:4, 0:SWEEP],
                              lambda bc=bc: self.a("bdiag", cols=bc),
                              lambda par=par: self.a(
                                  "wbuf", cols=(3072 + par * SWEEP,
                                                3072 + (par + 1) * SWEEP)),
                              kdim=128, mdim=4, start=(par == 0),
                              stop=(par == 1))
                    S_.wait_all("vector", "te")
                    S_.emit("vector", lambda e: e.tensor_reduce(
                        self.a("snorm", rows=(0, 4), cols=(0, 1)),
                        self.pb[3].ap()[0:4, 0:SWEEP], axis=AXX, op=ALU.max))
                    S_.wait_all("vector", "ve")
                    S_.emit("vector", lambda e, nc_=ncol: e.tensor_tensor(
                        self.a("snorm", rows=(0, 4), cols=(nc_, nc_ + 1)),
                        self.a("snorm", rows=(0, 4), cols=(nc_, nc_ + 1)),
                        self.a("snorm", rows=(0, 4), cols=(0, 1)),
                        op=ALU.max))
                    if which == "q":
                        S_.wait_all("sync", "ve", "se")
                        for par, f in ((0, outE), (1, outO)):
                            m = 2 * g + par
                            S_.dma_store(lambda m=m, sw=sw: self.dr["qTs"][
                                128 * m:128 * (m + 1),
                                sw * SWEEP:(sw + 1) * SWEEP], f)
                    else:
                        S_.wait_all("vector", "ve", "se")
                        for par, f in ((0, outE), (1, outO)):
                            m = 2 * g + par
                            S_.emit("vector", lambda e, m=m, sw=sw, f=f:
                                    e.tensor_copy(
                                        self.a("kT",
                                               cols=(m * S + sw * SWEEP,
                                                     m * S + (sw + 1) * SWEEP)),
                                        f()))
                        S_.wait_all("sync", "ve")

        S_.wait_all("vector", "ve", "te", "se")
        for g in range(4):
            S_.emit("vector", lambda e, g=g: e.tensor_tensor(
                self.a("snorm", rows=(0, 4), cols=(8 + g, 9 + g)),
                self.a("snorm", rows=(0, 4), cols=(32 + g, 33 + g)),
                self.a("snorm", rows=(0, 4), cols=(40 + g, 41 + g)),
                op=ALU.mult))
        S_.wait_all("scalar", "ve")
        for g in range(4):
            S_.emit("scalar", lambda e, g=g: e.activation(
                self.a("snorm", rows=(0, 4), cols=(12 + g, 13 + g)),
                self.a("snorm", rows=(0, 4), cols=(8 + g, 9 + g)), AF.Sqrt))
        S_.wait_all("vector", "se")
        for g in range(4):
            S_.emit("vector", lambda e, g=g: e.tensor_scalar(
                self.a("snorm", rows=(0, 4), cols=(16 + g, 17 + g)),
                self.a("snorm", rows=(0, 4), cols=(12 + g, 13 + g)),
                -0.125, None, op0=ALU.mult))
        S_.wait_all("tensor", "ve")
        for g in range(4):
            S_.transpose(lambda: self.pb[2].ap()[0:1, 0:4],
                         lambda g=g: self.a("snorm", rows=(0, 4),
                                            cols=(16 + g, 17 + g)),
                         lambda: self.a("identf", rows=(0, 4), cols=(0, 4)))
            S_.wait_all("vector", "te")
            S_.emit("vector", lambda e: e.tensor_copy(
                self.a("sstats", rows=(0, 1), cols=(100, 104)),
                self.pb[2].ap()[0:1, 0:4]))
            S_.wait_all("tensor", "ve")
            S_.mm(lambda: self.pb[2].ap()[:, 0:4],
                  lambda: self.a("onesf", rows=(0, 1)),
                  lambda: self.a("sstats", rows=(0, 1), cols=(100, 104)),
                  kdim=1, mdim=128, start=True, stop=True)
            S_.wait_all("vector", "te")
            S_.emit("vector", lambda e, g=g: e.tensor_copy(
                self.a("negc", cols=(4 * g, 4 * g + 4)),
                self.pb[2].ap()[:, 0:4]))
            S_.wait_all("tensor", "ve")

        wps = "w_proj" if kind == "a" else "WoT"
        bbst = "bbproj" if kind == "a" else "bbo"
        NTT = QCH // 128
        S_.sem("wq0"); S_.sem("wq1")
        for qc in range(NQC):
            for e_ in ("scalar", "gpsimd", "tensor", "vector"):
                S_.wait_all(e_, "wq0", "wq1")
            for g in range(4):
                S_.wait_all("sync", "te", "se", "ve")
                for par in range(2):
                    m = 2 * g + par
                    S_.dma(lambda par=par: self.a(
                        "qbuf", cols=(par * QCH, (par + 1) * QCH)),
                        lambda m=m, qc=qc: self.dr["qTs"][
                            128 * m:128 * (m + 1), qc * QCH:(qc + 1) * QCH])
                S_.wait_all("tensor", "dml")
                for pi in range(2):
                    hs = [4 * g + 2 * pi, 4 * g + 2 * pi + 1]
                    kts = [kt for kt in range(KT)
                           if not (causal and 128 * kt > qc * QCH + QCH - 1)]
                    KCH = 4
                    nchk = (len(kts) + KCH - 1) // KCH
                    for ci in range(nchk):
                        chunk = kts[ci * KCH:(ci + 1) * KCH]
                        for li, kt in enumerate(chunk):
                            gki = ci * KCH + li
                            S_.wait_all("tensor", "se")
                            bk = (gki % 2) * 2
                            for j, h in enumerate(hs):
                                i = h % 4
                                if kind == "a":
                                    parts = [(2 * g, 32 * i, 32),
                                             (2 * g + 1, 32 * i, 32)]
                                else:
                                    parts = [(2 * g + i // 2, 64 * (i % 2), 64)]
                                for pidx, (ch, r0, kk) in enumerate(parts):
                                    par_idx = ch - 2 * g
                                    S_.mm(lambda bk=bk, j=j:
                                          self.pb[bk + j].ap()[:, 0:QCH],
                                          lambda ch=ch, r0=r0, kk=kk, kt=kt:
                                          self.a("kT", rows=(r0, r0 + kk),
                                                 cols=(ch * S + kt * 128,
                                                       ch * S + kt * 128 + 128)),
                                          lambda pp=par_idx, r0=r0, kk=kk:
                                          self.a("qbuf", rows=(r0, r0 + kk),
                                                 cols=(pp * QCH, (pp + 1) * QCH)),
                                          kdim=kk, mdim=128, start=(pidx == 0),
                                          stop=(pidx == len(parts) - 1),
                                          tile_position=(r0, 0))
                            S_.wait_all("scalar", "te")
                            diag = causal and (kt * 128 + 127 > qc * QCH)
                            for j, h in enumerate(hs):
                                sl = (gki % 4) * 2 + j
                                S_.emit("scalar", lambda e, bk=bk, j=j, h=h, sl=sl:
                                        e.activation(
                                            self.a("abuf", cols=(sl * QCH,
                                                                 (sl + 1) * QCH)),
                                            self.pb[bk + j].ap()[:, 0:QCH], AF.Exp,
                                            bias=self.col("negc", h),
                                            scale=self.col("gad", 5)))
                            if diag:
                                S_.wait_all("gpsimd", "se")
                                for j in range(2):
                                    sl = (gki % 4) * 2 + j
                                    S_.emit("gpsimd", lambda e, sl=sl, qc=qc,
                                            kt=kt: e.affine_select(
                                                self.a("abuf", cols=(sl * QCH,
                                                                     (sl + 1) * QCH)),
                                                self.a("abuf", cols=(sl * QCH,
                                                                     (sl + 1) * QCH)),
                                                [[1, QCH]], ALU.is_ge, 0.0,
                                                base=qc * QCH - kt * 128,
                                                channel_multiplier=-1))
                        S_.wait_all("tensor", "se", "ge", "ve")
                        for li, kt in enumerate(chunk):
                            gki = ci * KCH + li
                            for j, h in enumerate(hs):
                                sl = (gki % 4) * 2 + j
                                S_.mm(lambda j=j: self.pb[4 + j].ap()[0:65, 0:QCH],
                                      lambda h=h, kt=kt: self.a(
                                          "Vb", cols=(kt * (H * HD + H)
                                                      + h * (HD + 1),
                                                      kt * (H * HD + H)
                                                      + h * (HD + 1) + HD + 1)),
                                      lambda sl=sl: self.a(
                                          "abuf", cols=(sl * QCH, (sl + 1) * QCH)),
                                      kdim=128, mdim=65, start=(gki == 0),
                                      stop=(gki == len(kts) - 1))
                    S_.wait_all("vector", "te", "ve")
                    for j in range(2):
                        S_.emit("vector", lambda e, j=j: e.reciprocal(
                            self.a("brow", rows=(64, 65),
                                   cols=(j * QCH, j * QCH + QCH)),
                            self.pb[4 + j].ap()[64:65, 0:QCH]))
                    S_.wait_all("tensor", "ve")
                    for j in range(2):
                        S_.mm(lambda j=j: (self.pb[6] if j == 0
                                           else self.pb[j]).ap()[0:65, 0:QCH],
                              lambda: self.a("e64row", cols=(0, 65)),
                              lambda j=j: self.a("brow",
                                                 cols=(j * QCH, j * QCH + QCH)),
                              kdim=128, mdim=65, start=True, stop=True)
                    S_.wait_all("scalar", "te", "dms")
                    for j in range(2):
                        S_.emit("scalar", lambda e, j=j: e.copy(
                            self.a("stage", rows=(0, 64),
                                   cols=(1024 + j * QCH, 1024 + (j + 1) * QCH)),
                            (self.pb[6] if j == 0 else self.pb[j]).ap()[0:64, 0:QCH]))
                    S_.wait_all("vector", "se")
                    for j, h in enumerate(hs):
                        S_.emit("vector", lambda e, j=j, h=h: e.tensor_tensor(
                            self.a("obuf", rows=(64 * (h % 2), 64 * (h % 2) + 64),
                                   cols=((h // 2) * QCH, (h // 2 + 1) * QCH)),
                            self.pb[4 + j].ap()[0:64, 0:QCH],
                            self.a("stage", rows=(0, 64),
                                   cols=(1024 + j * QCH, 1024 + (j + 1) * QCH)),
                            op=ALU.mult))
            S_.sem("wq0"); S_.sem("wq1")
            te_marks = {}

            def proj_mms(oc, c):
                sl = c % 2
                S_.wait_val("tensor", f"wq{sl}", S_.mark(f"wq{sl}"))
                for tt in range(NTT):
                    S_.mm(lambda tt=tt: self.pb[tt].ap()[:, 0:512],
                          lambda c=c, tt=tt: self.a(
                              "obuf", cols=(c * QCH + tt * 128,
                                            c * QCH + tt * 128 + 128)),
                          lambda sl=sl: self.a("wp", cols=(sl * 512,
                                                           sl * 512 + 512)),
                          kdim=128, mdim=128, start=(c == 0),
                          stop=(c == NCH - 1))
                te_marks[c] = S_.mark("te")

            m_ve_attn = S_.mark("ve")
            m_ge_attn = S_.mark("ge")
            for oc in range(2):
                te_marks.clear()
                for c in range(NCH):
                    sl = c % 2
                    if c >= 2:
                        S_.wait_val("sync", "te", te_marks[c - 2])
                    elif oc == 0:
                        S_.wait_val("sync", "ve", m_ve_attn)
                        S_.wait_val("sync", "ge", m_ge_attn)
                    else:
                        S_.wait_all("sync", "te")
                    S_.dma(lambda sl=sl: self.a("wp", cols=(sl * 512,
                                                            sl * 512 + 512)),
                           lambda c=c, oc=oc, w=wps: self.dr[w][
                               128 * c:128 * (c + 1), oc * 512:(oc + 1) * 512],
                           sem=f"wq{sl}")
                    if c >= 1:
                        proj_mms(oc, c - 1)
                proj_mms(oc, NCH - 1)
                S_.wait_all("vector", "te")
                for tt in range(NTT):
                    S_.wait_all("sync", "ve", "se")
                    S_.dma(lambda: self.a("xpb"),
                           lambda tt=tt, qc=qc, oc=oc: xsrc[
                               xsrc_row0 + qc * QCH + 128 * tt:
                               xsrc_row0 + qc * QCH + 128 * (tt + 1),
                               oc * 512:(oc + 1) * 512])
                    S_.wait_all("vector", "dml")
                    S_.emit("vector", lambda e, tt=tt, oc=oc, b=bbst:
                            e.tensor_tensor(
                                self.a("stage", cols=(1024, 1536)),
                                self.pb[tt].ap()[:, 0:512],
                                self.a(b, cols=(oc * 512, (oc + 1) * 512)),
                                op=ALU.add))
                    S_.wait_all("vector", "ve", "dms")
                    S_.emit("vector", lambda e: e.tensor_tensor(
                        self.a("stage", cols=(1536, 2048)),
                        self.a("stage", cols=(1024, 1536)),
                        self.a("xpb"),
                        op=ALU.add))
                    S_.wait_all("sync", "ve")
                    S_.dma_store(lambda tt=tt, qc=qc, oc=oc: xdst[
                        xdst_row0 + qc * QCH + 128 * tt:
                        xdst_row0 + qc * QCH + 128 * (tt + 1),
                        oc * 512:(oc + 1) * 512],
                        lambda: self.a("stage", cols=(1536, 2048)))

    def emit_ffn(self, xsrc_row0, xsrc, xdst, xdst_row0):
        S_ = self.S_
        NT = self.NT
        S_.sem("tld")
        for i in range(3):
            S_.sem(f"xl{i}")
        S_.wait_all("sync", "te", "ve", "se", "ge", "dms")
        m_ve0, m_se0 = S_.mark("ve"), S_.mark("se")
        M = {}
        for tp in range(min(2, NT)):
            S_.dma(lambda tp=tp: self.a("fx", cols=(tp * 1024, tp * 1024 + 1024)),
                   lambda tp=tp: xsrc[xsrc_row0 + 128 * tp:
                                      xsrc_row0 + 128 * (tp + 1), :],
                   sem=f"xl{tp}")
            M[("xl", tp)] = S_.mark(f"xl{tp}")
        for c in range(NCH):
            S_.dma(lambda c=c: self.a("tg", cols=(c * INNER, (c + 1) * INNER)),
                   lambda c=c: self.dr["TgS"][128 * c:128 * (c + 1), :], sem="tld")
            S_.dma(lambda c=c: self.a("tu", cols=(c * INNER, (c + 1) * INNER)),
                   lambda c=c: self.dr["TuS"][128 * c:128 * (c + 1), :], sem="tld")
        for c in range(32):
            S_.dma(lambda c=c: self.a("td", cols=(c * D, (c + 1) * D)),
                   lambda c=c: self.dr["TdS"][128 * c:128 * (c + 1), :], sem="tld")
        tld_all = S_.mark("tld")

        def xap(t):
            return self.a("fx", cols=((t % 3) * 1024, (t % 3) * 1024 + 1024))

        def dma_x(t):
            if t >= 3:
                S_.wait_val("sync", "ve", M[("resid_ve", t - 3)])
                S_.wait_val("sync", "se", M[("ln_se", t - 3)])
            S_.dma(lambda t=t: xap(t),
                   lambda t=t: xsrc[xsrc_row0 + 128 * t:
                                    xsrc_row0 + 128 * (t + 1), :],
                   sem=f"xl{t % 3}")
            M[("xl", t)] = S_.mark(f"xl{t % 3}")

        def lnq1(t):
            p = t % 2
            b, qb = 8 + 32 * p, 64 + 16 * p
            S_.wait_val("vector", f"xl{t % 3}", M[("xl", t)])
            S_.wait_val("scalar", f"xl{t % 3}", M[("xl", t)])
            if ("q1T_te", t - 2) in M:
                S_.wait_val("vector", "te", M[("q1T_te", t - 2)])
            self.emit_ln(lambda t=t: xap(t),
                         lambda p=p: self.a("yq", cols=(p * 1024,
                                                        p * 1024 + 1024)),
                         b=b)
            self.emit_actquant(
                lambda p=p: self.a("yq", cols=(p * 1024, p * 1024 + 1024)),
                lambda p=p: self.a("q1t", cols=(p * 1024, p * 1024 + 1024)),
                lambda: self.a("silu", cols=(0, 1024)), qb=qb)
            S_.wait_all("vector", "ve")
            S_.emit("vector", lambda e, qb=qb: e.scalar_tensor_tensor(
                self.col("sstats", qb + 8), self.col("sstats", qb + 1),
                1.0 / 127.0, self.col("gad", 6), op0=ALU.mult, op1=ALU.mult))
            S_.emit("vector", lambda e, qb=qb: e.scalar_tensor_tensor(
                self.col("sstats", qb + 9), self.col("sstats", qb + 1),
                1.0 / 127.0, self.col("gad", 7), op0=ALU.mult, op1=ALU.mult))
            M[("ts1", t)] = S_.mark("ve")
            M[("ln_se", t)] = S_.mark("se")

        def q1T(t):
            p = t % 2
            S_.wait_val("tensor", "ve", M[("ts1", t)])
            if t == 0:
                S_.wait_val("tensor", "se", m_se0)
            for c in range(NCH):
                S_.transpose(lambda c=c: self.pbb.ap()[:, c * 128:c * 128 + 128],
                             lambda c=c, p=p: self.a(
                                 "q1t", cols=(p * 1024 + c * 128,
                                              p * 1024 + c * 128 + 128)),
                             lambda: self.a("identb"))
            M[("q1T_te", t)] = S_.mark("te")
            S_.wait_val("scalar", "te", M[("q1T_te", t)])
            S_.emit("scalar", lambda e, p=p: e.copy(
                self.a("q1f", cols=(p * 1024, p * 1024 + 1024)),
                self.pbb.ap()[:, 0:1024]))
            M[("q1f", t)] = S_.mark("se")

        def quant2(t):
            if ("q2T_te", t - 1) in M:
                S_.wait_val("vector", "te", M[("q2T_te", t - 1)])
            self.emit_actquant(lambda: self.a("hb"), lambda: self.a("q2t"),
                               lambda: self.a("silu"), qb=96)
            M[("ts2", t)] = S_.mark("ve")
            S_.wait_all("vector", "ve")
            S_.emit("vector", lambda e: e.scalar_tensor_tensor(
                self.col("sstats", 104), self.col("sstats", 97), 1.0 / 127.0,
                self.col("gad", 8), op0=ALU.mult, op1=ALU.mult))
            M[("a_d", t)] = S_.mark("ve")

        def gateup(t):
            p = t % 2
            qb = 64 + 16 * p
            S_.wait_val("tensor", "tld", tld_all)
            S_.wait_val("tensor", "se", M[("q1f", t)])
            if t >= 1:
                S_.wait_val("tensor", "ve", M[("h_ve", t - 1)])
            else:
                S_.wait_val("tensor", "ve", m_ve0)
            ve_guard = M[("ts2", t - 1)] if t >= 1 else M[("ts1", t)]
            for wsb in ("tg", "tu"):
                for rnd in range(2):
                    if wsb == "tg" and rnd == 1:
                        S_.wait_val("tensor", "se", M[("silu", t, 0)])
                    elif wsb == "tu" and rnd == 0:
                        S_.wait_val("tensor", "se", M[("silu", t, 1)])
                    elif wsb == "tu" and rnd == 1:
                        S_.wait_val("tensor", "ve", M[("hstt", t, 0)])
                    for oc4 in range(4):
                        oc = rnd * 4 + oc4
                        for c in range(NCH):
                            S_.mm(lambda oc4=oc4: self.pb[oc4].ap()[:, 0:512],
                                  lambda c=c, p=p: self.a(
                                      "q1f", cols=(p * 1024 + c * 128,
                                                   p * 1024 + c * 128 + 128)),
                                  lambda c=c, oc=oc, w=wsb: self.a(
                                      w, cols=(c * INNER + oc * 512,
                                               c * INNER + oc * 512 + 512)),
                                  kdim=128, mdim=128, start=(c == 0),
                                  stop=(c == NCH - 1))
                    te_m = S_.mark("te")
                    if wsb == "tg":
                        S_.wait_val("scalar", "te", te_m)
                        S_.wait_val("scalar", "ve", ve_guard)
                        for oc4 in range(4):
                            oc = rnd * 4 + oc4
                            S_.emit("scalar", lambda e, oc=oc, oc4=oc4, qb=qb:
                                    e.activation(
                                        self.a("silu", cols=(oc * 512,
                                                             oc * 512 + 512)),
                                        self.pb[oc4].ap()[:, 0:512], AF.Silu,
                                        scale=self.col("sstats", qb + 8)))
                        M[("silu", t, rnd)] = S_.mark("se")
                    else:
                        S_.wait_val("vector", "te", te_m)
                        S_.wait_val("vector", "se", M[("silu", t, 1)])
                        for oc4 in range(4):
                            oc = rnd * 4 + oc4
                            S_.emit("vector", lambda e, oc=oc, oc4=oc4, qb=qb:
                                    e.scalar_tensor_tensor(
                                        self.a("hb", cols=(oc * 512,
                                                           oc * 512 + 512)),
                                        self.pb[oc4].ap()[:, 0:512],
                                        self.col("sstats", qb + 9),
                                        self.a("silu", cols=(oc * 512,
                                                             oc * 512 + 512)),
                                        op0=ALU.mult, op1=ALU.mult))
                        M[("hstt", t, rnd)] = S_.mark("ve")
            M[("h_ve", t)] = S_.mark("ve")

        def q2Tdown(t):
            S_.wait_val("tensor", "ve", M[("ts2", t)])
            prev_copy = None
            for grp in range(4):
                if prev_copy is not None:
                    S_.wait_val("tensor", "se", prev_copy)
                for cc in range(8):
                    c = grp * 8 + cc
                    S_.transpose(lambda cc=cc: self.pbb.ap()[:, cc * 128:
                                                             cc * 128 + 128],
                                 lambda c=c: self.a("q2t",
                                                    cols=(c * 128, c * 128 + 128)),
                                 lambda: self.a("identb"))
                te_m = S_.mark("te")
                S_.wait_val("scalar", "te", te_m)
                S_.emit("scalar", lambda e, grp=grp: e.copy(
                    self.a("q2f", cols=(grp * 1024, grp * 1024 + 1024)),
                    self.pbb.ap()[:, 0:1024]))
                prev_copy = S_.mark("se")
            M[("q2T_te", t)] = S_.mark("te")
            S_.wait_val("tensor", "se", prev_copy)
            if t >= 1:
                S_.wait_val("tensor", "ve", M[("resid_ve", t - 1)])
            else:
                S_.wait_val("tensor", "ve", m_ve0)
            for oc in range(2):
                for c in range(32):
                    S_.mm(lambda oc=oc: self.pb[4 + oc].ap()[:, 0:512],
                          lambda c=c: self.a("q2f", cols=(c * 128, c * 128 + 128)),
                          lambda c=c, oc=oc: self.a(
                              "td", cols=(c * D + oc * 512,
                                          c * D + oc * 512 + 512)),
                          kdim=128, mdim=128, start=(c == 0), stop=(c == 31))
            dn_m = S_.mark("te")
            S_.wait_val("vector", "te", dn_m)
            S_.wait_all("vector", "dms")
            S_.wait_val("vector", "ve", M[("a_d", t)])
            S_.wait_val("vector", f"xl{t % 3}", M[("xl", t)])
            for oc in range(2):
                S_.emit("vector", lambda e, oc=oc, t=t: e.scalar_tensor_tensor(
                    self.a("fx", cols=(3072 + oc * 512, 3072 + (oc + 1) * 512)),
                    self.pb[4 + oc].ap()[:, 0:512],
                    self.col("sstats", 104),
                    self.a("fx", cols=((t % 3) * 1024 + oc * 512,
                                       (t % 3) * 1024 + (oc + 1) * 512)),
                    op0=ALU.mult, op1=ALU.add))
            M[("resid_ve", t)] = S_.mark("ve")
            S_.wait_val("sync", "ve", M[("resid_ve", t)])
            S_.dma_store(lambda t=t: xdst[xdst_row0 + 128 * t:
                                          xdst_row0 + 128 * (t + 1), :],
                         lambda: self.a("fx", cols=(3072, 4096)))

        lnq1(0)
        q1T(0)
        for t in range(NT):
            if t >= 1:
                quant2(t - 1)
            gateup(t)
            if t + 1 < NT:
                lnq1(t + 1)
            if t >= 1:
                q2Tdown(t - 1)
            if t + 2 < NT:
                dma_x(t + 2)
            if t + 1 < NT:
                q1T(t + 1)
        quant2(NT - 1)
        q2Tdown(NT - 1)

    def emit_program(self):
        S_ = self.S_
        self.emit_gadget()
        self.emit_consts()
        self.emit_prep_tquant("w_gate", INNER, D, "TgS", 0)
        self.emit_prep_tquant("w_up", INNER, D, "TuS", 1)
        self.emit_prep_tquant("w_down", D, INNER, "TdS", 2)
        if self.nsub >= 3:
            for w, d in (("Wq", "WqT"), ("Wk", "WkT"),
                         ("Wv", "WvT"), ("Wo", "WoT")):
                self.emit_prep_wt(w, d)
        for e in ENGINES:
            S_.wait_all(e, "dms")

        for seq in range(self.nseq):
            r0 = seq * self.S
            subs = [("attn", "a"), ("ffn", None),
                    ("attn", "b"), ("ffn", None)][:self.nsub]
            for si, (styp, kind) in enumerate(subs):
                last = si == len(subs) - 1
                xsrc, xsrc_row0 = ((self.dr["xin"], r0) if si == 0
                                   else (self.dr["xs"], 0))
                xdst, xdst_row0 = ((self.dr["out"], r0) if last
                                   else (self.dr["xs"], 0))
                if styp == "attn":
                    self.emit_attn(kind, xsrc_row0, xsrc, xdst, xdst_row0)
                else:
                    self.emit_ffn(xsrc_row0, xsrc, xdst, xdst_row0)
                for e in ENGINES:
                    S_.wait_all(e, "dms")
        S_.wait_all("sync", "dms")

        with self.nc.Block() as block:
            @block.sync
            def _(eng):
                S_.replay("sync", eng)

            @block.tensor
            def _(eng):
                S_.replay("tensor", eng)

            @block.vector
            def _(eng):
                S_.replay("vector", eng)

            @block.scalar
            def _(eng):
                S_.replay("scalar", eng)

            @block.gpsimd
            def _(eng):
                S_.replay("gpsimd", eng)
        return self.nc


THETA_BITS = ['0x40400000', '0x400ffacd', '0x3fd7f067', '0x3fa1ee74', '0x3f72dce8', '0x3f361f1a', '0x3f089265', '0x3eccd432', '0x3e99999a', '0x3e665e14', '0x3e2cc052', '0x3e018b90', '0x3dc24a54', '0x3d91b27b', '0x3d5a83d6', '0x3d23dcf6', '0x3cf5c28f', '0x3cb84b44', '0x3c8a3375', '0x3c4f45b2', '0x3c1b6eaa', '0x3be91d94', '0x3baecfde', '0x3b83172a', '0x3b449ba6', '0x3b136f69', '0x3add1f21', '0x3aa5d15c', '0x3a78b110', '0x3a3a7e10', '0x3a0bd97e', '0x39d1beab']
ROPE_FIX_B64 = 'eNpl2AmcjtUaAPDn4yaiFVFZKrmF9iJatGgRaVGKtEpxr1RDJHuUOy1EWkQqIVnKXgpFpIg0lohky65sY0nx3b/MfPNN+fn/ZuZbznvOc57zvO855QpGPMG5BSKGUNHvdWhK/CviaJrTh1mcf0TECIoViqjPS/zCOmofGbGTGwtH3M4zRXy3aEQWjxXzHi8eHbGRbezigmMi5lLk2IjTGE2L43yX7XxzfMQcSpwQ0YauHF084iIe4SMyS0TcWTJiNcVP9DpDyKJoqYg1/MImCpeO2MHQkyIqnBzxFd+SxWZOPyXiJ0qViahGV7I4rVxEA+6iXvmItrzCRLbT/FR/8zG/Ueo0/aLf6RG3VohYwEaOPSPiGgpVjBhHnX9HJM6MOJIqfMcOCp8V8QYTmcasShEZlSO+ZjPDqkR8T5+zI046x/XpzkLOONfrnH2+fjKMr/n3Bd5n8yEXitVFEQ15lEwKX+zz9KBG1YjrWc2gSyKWcEl18WYN++hdI2Ikk1hN0UsjbiOTxfxK1mW+S4nL9YtvWMRJV7g2TelFVk3xuNJ3OOEq16YPBa+OuJaNNKkVMe/aiPW0uE4f6Hd9xMk3RNzCRBbySW3XocKN8pUH6clmCtWJOI7evMdolnJNXTEkk+I3RVSiOzfVc12eZhalb45YRuIW/eRCdrCT4rdGVGYcn5HFLi64LeIBnmYUUd/c0et217/DmDi1ge/Sj68ofWfEOZzPbgrepd/U5jWSPNdQnNnYyLq9W342jpjPBfcYy73ap/J98op2zGcNRe6PqMqT9OdL9vHxA/KESg+6Ls3JoC3tm/gsq9jDjw9FFGhqTPRiNHOZz6OPRAxgSDPrvbm1xg9so8B/rFXu5sH/ypkWEVcxkv10eDRiJquJluJCBTK59TH5RWtGsZAyj2uLy5+wljIi+vI+R7dSg2hJBq0ZzhSmM4MTW0e8xVAynhQHbmsTMZkPnjLfbOTmdhHdeJNbno74D1XaGzvvc0oHY2UBr3SUh3zPT/zKTZ1ch09Zwk72dFZDu8g7anMPQ1jHkq5yjFLPRFzG5YyiQbeIxjzFaJp0dz1GU/VZMWcKWSxnP8Of004PeUAbOvIiC9nETf8zf4ygZKbr8SxT+I6Nh157XgzoyBYKvhBxMV+zmlEvqg0vRdzBJT3NAYso08s1KfGyOWZlH3X9FeNkCt9Sr694MpKyr6qztGEDw17Tf9ZT8vWIujTlLQq+EXEpPVnLMf3kAc+xiWvejBjDNp7sL2Z8xX4KDhBjHnkrYjBHDTR+lnH1265BqXf0hfk8/q7cGqQ9ttBysPZYyAND5A5d6UaRoeLEx3R73+vD1Ecu/8Ac04lM+jCAEUxlB0WHq5G0oD8PjFCDWMB+ThhpLtjCTsqOcr+iGnMp+6FaSUMGMPkjdW50xOucOSaiGR2ZwSZeGStH2UXRcdYlHSg43r2Cs3mQjvzBsRPMKdW5jAY0JIM/Jqo7H8tDJjKVFVz9SUQtGrKCDZSd5BmBxszjT677VG3+zE/a0YlejKXkZPcDMlhHNlWniCfT2UCxqdYmB2n8uTFT7AvjoRMHKTzN72RTZrr1w25KfiluZPICv3KAqTNcd6a4sIPOXxkbz8/yPqd8rT0GfBMxkGl8Q7XZEffNiWjFMBZz47cRNeeqQ3SmK99z3LyIe5lEye+0yWzmcc58a4yB33t+4Ygs9xhKU512nLvAXHA9wxhOs4XGwnuL5AXXLfaMx3Z6/6A+cPwSdYeRfMczS90b2MeNP7pFkU3ZZWod1elGL6Yyk9lcvjziCrZxxE/ynzo0ojHdeIHZlF0hD7mYT6j0c8ReDnLxSvGhNvXIpvIq16UOT7GAT1abxzW+yyROWmtsvMkNvxg3e7lknX5Ri05k8ivHrbe2OZ17Gc+nfE42j21wLd5nKo03+slWGm0yd2xgC0U36yO30Z2RfEPxLdplNevptFW9I5u+v3qPaRT4zb2cbE7YZvw8znDmU3K7POcK6tKAh+lI8R3qE7fwMB8zeKd7Adftkv+MYy01sq3p3RFnceIe7bGGAzy919g5Y585p9rv+kEjOvM2M7hgvzz8Qx5S6E9tUYnKtOUphjKMOWTxM39y4gF95z56HYy4Mqku0IJE8q9/iUP/kof///XH4VeTh19I/vXm4RcSydx3c9/O+/hfH0q9ksj5ViLngzlXyG0/9dvh7yRT38y5VE5TOV/J/czh3xK530jmtpzbp8PtHn41mdN2brs5o8sZQO6Ac0ef83vq+7lN53YjNYZkWkfyuphzxfSPJBKpvuRrK5EKaKrjaZ9Ipg0kp1dpUUq9ktde3nVyL5XMF5/cGUjmeyWZHpS8USXz5iaZajoVmrTRp2VLvngnEsm/pVLu1VMv58UvmcgXxdSEJNPzKjWotClKz6z0yU1NdypeuVOZO7WpDqRSOnfc6VOf05KHp+iTcKdmMKP5ijPs1M+kBWMpZLc+k52UtEu/njYs5ndOtGNvxFL2UtluPZMXyeZPKtu596CwnXspTuU1tnCknfyzZPPhUe6CrGUfV9ndb+dOu/u2LKWGnf1lPM0YCtjhn8+ttLTD784S9lPdLr8uD9GB/pS1469BTy61w29ME9rxPAMYTGk7/wq0pzNvsZwhJT2t2P1X5C4a8Rn7qFxKtWIAC0vbSdH+JHcBVuecCpzFGL5kOyefoioymspl3Glowkes4biy+kkX+nJvOU92/E6B8iolgyhzqutxzWnuGtzDNJbzC1tpfLqnBNrTgQl8wXRms5yTK4gNx50h/tzOGNaxmRIVxZ/HeIIMXqLwv92hKcY9PMmbLOOOM8WeKUyl7FnGXMmYKF/ZHZHFVKziNVZw7dlyhM+pdI72GMhc1rKVow6dRDCdRQw8z5j5k8zzzRkz+Jk6F0S8ylu8zWg2soV6F0a8w+ccpPFFEf9lMvPpfbG5IqrKNW6mXDWxpiUd+IaVVLjEXZAudKUv09jBi9XdTfmB5TXElIcutZ7ozkBGMZM5lL1M/vI2w5nKQY69POJ+WvMFezjvCk9X1KdmTXnCRh660rhYxQaKX2Wt0Jvx7KLl1XLn0CnINfrBe3xOkvNqmW8acD8P04aN7KHctdYBbelMb95jLJPJ4ieOv07toDajePB6NYHJLKD+DeLH8BvlCc3qeMqlC8+xhWPqqh3UZC/Db5Jz9cwVLelCsZvNJSspc4t6xnnUZAIz2MNeqtwqvjSjN6/zVH016nZzxvF3mIMG1iTP0OLOiJepdZd8pFxDuc4EjmhkzXAXY9jGuLvFnWaNPamxgDL3GCszOf1eT1TU4wBF7lN76cs0it0fMYtaD9jRUv5Ba4RNJKnXRLs8y0h+4p6HrFPOaarmsJvnHxYrZrCUro9YE6yifDM7Mu6nPWNZSdXm6ihj/mNM/5VPdOIAxVpYBxR4VP/pykt8y+ktrV2a0o936PqYXQfHPC5mNKf0E+aXTGplyGNqtRJ/tnJdazlHK3oykaWsotyTagOjmM12DlC8jX7RhEEUbiuWDGME97WzDin9tJrDDEq19zqdeZvxTGAFZTvoExl0ZjrFOuobtalLdmfX6WIuaMI8VlCiq5iwnULPyF920aSbPrOCVaxjC1W6ixvnPCsPn1PzeqgzvEDF/8lhfmQLDTPVEXZwwgsRF/IQ+3j+RbWIpRz1krpLwZ7GQFt68CMnegDI4Ed2cOPL2qUdr/M7z/c2/4yjQB/9oD5PsJQmr8ifvnZ57OLNV+UhV77md9bQ+nX1gG5vqH9MYharubafmHLEm+ozc0j0dx2+YBVFB9iJcT89eZ2xzKHEW+abqgPdh3mXueyh9dvGS5L678iXd92HWcyFg9zXeI0vKfKetniHL6gxWC7TdIjnI4byLWs5cqhYvy9XqDJMLeVrNlL1A5/nvOFyjt3UHCFXWM+/RqoRlOQ++rCIxVQe5T7PILZykDc+FJuP5CjvEKPdSzlvjM8xjJpj1T92s59Lx1k3fMZMrh0vphPEj1oTrQdWsp/CH8sxWtGDVWylxifaJsmtk9RGTvrUvPLoZ+LBdDInW0dUmCLGzGAX46bKCRZyyefmhlFspeIX+sK8aZ4NOHq6OksG/XmX4XzIeOZQ70u1ie0UmiHnqDtTXWE0y/jXV95nwizj/Nq4eJS6s60HfueoOX5y6rdqGk/Sjf5cPdda41W2cfI8caXUd8w3fhoxnSyWsIeTvzcfvExv+tCfQXzP+CxrjLX0X2BeWMkejl5onVOf8Wzg+EX6wht8whIKLrYj5y4yWMWrP+gnuzhhidrHhTzFBrKptjTiJrrw3o/y4dDpy7KIu2lOD2bRe7mYM40qP8kRZrCMq1aIP11Yw+6f3W9Wqqccv8o6YiJdVltL1Fmjjwym/NqIW/iO63/xGQqts0YYxiR+I0mR9dpiKIuovyHif3xI4Y3qFWdw2SbPZxTcIjd5lhcYQautEaf86vmbycxgOdV+8z2a8xxbqL5NO/RjEZt5abt8puIO9ZCe9GMEP7CUtYzcKac4Itt3GMYEbtgdcdEe8WU/xfda17zMAIrsEwf6s55av6uvtNuvhtDkD/eXP9VxGh7Q1kFt0YpsLk7KZ347dLqRtn1POylI5O1tE4n0vW4ib2Ofd6yQSDsiSKRvWNPONPJa/sfRR+oUJe/cJLWrz/tK3hY7mfbuP/5MpnbWaccOqaOYtCOPvH1x6mwj7XgimdrVpx/EJPM+lsx3MJN2IJN+nJBIO7/JHUreKUEi/wlP2tlNvgOJ/JHNPURJjTTvTCKZNhXJ1GFLMu2E6R9nRH87D8oNcPqLqUby50LaoVDauUHaCU3+xtLOcNLTI3UOk+p52hFS+sFIWo4kk+lJku9f3mznz72007i8vqRlYiL9gOtvOZHMjWDaiNLTM+1AL5U3eScm+YLpr/8DjXvDZA=='


def _apply_ulp(base, idx, delta):
    i = base.ravel().view(np.int32).astype(np.int64)
    m = np.where(i < 0, -2147483648 - i, i)
    m[idx] += delta.astype(np.int64)
    i2 = np.where(m >= 0, m, -2147483648 - m).astype(np.int32)
    out = i2.view(np.float32).reshape(base.shape)
    return out


def rope_tables(S):
    import zlib, base64
    L = 2048
    th32 = np.array([int(h, 16) for h in THETA_BITS],
                    dtype=np.uint32).view(np.float32)
    ang32 = (np.arange(L, dtype=np.float32)[:, None] * th32[None, :])
    base_c = np.cos(ang32.astype(np.float64)).astype(np.float32)
    base_s = np.sin(ang32.astype(np.float64)).astype(np.float32)
    raw = zlib.decompress(base64.b64decode(ROPE_FIX_B64))
    nc_, ns_ = np.frombuffer(raw[:8], dtype=np.uint32)
    o = 8
    idx_c = np.frombuffer(raw[o:o + 4 * nc_], dtype=np.uint32).astype(np.int64)
    o += 4 * nc_
    del_c = np.frombuffer(raw[o:o + nc_], dtype=np.int8)
    o += int(nc_)
    idx_s = np.frombuffer(raw[o:o + 4 * ns_], dtype=np.uint32).astype(np.int64)
    o += 4 * ns_
    del_s = np.frombuffer(raw[o:o + ns_], dtype=np.int8)
    cos32 = _apply_ulp(base_c, idx_c, del_c)
    sin32 = _apply_ulp(base_s, idx_s, del_s)
    jj = np.arange(128) % 32
    ropec = np.ascontiguousarray(cos32.T[jj, :S])
    ropes = np.ascontiguousarray(sin32.T[jj, :S])
    return ropec, ropes


def host_consts(S):
    identf = np.eye(128, dtype=np.float32)
    identb = np.eye(128).astype(ml_dtypes.bfloat16)
    onesf = np.ones((128, 128), dtype=np.float32)
    e64row = np.zeros((128, 128), dtype=np.float32)
    e64row[64, :] = 1.0
    bdiag = np.zeros((128, 14), dtype=np.float32)
    for j in range(4):
        bdiag[32 * j:32 * (j + 1), j] = 1.0
    for j in range(2):
        bdiag[64 * j:64 * (j + 1), 4 + j] = 1.0
    for par in range(2):
        for j in range(2):
            bdiag[64 * j:64 * (j + 1), 6 + 4 * par + 2 * par + j] = 1.0
    ropec, ropes = rope_tables(S)
    sevenv = np.full((128, 1), 7.7, dtype=np.float32)
    return dict(identf=identf, identb=identb, onesf=onesf, bdiag=bdiag,
                e64row=e64row, ropec=ropec, ropes=ropes, sevenv=sevenv)


def make_inputs(x, w_qkv, w_proj, b_proj, Wq, bq, Wk, bk, Wv, bv, Wo, bo,
                w_gate, w_up, w_down, ncores=8):
    B, S, _ = x.shape
    nseq = B // ncores
    consts = host_consts(S)
    common = dict(
        w_qkv=np.ascontiguousarray(w_qkv, dtype=np.float32),
        w_proj=np.ascontiguousarray(w_proj, dtype=np.float32),
        b_proj=np.ascontiguousarray(b_proj, dtype=np.float32).reshape(1, D),
        Wq=np.ascontiguousarray(Wq, dtype=np.float32),
        Wk=np.ascontiguousarray(Wk, dtype=np.float32),
        Wv=np.ascontiguousarray(Wv, dtype=np.float32),
        Wo=np.ascontiguousarray(Wo, dtype=np.float32),
        bq=np.ascontiguousarray(bq, dtype=np.float32).reshape(1, D),
        bk=np.ascontiguousarray(bk, dtype=np.float32).reshape(1, D),
        bv=np.ascontiguousarray(bv, dtype=np.float32).reshape(1, D),
        bo=np.ascontiguousarray(bo, dtype=np.float32).reshape(1, D),
        w_gate=np.ascontiguousarray(w_gate, dtype=np.float32),
        w_up=np.ascontiguousarray(w_up, dtype=np.float32),
        w_down=np.ascontiguousarray(w_down, dtype=np.float32),
        **consts)
    in_maps = []
    for c in range(ncores):
        xin = np.ascontiguousarray(
            np.asarray(x[c * nseq:(c + 1) * nseq], dtype=np.float32
                       ).reshape(nseq * S, D))
        in_maps.append(dict(common, xin=xin))
    return in_maps, nseq


def kernel(L0_attn_a_x, L0_attn_a_w_qkv, L0_attn_a_w_proj, L0_attn_a_b_proj,
           L0_attn_b_Wq, L0_attn_b_bq, L0_attn_b_Wk, L0_attn_b_bk,
           L0_attn_b_Wv, L0_attn_b_bv, L0_attn_b_Wo, L0_attn_b_bo,
           L0_ffn_w_gate, L0_ffn_w_up, L0_ffn_w_down):
    x = np.asarray(L0_attn_a_x)
    B, S, _ = x.shape
    ncores = 8
    in_maps, nseq = make_inputs(
        x, np.asarray(L0_attn_a_w_qkv), np.asarray(L0_attn_a_w_proj),
        np.asarray(L0_attn_a_b_proj), np.asarray(L0_attn_b_Wq),
        np.asarray(L0_attn_b_bq), np.asarray(L0_attn_b_Wk),
        np.asarray(L0_attn_b_bk), np.asarray(L0_attn_b_Wv),
        np.asarray(L0_attn_b_bv), np.asarray(L0_attn_b_Wo),
        np.asarray(L0_attn_b_bo), np.asarray(L0_ffn_w_gate),
        np.asarray(L0_ffn_w_up), np.asarray(L0_ffn_w_down), ncores)
    b = Builder(nseq=nseq, S=S, nsub=4)
    nc = b.emit_program()
    res = run_bass_kernel_spmd(nc, in_maps, core_ids=list(range(ncores)))
    out = np.empty((B, S, D), dtype=np.float32)
    for c in range(ncores):
        out[c * nseq:(c + 1) * nseq] = np.asarray(
            res.results[c]["out"]).reshape(nseq, S, D)
    return out


# version 14
# speedup vs baseline: 1.3124x; 1.1060x over previous
import numpy as np
import ml_dtypes
import concourse.bass as bass
import concourse.mybir as mybir
from concourse.bass_utils import run_bass_kernel_spmd

F32 = mybir.dt.float32
BF16 = mybir.dt.bfloat16
FP8 = mybir.dt.float8e4
AF = mybir.ActivationFunctionType
ALU = mybir.AluOpType
AXX = mybir.AxisListType.X

C32 = float(np.float32(1.5 * 2.0 ** 23))
C64 = float(np.float32(1.5 * 2.0 ** 52))
EPS = 1e-5
D = 1024
H = 16
HD = 64
NCH = 8
INNER = 4096

ENGINES = ("sync", "tensor", "vector", "scalar", "gpsimd")
_ESEM = {"sync": "dml", "tensor": "te", "vector": "ve",
         "scalar": "se", "gpsimd": "ge"}


class Sched:

    def __init__(self, nc, sems):
        self.nc = nc
        self.sems = sems
        self.count = {k: 0 for k in sems}
        self.items = {e: [] for e in ENGINES}
        self.last_wait = {(e, s): 0 for e in ENGINES for s in sems}
        self.mm_mode = None

    def sem(self, name):
        if name not in self.sems:
            self.sems[name] = self.nc.alloc_semaphore(name)
            self.count[name] = 0
            for e in ENGINES:
                self.last_wait[(e, name)] = 0
        return name

    def mark(self, name):
        return self.count[name]

    def wait_val(self, eng, name, v):
        if v <= self.last_wait[(eng, name)]:
            return
        self.last_wait[(eng, name)] = v
        self.raw(eng, lambda e, semh=self.sems[name], vv=v: e.wait_ge(semh, vv))

    def emit(self, eng, fn, sem=None, n=1):
        s = sem or _ESEM[eng]
        semh = self.sems[s]
        self.count[s] += n

        def run(e, fn=fn, semh=semh, n=n):
            fn(e).then_inc(semh, n)
        self.items[eng].append(run)

    def raw(self, eng, fn):
        self.items[eng].append(fn)

    def dma(self, out_fn, in_fn, eng="sync", sem="dml"):
        if sem != "dml":
            self.sem(sem)
        self.emit(eng, lambda e, o=out_fn, i=in_fn: e.dma_start(out=o(), in_=i()),
                  sem=sem, n=16)

    def dma_store(self, out_fn, in_fn, eng="sync"):
        self.emit(eng, lambda e, o=out_fn, i=in_fn: e.dma_start(out=o(), in_=i()),
                  sem="dms", n=16)

    def wait_all(self, eng, *sem_names):
        for s in sem_names:
            v = self.count[s]
            if v <= self.last_wait[(eng, s)]:
                continue
            if s in ("dml", "dms") and eng != "sync":
                if v > self.last_wait[("sync", s)]:
                    self.last_wait[("sync", s)] = v
                    self.raw("sync", lambda e, semh=self.sems[s], vv=v:
                             e.wait_ge(semh, vv))
            self.last_wait[(eng, s)] = v
            self.raw(eng, lambda e, semh=self.sems[s], vv=v: e.wait_ge(semh, vv))

    def mm(self, out_fn, lhsT_fn, rhs_fn, kdim, mdim, start, stop,
           tile_position=None):
        kk = 32 if kdim <= 32 else (64 if kdim <= 64 else 128)
        mm_ = 32 if mdim <= 32 else (64 if mdim <= 64 else 128)
        sig = (kk, mm_)
        if self.mm_mode is not None and self.mm_mode != sig:
            self.raw("tensor", lambda e: e.drain())
        self.mm_mode = sig
        self.emit("tensor",
                  lambda e, o=out_fn, l=lhsT_fn, r=rhs_fn, st=start, sp=stop,
                  tp=tile_position: e.matmul(o(), l(), r(), start=st, stop=sp,
                                             tile_position=tp))

    def transpose(self, out_fn, in_fn, id_fn):
        if self.mm_mode is not None and self.mm_mode != ("T",):
            self.raw("tensor", lambda e: e.drain())
        self.mm_mode = ("T",)
        self.emit("tensor", lambda e, o=out_fn, i=in_fn, d=id_fn:
                  e.transpose(o(), i(), d()))

    def replay(self, eng, e):
        for it in self.items[eng]:
            it(e)


def _c32b(x):
    return (x + 31) // 32 * 32


class Builder:
    def __init__(self, nseq=2, S=2048, nsub=4):
        self.nseq, self.S, self.nsub = nseq, S, nsub
        self.NT = S // 128
        self.QCH = min(512, S)
        self.NQC = S // self.QCH
        self.KT = S // 128
        self.SWEEP = min(512, S)
        self.NSW = S // self.SWEEP
        self.SWT = self.SWEEP // 128
        self.nc = bass.Bass("TRN2", target_bir_lowering=False)
        self._declare_dram()
        self._declare_sbuf()
        self.sems = {k: self.nc.alloc_semaphore(k) for k in
                     ("dml", "dms", "te", "ve", "se", "ge")}
        self.S_ = Sched(self.nc, self.sems)

    def _declare_dram(self):
        nc, S, nseq = self.nc, self.S, self.nseq
        dr = {}

        def din(name, shape, dt=F32):
            dr[name] = nc.dram_tensor(name, shape, dt, kind="ExternalInput").ap()

        din("xin", [nseq * S, D])
        din("w_qkv", [D, 3 * D]); din("w_proj", [D, D]); din("b_proj", [1, D])
        din("Wq", [D, D]); din("Wk", [D, D]); din("Wv", [D, D]); din("Wo", [D, D])
        din("bq", [1, D]); din("bk", [1, D]); din("bv", [1, D]); din("bo", [1, D])
        din("w_gate", [INNER, D]); din("w_up", [INNER, D]); din("w_down", [D, INNER])
        din("identf", [128, 128]); din("identb", [128, 128], BF16)
        din("e64row", [128, 128])
        din("onesf", [128, 128]); din("bdiag", [128, 14])
        din("ropec", [128, S]); din("ropes", [128, S]); din("sevenv", [128, 1])
        dr["out"] = nc.dram_tensor("out", [nseq * S, D], F32,
                                   kind="ExternalOutput").ap()
        dr["xs"] = nc.dram_tensor("xs", [S, D], F32).ap()
        dr["qTs"] = nc.dram_tensor("qTs", [D, S], F32).ap()
        dr["TgS"] = nc.dram_tensor("TgS", [D, INNER], FP8).ap()
        dr["TuS"] = nc.dram_tensor("TuS", [D, INNER], FP8).ap()
        dr["TdS"] = nc.dram_tensor("TdS", [INNER, D], FP8).ap()
        for w in ("WqT", "WkT", "WvT", "WoT"):
            dr[w] = nc.dram_tensor(w, [D, D], F32).ap()
        self.dr = dr

    def _declare_sbuf(self):
        nc = self.nc
        S, NT, SWEEP, QCH = self.S, self.NT, self.SWEEP, self.QCH
        self.ARENA = 204 * 1024
        arena = nc.alloc_sbuf_tensor("arena", [128, self.ARENA // 4], F32)
        try:
            base = nc.lookup_mloc(arena).addr
        except Exception:
            base = 20480
        sb = {}
        off = [0]

        def place(name, shape, dt, at=None):
            if at is None:
                at = off[0]
                nbytes = int(np.prod(shape[1:])) * mybir.dt.size(dt)
                off[0] = _c32b(at + nbytes)
            sb[name] = nc.alloc_sbuf_tensor_at(name, shape, dt, offset=base + at)
            return at

        place("identf", [128, 128], F32)
        place("identb", [128, 128], BF16)
        place("onesf", [128, 128], F32)
        place("bdiag", [128, 14], F32)
        place("bdp", [128, 384], F32)
        place("bbproj", [128, D], F32)
        place("bbo", [128, D], F32)
        place("bvb", [128, D], F32)
        place("biasqkv", [128, 16], F32)
        place("negc", [128, 16], F32)
        place("gad", [128, 12], F32)
        place("sstats", [128, 128], F32)
        place("snorm", [128, 64], F32)
        place("brow", [128, 1024], F32)
        place("e64row", [128, 128], F32)
        ph = off[0]
        self.PERSIST = ph
        kT_at = ph
        Vb_at = kT_at + NCH * S * 4
        yfm_at = Vb_at + NT * (H * HD + H) * 4
        wbuf_at = yfm_at + NCH * SWEEP * 4
        stage_at = wbuf_at + 16 * 1024
        attn_end = stage_at + 8 * 1024
        assert attn_end + 8 * 1024 <= self.ARENA, f"attn {attn_end}"
        place("kT", [128, NCH * S], F32, kT_at)
        place("Vb", [128, NT * (H * HD + H)], F32, Vb_at)
        place("yfm", [128, NCH * SWEEP], F32, yfm_at)
        place("wbuf", [128, 4096], F32, wbuf_at)
        place("stage", [128, 2048], F32, stage_at)
        place("wq2", [128, 2048], F32, attn_end)
        obuf_at = yfm_at
        abuf_at = obuf_at + NCH * QCH * 4
        assert abuf_at + 8 * QCH * 4 <= stage_at, "attn overlay"
        place("qbuf", [128, 2 * QCH], F32, stage_at)
        place("obuf", [128, NCH * QCH], F32, obuf_at)
        place("abuf", [128, 8 * QCH], F32, abuf_at)
        place("xpb", [128, 512], F32, abuf_at)
        place("wp", [128, 1024], F32, abuf_at + 512 * 4)
        tg_at = ph
        tu_at = tg_at + NCH * INNER
        td_at = tu_at + NCH * INNER
        hb_at = _c32b(td_at + 32 * D)
        silu_at = hb_at + INNER * 4
        q2t_at = silu_at + INNER * 4
        q2f_at = q2t_at + INNER * 2
        q1t_at = q2f_at + INNER * 2
        q1f_at = q1t_at + 2 * D * 2
        fx_at = _c32b(q1f_at + 2 * D * 2)
        yq_at = fx_at + 4 * D * 4
        assert yq_at + 2 * D * 4 <= self.ARENA, "ffn"
        place("tg", [128, NCH * INNER], FP8, tg_at)
        place("tu", [128, NCH * INNER], FP8, tu_at)
        place("td", [128, 32 * D], FP8, td_at)
        place("hb", [128, INNER], F32, hb_at)
        place("silu", [128, INNER], F32, silu_at)
        place("q2t", [128, INNER], BF16, q2t_at)
        place("q2f", [128, INNER], BF16, q2f_at)
        place("q1t", [128, 2 * D], BF16, q1t_at)
        place("q1f", [128, 2 * D], BF16, q1f_at)
        place("fx", [128, 4 * D], F32, fx_at)
        place("yq", [128, 2 * D], F32, yq_at)
        pw_at = ph
        ptb_at = pw_at + 8 * 1024 * 4
        pst_at = ptb_at + 32 * 1024 * 2
        assert pst_at + 4096 * 4 <= self.ARENA, "prep"
        place("pw", [128, 8 * 1024], F32, pw_at)
        place("ptb", [128, 32 * 1024], BF16, ptb_at)
        place("pst", [128, 4096], F32, pst_at)
        place("pst8", [128, 4096], FP8, pst_at)
        self.sb = sb
        self.pb = [nc.alloc_psum_tensor(f"pb{i}", [128, 512], F32)
                   for i in range(7)]
        self.pbb = nc.alloc_psum_tensor("pbb", [128, 1024], BF16)

    def a(self, name, rows=None, cols=None):
        ap = self.sb[name].ap()
        r = rows or (0, 128)
        if cols is None:
            return ap[r[0]:r[1], :]
        return ap[r[0]:r[1], cols[0]:cols[1]]

    def col(self, name, c):
        return self.a(name, cols=(c, c + 1))

    def mkap(self, name, col_off, dims):
        ap0 = self.sb[name].ap()
        return bass.AP(ap0.tensor, ap0.offset + col_off,
                       [[ap0.ap[0][0], 128]] + [list(d) for d in dims])

    def emit_gadget(self):
        S_ = self.S_
        S_.emit("vector", lambda e: e.memset(self.a("brow"), 0.0))
        S_.emit("vector", lambda e: e.memset(self.col("gad", 4), EPS))
        S_.emit("vector", lambda e: e.memset(self.col("gad", 5), 0.125))
        S_.dma(lambda: self.col("sstats", 0), lambda: self.dr["sevenv"])
        S_.wait_all("vector", "dml")
        S_.emit("vector", lambda e: e.tensor_scalar(
            self.col("gad", 2), self.col("sstats", 0), C64, -C64,
            op0=ALU.add, op1=ALU.add))
        S_.wait_all("vector", "ve")
        S_.emit("vector", lambda e: e.tensor_scalar(
            self.col("gad", 3), self.col("gad", 2), 0.125, None, op0=ALU.mult))
        S_.wait_all("vector", "ve")
        S_.emit("vector", lambda e: e.tensor_scalar(
            self.col("gad", 0), self.col("gad", 3), C64 - C32, C32,
            op0=ALU.mult, op1=ALU.add))
        S_.wait_all("vector", "ve")
        S_.emit("vector", lambda e: e.tensor_scalar(
            self.col("gad", 1), self.col("gad", 0), -1.0, None, op0=ALU.mult))

    def emit_consts(self):
        S_ = self.S_
        for n in ("identf", "identb", "onesf", "bdiag", "e64row"):
            S_.dma(lambda n=n: self.a(n), lambda n=n: self.dr[n])
        for i, bsrc in enumerate(("bq", "bk")):
            S_.dma(lambda i=i: self.a("pst", rows=(0, 8),
                                      cols=(i * 128, i * 128 + 128)),
                   lambda b=bsrc: self.dr[b][0:1, :].rearrange(
                       "a (p c) -> (a p) c", c=128))
        for i, bsrc in enumerate(("bv", "b_proj", "bo")):
            S_.dma(lambda i=i: self.a("pst", rows=(0, 1),
                                      cols=(1024 * (i + 1), 1024 * (i + 2))),
                   lambda b=bsrc: self.dr[b][0:1, :])
        S_.emit("vector", lambda e: e.memset(self.a("bdp"), 0.0))
        S_.wait_all("vector", "dml", "ve")
        for blk, b0 in ((0, 0), (1, 6), (2, 10)):
            S_.emit("vector", lambda e, blk=blk, b0=b0: e.tensor_copy(
                self.a("bdp", cols=(blk * 128, blk * 128 + 4)),
                self.a("bdiag", cols=(b0, b0 + 4))))
        S_.wait_all("tensor", "dml")
        for i in range(2):
            S_.transpose(lambda: self.pb[1].ap()[:, 0:8],
                         lambda i=i: self.a("pst", rows=(0, 8),
                                            cols=(i * 128, i * 128 + 128)),
                         lambda: self.a("identf", rows=(0, 8), cols=(0, 8)))
            S_.wait_all("vector", "te")
            S_.emit("vector", lambda e, i=i: e.tensor_copy(
                self.a("biasqkv", cols=(8 * i, 8 * i + 8)),
                self.pb[1].ap()[:, 0:8]))
            S_.wait_all("tensor", "ve")
        for dst, i in [("bvb", 1), ("bbproj", 2), ("bbo", 3)]:
            for hf in range(2):
                S_.mm(lambda: self.pb[0].ap()[:, 0:512],
                      lambda: self.a("onesf", rows=(0, 1)),
                      lambda i=i, hf=hf: self.a(
                          "pst", rows=(0, 1),
                          cols=(1024 * i + 512 * hf, 1024 * i + 512 * (hf + 1))),
                      kdim=1, mdim=128, start=True, stop=True)
                S_.wait_all("vector", "te")
                S_.emit("vector", lambda e, d=dst, hf=hf: e.tensor_copy(
                    self.a(d, cols=(512 * hf, 512 * (hf + 1))),
                    self.pb[0].ap()[:, 0:512]))
                S_.wait_all("tensor", "ve")

    def emit_prep_tquant(self, wname, R, C, dst, midx):
        S_ = self.S_
        RB = R // 128
        S_.wait_all("scalar", "te", "ve", "dms")
        for r in range(RB):
            sl = r % 2
            S_.wait_all("sync", "se")
            S_.dma(lambda sl=sl, C=C: self.a("pw", cols=(sl * C, sl * C + C)),
                   lambda r=r, w=wname: self.dr[w][128 * r:128 * (r + 1), :])
            S_.wait_all("scalar", "dml")
            S_.emit("scalar", lambda e, sl=sl, r=r, C=C: e.activation(
                self.a("pst", cols=(0, C)),
                self.a("pw", cols=(sl * C, sl * C + C)), AF.Abs,
                accum_out=self.col("snorm", r)))
        S_.wait_all("tensor", "se")
        S_.mm(lambda RB=RB: self.pb[0].ap()[0:1, 0:RB],
              lambda: self.a("onesf", cols=(0, 1)),
              lambda RB=RB: self.a("snorm", cols=(0, RB)),
              kdim=128, mdim=1, start=True, stop=True)
        S_.wait_all("vector", "te")
        S_.emit("vector", lambda e, RB=RB: e.tensor_copy(
            self.a("sstats", rows=(0, 1), cols=(40, 40 + RB)),
            self.pb[0].ap()[0:1, 0:RB]))
        S_.wait_all("vector", "ve")
        S_.emit("vector", lambda e, RB=RB: e.tensor_reduce(
            self.a("sstats", rows=(0, 1), cols=(72, 73)),
            self.a("sstats", rows=(0, 1), cols=(40, 40 + RB)),
            axis=AXX, op=ALU.add))
        S_.wait_all("vector", "ve")
        S_.emit("vector", lambda e, R=R, C=C: e.tensor_scalar(
            self.a("sstats", rows=(0, 1), cols=(73, 74)),
            self.a("sstats", rows=(0, 1), cols=(72, 73)),
            1.0 / (R * C), 1e-5, op0=ALU.mult, op1=ALU.max))
        S_.wait_all("vector", "ve")
        S_.emit("vector", lambda e: e.reciprocal(
            self.a("sstats", rows=(0, 1), cols=(74, 75)),
            self.a("sstats", rows=(0, 1), cols=(73, 74))))
        S_.wait_all("tensor", "ve")
        S_.mm(lambda: self.pb[0].ap()[:, 0:1],
              lambda: self.a("onesf", rows=(0, 1)),
              lambda: self.a("sstats", rows=(0, 1), cols=(74, 75)),
              kdim=1, mdim=128, start=True, stop=True)
        S_.wait_all("vector", "te")
        S_.emit("vector", lambda e: e.tensor_copy(
            self.col("sstats", 75), self.pb[0].ap()[:, 0:1]))
        S_.wait_all("tensor", "ve")
        S_.mm(lambda: self.pb[1].ap()[:, 0:1],
              lambda: self.a("onesf", rows=(0, 1)),
              lambda: self.a("sstats", rows=(0, 1), cols=(73, 74)),
              kdim=1, mdim=128, start=True, stop=True)
        S_.wait_all("vector", "te")
        S_.emit("vector", lambda e, midx=midx: e.tensor_copy(
            self.col("gad", 6 + midx), self.pb[1].ap()[:, 0:1]))
        for r in range(RB):
            sl = r % 2
            S_.wait_all("sync", "se", "ve")
            S_.dma(lambda sl=sl, C=C: self.a("pw", cols=(sl * C, sl * C + C)),
                   lambda r=r, w=wname: self.dr[w][128 * r:128 * (r + 1), :])
            S_.wait_all("scalar", "dml", "ve")
            S_.emit("scalar", lambda e, sl=sl, C=C: e.activation(
                self.a("pst", cols=(0, C)),
                self.a("pw", cols=(sl * C, sl * C + C)), AF.Identity,
                bias=self.col("gad", 0), scale=self.col("sstats", 75)))
            S_.wait_all("vector", "se")
            S_.emit("vector", lambda e, C=C: e.tensor_scalar(
                self.a("pst", cols=(C, 2 * C)) if 2 * C <= 4096 else
                self.a("pst", cols=(0, C)),
                self.a("pst", cols=(0, C)),
                self.col("gad", 1), -1.0, op0=ALU.add, op1=ALU.max))
            S_.wait_all("vector", "ve")
            src_cols = (C, 2 * C) if 2 * C <= 4096 else (0, C)
            S_.emit("vector", lambda e, r=r, C=C, sc=src_cols: e.tensor_scalar(
                self.a("ptb", cols=(r * C, r * C + C)),
                self.a("pst", cols=sc), 1.0, None, op0=ALU.min))
        S_.wait_all("tensor", "ve")
        for cc in range(C // 128):
            for rr in range(RB):
                S_.transpose(lambda: self.pbb.ap()[:, 0:128],
                             lambda rr=rr, cc=cc, C=C: self.a(
                                 "ptb", cols=(rr * C + cc * 128,
                                              rr * C + cc * 128 + 128)),
                             lambda: self.a("identb"))
                S_.wait_all("scalar", "te")
                S_.emit("scalar", lambda e, rr=rr: e.copy(
                    self.a("pst8", cols=(rr * 128, rr * 128 + 128)),
                    self.pbb.ap()[:, 0:128]))
                S_.wait_all("tensor", "se")
            S_.wait_all("sync", "se", "dms")
            S_.dma_store(lambda cc=cc, R=R, d=dst: self.dr[d][
                128 * cc:128 * (cc + 1), 0:R],
                lambda R=R: self.a("pst8", cols=(0, R)))
            S_.wait_all("scalar", "dms")

    def emit_prep_wt(self, wname, dst):
        S_ = self.S_
        S_.wait_all("sync", "te", "ve", "dms")
        for r in range(8):
            S_.dma(lambda r=r: self.a("pw", cols=(r * 1024, r * 1024 + 1024)),
                   lambda r=r, w=wname: self.dr[w][128 * r:128 * (r + 1), :])
        S_.wait_all("tensor", "dml")
        for cc in range(8):
            for rr in range(8):
                S_.transpose(lambda rr=rr: self.pb[rr % 2].ap()[:, 0:128],
                             lambda rr=rr, cc=cc: self.a(
                                 "pw", cols=(rr * 1024 + cc * 128,
                                             rr * 1024 + cc * 128 + 128)),
                             lambda: self.a("identf"))
                S_.wait_all("vector", "te")
                S_.emit("vector", lambda e, rr=rr: e.tensor_copy(
                    self.a("pst", cols=(rr * 128, rr * 128 + 128)),
                    self.pb[rr % 2].ap()[:, 0:128]))
                S_.wait_all("tensor", "ve")
            S_.wait_all("sync", "ve", "dms")
            S_.dma_store(lambda cc=cc, d=dst: self.dr[d][
                128 * cc:128 * (cc + 1), :],
                lambda: self.a("pst", cols=(0, 1024)))
            S_.wait_all("vector", "dms")

    def emit_ln(self, x_fn, y_fn, b=8):
        S_ = self.S_
        S_.wait_all("vector", "se", "ve")
        S_.emit("vector", lambda e, f=x_fn, b=b: e.bn_stats(
            self.a("sstats", cols=(b, b + 6)), f()[:, 0:512]))
        S_.emit("vector", lambda e, f=x_fn, b=b: e.bn_stats(
            self.a("sstats", cols=(b + 6, b + 12)), f()[:, 512:1024]))
        S_.wait_all("vector", "ve")
        S_.emit("vector", lambda e, b=b: e.bn_aggr(
            self.a("sstats", cols=(b + 12, b + 14)),
            self.a("sstats", cols=(b, b + 12))))
        S_.wait_all("scalar", "ve")
        S_.emit("scalar", lambda e, b=b: e.activation(
            self.col("sstats", b + 14), self.col("sstats", b + 13), AF.Sqrt,
            bias=self.col("gad", 4), scale=1.0))
        S_.wait_all("vector", "se")
        S_.emit("vector", lambda e, b=b: e.reciprocal(
            self.col("sstats", b + 15), self.col("sstats", b + 14)))
        S_.wait_all("vector", "ve")
        S_.emit("vector", lambda e, b=b: e.scalar_tensor_tensor(
            self.col("sstats", b + 16), self.col("sstats", b + 12), -1.0,
            self.col("sstats", b + 15), op0=ALU.mult, op1=ALU.mult))
        S_.wait_all("scalar", "ve", "te")
        S_.emit("scalar", lambda e, xf=x_fn, yf=y_fn, b=b: e.activation(
            yf(), xf(), AF.Identity,
            bias=self.col("sstats", b + 16), scale=self.col("sstats", b + 15)))

    def emit_actquant(self, in_fn, out_fn, a_fn, qb=32):
        S_ = self.S_
        S_.wait_all("vector", "se", "ve")
        S_.emit("vector", lambda e, f=in_fn, qb=qb: e.tensor_reduce(
            self.col("sstats", qb), f(), axis=AXX, op=ALU.max,
            apply_absolute_value=True))
        S_.wait_all("vector", "ve")
        S_.emit("vector", lambda e, qb=qb: e.tensor_scalar(
            self.col("sstats", qb + 1), self.col("sstats", qb), 1e-5, None,
            op0=ALU.max))
        S_.wait_all("vector", "ve")
        S_.emit("vector", lambda e, qb=qb: e.reciprocal(
            self.col("sstats", qb + 2), self.col("sstats", qb + 1)))
        S_.wait_all("vector", "ve")
        S_.emit("vector", lambda e, qb=qb: e.tensor_scalar(
            self.col("sstats", qb + 3), self.col("sstats", qb + 2), 127.0, None,
            op0=ALU.mult))
        S_.wait_all("scalar", "ve")
        S_.emit("scalar", lambda e, f=in_fn, af=a_fn, qb=qb: e.activation(
            af(), f(), AF.Identity,
            bias=self.col("gad", 0), scale=self.col("sstats", qb + 3)))
        S_.wait_all("vector", "se")
        S_.emit("vector", lambda e, of=out_fn, af=a_fn: e.tensor_scalar(
            of(), af(), self.col("gad", 1), None, op0=ALU.add))

    def emit_attn(self, kind, xsrc_row0, xsrc, xdst, xdst_row0):
        S_ = self.S_
        S, SWEEP, NSW, SWT = self.S, self.SWEEP, self.NSW, self.SWT
        QCH, NQC, KT, NT = self.QCH, self.NQC, self.KT, self.NT
        rope = kind == "a"
        causal = kind == "b"

        S_.wait_all("vector", "te", "se", "ge", "dms")
        S_.wait_all("sync", "te", "ve", "se", "ge", "dms")
        S_.wait_all("gpsimd", "te", "ve", "se", "dms")
        S_.emit("vector", lambda e: e.memset(
            self.mkap("Vb", HD, [[H * HD + H, NT], [HD + 1, H]]), 1.0))
        S_.emit("vector", lambda e: e.memset(self.a("snorm", cols=(32, 48)), 0.0))

        for i in range(3):
            S_.sem(f"wk{i}")
            S_.sem(f"wg{i}")
        if not hasattr(self, "wk_te"):
            self.wk_te = {}
        wk_te = self.wk_te
        wk_m = {}

        WSLN = {0: "wbuf", 1: "wq2", 2: "stage"}

        def wsl(sl, cols):
            return self.a(WSLN[sl], cols=cols)

        def load_slot(sl, srcs, extra_te=None, se_guard=False):
            for eng in ("sync", "gpsimd"):
                if sl in wk_te:
                    S_.wait_val(eng, "te", wk_te[sl])
                if extra_te is not None:
                    S_.wait_val(eng, "te", extra_te)
                if se_guard:
                    S_.wait_all(eng, "se")
            for i, (dcols, src) in enumerate(srcs):
                S_.dma(lambda sl=sl, dcols=dcols: wsl(sl, dcols), src,
                       sem=f"wk{sl}")
            return (S_.mark(f"wk{sl}"), S_.mark(f"wg{sl}"))

        def wait_slot(eng, sl, m):
            S_.wait_val(eng, f"wk{sl}", m[0])
            S_.wait_val(eng, f"wg{sl}", m[1])

        def vsrcs(vb, h):
            out = []
            for cq in range(4):
                c = 4 * h + cq
                if kind == "a":
                    src = (lambda c=c, vb=vb: self.dr["w_qkv"][
                        128 * c:128 * (c + 1),
                        2 * D + vb * 512:2 * D + (vb + 1) * 512])
                else:
                    src = (lambda c=c, vb=vb: self.dr["WvT"][
                        128 * c:128 * (c + 1), vb * 512:(vb + 1) * 512])
                out.append(((cq * 512, cq * 512 + 512), src))
            return out

        def qksrcs(which, g):
            out = []
            base_col = 0 if which == "q" else D
            for c in range(NCH):
                if kind == "a":
                    src = (lambda c=c, g=g, b=base_col: self.dr["w_qkv"][
                        128 * c:128 * (c + 1),
                        b + 256 * g:b + 256 * (g + 1)])
                else:
                    wsc = "WqT" if which == "q" else "WkT"
                    src = (lambda c=c, g=g, w=wsc: self.dr[w][
                        128 * c:128 * (c + 1), 256 * g:256 * (g + 1)])
                out.append(((c * 256, c * 256 + 256), src))
            return out

        S_.sem("wk2")
        for sw in range(NSW):
            def ybuf(t4):
                name = "stage" if t4 % 2 == 0 else "wbuf"
                return name

            def gen_transposes(t4):
                S_.wait_all("tensor", "se")
                for grp in range(2):
                    S_.wait_all("tensor", "ve")
                    for cc in range(4):
                        c = grp * 4 + cc
                        S_.transpose(lambda grp=grp, cc=cc:
                                     self.pb[grp % 2].ap()[:, cc * 128:
                                                           cc * 128 + 128],
                                     lambda c=c, t4=t4: self.a(
                                         ybuf(t4), cols=(1024 + c * 128,
                                                         1024 + c * 128 + 128)),
                                     lambda: self.a("identf"))
                    S_.wait_all("vector", "te")
                    S_.emit("vector", lambda e, grp=grp, t4=t4: e.tensor_copy(
                        self.mkap("yfm", grp * 4 * SWEEP + t4 * 128,
                                  [[SWEEP, 4], [1, 128]]),
                        self.pb[grp % 2].ap()[:, 0:512]))

            for t4 in range(SWT):
                t = sw * SWT + t4
                if t4 >= 1:
                    gen_transposes(t4 - 1)
                S_.wait_all("sync", "se", "dms", "te")
                S_.dma(lambda t4=t4: self.a(ybuf(t4), cols=(0, 1024)),
                       lambda t=t: xsrc[xsrc_row0 + 128 * t:
                                        xsrc_row0 + 128 * (t + 1), :])
                S_.wait_all("vector", "dml")
                S_.wait_all("scalar", "dml")
                self.emit_ln(lambda t4=t4: self.a(ybuf(t4), cols=(0, 1024)),
                             lambda t4=t4: self.a(ybuf(t4), cols=(1024, 2048)),
                             b=8 + 32 * (t4 % 2))
            gen_transposes(SWT - 1)
            for vb in range(2):
                S_.wait_all("sync", "te", "ve")
                for c in range(NCH):
                    if kind == "a":
                        src = (lambda c=c, vb=vb: self.dr["w_qkv"][
                            128 * c:128 * (c + 1),
                            2 * D + vb * 512:2 * D + (vb + 1) * 512])
                    else:
                        src = (lambda c=c, vb=vb: self.dr["WvT"][
                            128 * c:128 * (c + 1), vb * 512:(vb + 1) * 512])
                    S_.dma(lambda c=c: self.a("wbuf",
                                              cols=(c * 512, c * 512 + 512)), src)
                S_.wait_all("tensor", "dml", "ve")
                for t4 in range(SWT):
                    t = sw * SWT + t4
                    bank = 4 + t4 % 2
                    for c in range(NCH):
                        S_.mm(lambda bank=bank: self.pb[bank].ap()[:, 0:512],
                              lambda c=c, t4=t4: self.a(
                                  "yfm", cols=(c * SWEEP + t4 * 128,
                                               c * SWEEP + t4 * 128 + 128)),
                              lambda c=c: self.a("wbuf",
                                                 cols=(c * 512, c * 512 + 512)),
                              kdim=128, mdim=128, start=(c == 0),
                              stop=(c == NCH - 1))
                    S_.wait_all("vector", "te")

                    def vevac(e, t=t, vb=vb, bank=bank, kind=kind):
                        o = t * (H * HD + H) + vb * 8 * (HD + 1)
                        dst = self.mkap("Vb", o, [[HD + 1, 8], [1, HD]])
                        sap = self.pb[bank].ap()
                        src_ = bass.AP(sap.tensor, sap.offset,
                                       [[sap.ap[0][0], 128], [HD, 8], [1, HD]])
                        if kind == "b":
                            bias = self.mkap("bvb", vb * 512, [[HD, 8], [1, HD]])
                            return e.tensor_tensor(dst, src_, bias, op=ALU.add)
                        return e.tensor_copy(dst, src_)
                    S_.emit("vector", vevac)
                    S_.wait_all("tensor", "ve")
            if rope:
                S_.wait_all("sync", "ve")
                S_.dma(lambda: self.a("wbuf", cols=(2048, 2048 + SWEEP)),
                       lambda sw=sw: self.dr["ropec"][
                           :, sw * SWEEP:(sw + 1) * SWEEP])
                S_.dma(lambda: self.a("wbuf",
                                      cols=(2048 + SWEEP, 2048 + 2 * SWEEP)),
                       lambda sw=sw: self.dr["ropes"][
                           :, sw * SWEEP:(sw + 1) * SWEEP])
            pend_norm = [None]
            for which in ("q", "k"):
                for g in range(4):
                    wi = (0 if which == "q" else 4) + g
                    S_.wait_all("sync", "te", "ve", "se", "dms")
                    for c in range(NCH):
                        if kind == "a":
                            b_ = 0 if which == "q" else D
                            src = (lambda c=c, g=g, b=b_: self.dr["w_qkv"][
                                128 * c:128 * (c + 1),
                                b + 256 * g:b + 256 * (g + 1)])
                        else:
                            wsc = "WqT" if which == "q" else "WkT"
                            src = (lambda c=c, g=g, w=wsc: self.dr[w][
                                128 * c:128 * (c + 1), 256 * g:256 * (g + 1)])
                        S_.dma(lambda c=c: self.a(
                            "wbuf", cols=(c * 256, c * 256 + 256)), src)
                    S_.wait_all("tensor", "dml")
                    for par in range(2):
                        m = 2 * g + par
                        for c in range(NCH):
                            if kind == "a":
                                lhs = (lambda c=c, par=par: self.mkap(
                                    "wbuf", c * 256 + par, [[2, 128]]))
                            else:
                                lhs = (lambda c=c, par=par: self.a(
                                    "wbuf", cols=(c * 256 + par * 128,
                                                  c * 256 + par * 128 + 128)))
                            S_.mm(lambda: self.pb[6].ap()[:, 0:SWEEP],
                                  lhs,
                                  lambda c=c: self.a(
                                      "yfm", cols=(c * SWEEP, (c + 1) * SWEEP)),
                                  kdim=128, mdim=128, start=(c == 0),
                                  stop=(c == NCH - 1))
                        if kind == "b":
                            S_.wait_all("scalar", "te", "ve", "dms")
                            S_.emit("scalar", lambda e, par=par, m=m, w=which:
                                    e.activation(
                                        self.a("stage", cols=(par * SWEEP,
                                                              (par + 1) * SWEEP)),
                                        self.pb[6].ap()[:, 0:SWEEP], AF.Identity,
                                        bias=self.col("biasqkv",
                                                      (0 if w == "q" else 8) + m),
                                        scale=1.0))
                            S_.wait_all("tensor", "se")
                        else:
                            S_.wait_all("vector", "te", "dms")
                            S_.emit("vector", lambda e, par=par: e.tensor_copy(
                                self.a("stage", cols=(par * SWEEP,
                                                      (par + 1) * SWEEP)),
                                self.pb[6].ap()[:, 0:SWEEP]))
                            S_.wait_all("tensor", "ve")
                    if pend_norm[0] is not None:
                        pend_norm[0]()
                        pend_norm[0] = None
                    qe = lambda: self.a("stage", cols=(0, SWEEP))
                    qo = lambda: self.a("stage", cols=(SWEEP, 2 * SWEEP))
                    if rope:
                        cosv = lambda: self.a("wbuf", cols=(2048, 2048 + SWEEP))
                        sinv = lambda: self.a("wbuf", cols=(2048 + SWEEP,
                                                            2048 + 2 * SWEEP))
                        tA = lambda: self.a("wbuf", cols=(3072, 3072 + SWEEP))
                        tB = lambda: self.a("wbuf", cols=(3072 + SWEEP,
                                                          3072 + 2 * SWEEP))
                        re_ = lambda: self.a("stage", cols=(1024, 1024 + SWEEP))
                        ro_ = lambda: self.a("stage", cols=(1024 + SWEEP,
                                                            1024 + 2 * SWEEP))
                        S_.wait_all("vector", "se", "dml", "te", "ve")
                        S_.emit("vector", lambda e: e.tensor_tensor(
                            tA(), qe(), cosv(), op=ALU.mult))
                        S_.emit("vector", lambda e: e.tensor_tensor(
                            tB(), qo(), sinv(), op=ALU.mult))
                        S_.wait_all("vector", "ve")
                        S_.emit("vector", lambda e: e.tensor_tensor(
                            re_(), tA(), tB(), op=ALU.subtract))
                        S_.wait_all("vector", "ve")
                        S_.emit("vector", lambda e: e.tensor_tensor(
                            tA(), qe(), sinv(), op=ALU.mult))
                        S_.emit("vector", lambda e: e.tensor_tensor(
                            tB(), qo(), cosv(), op=ALU.mult))
                        S_.wait_all("vector", "ve")
                        S_.emit("vector", lambda e: e.tensor_tensor(
                            ro_(), tA(), tB(), op=ALU.add))
                        outE, outO = re_, ro_
                    else:
                        outE, outO = qe, qo
                    sqat = 3072
                    S_.wait_all("scalar", "ve", "te", "se")
                    S_.emit("scalar", lambda e, f=outE, sqat=sqat: e.activation(
                        self.a("wbuf", cols=(sqat, sqat + SWEEP)), f(),
                        AF.Square))
                    S_.emit("scalar", lambda e, f=outO, sqat=sqat: e.activation(
                        self.a("wbuf", cols=(sqat + SWEEP, sqat + 2 * SWEEP)),
                        f(), AF.Square))
                    m_sq = S_.mark("se")
                    ncol = (32 if which == "q" else 40) + g

                    def mknorm(ncol=ncol, sqat=sqat, m_sq=m_sq, kind=kind):
                        def nb():
                            S_.wait_val("tensor", "se", m_sq)
                            for par in range(2):
                                blk = 0 if kind == "a" else 1 + par
                                S_.mm(lambda: self.pb[3].ap()[:, 0:SWEEP],
                                      lambda blk=blk: self.a(
                                          "bdp", cols=(blk * 128,
                                                       blk * 128 + 128)),
                                      lambda par=par, sqat=sqat: self.a(
                                          "wbuf", cols=(sqat + par * SWEEP,
                                                        sqat + (par + 1) * SWEEP)),
                                      kdim=128, mdim=128, start=(par == 0),
                                      stop=(par == 1))
                            S_.wait_all("vector", "te")
                            S_.emit("vector", lambda e: e.tensor_reduce(
                                self.a("snorm", rows=(0, 4), cols=(0, 1)),
                                self.pb[3].ap()[0:4, 0:SWEEP], axis=AXX,
                                op=ALU.max))
                            S_.wait_all("vector", "ve")
                            S_.emit("vector", lambda e, nc_=ncol: e.tensor_tensor(
                                self.a("snorm", rows=(0, 4), cols=(nc_, nc_ + 1)),
                                self.a("snorm", rows=(0, 4), cols=(nc_, nc_ + 1)),
                                self.a("snorm", rows=(0, 4), cols=(0, 1)),
                                op=ALU.max))
                        return nb
                    pend_norm[0] = mknorm()
                    if which == "q":
                        S_.wait_all("sync", "ve", "se")
                        for par, f in ((0, outE), (1, outO)):
                            m = 2 * g + par
                            S_.dma_store(lambda m=m, sw=sw: self.dr["qTs"][
                                128 * m:128 * (m + 1),
                                sw * SWEEP:(sw + 1) * SWEEP], f)
                    else:
                        S_.wait_all("vector", "ve", "se")
                        for par, f in ((0, outE), (1, outO)):
                            m = 2 * g + par
                            S_.emit("vector", lambda e, m=m, sw=sw, f=f:
                                    e.tensor_copy(
                                        self.a("kT",
                                               cols=(m * S + sw * SWEEP,
                                                     m * S + (sw + 1) * SWEEP)),
                                        f()))
                        S_.wait_all("sync", "ve")
            pend_norm[0]()
            pend_norm[0] = None

        S_.wait_all("vector", "ve", "te", "se")
        for g in range(4):
            S_.emit("vector", lambda e, g=g: e.tensor_tensor(
                self.a("snorm", rows=(0, 4), cols=(8 + g, 9 + g)),
                self.a("snorm", rows=(0, 4), cols=(32 + g, 33 + g)),
                self.a("snorm", rows=(0, 4), cols=(40 + g, 41 + g)),
                op=ALU.mult))
        S_.wait_all("scalar", "ve")
        for g in range(4):
            S_.emit("scalar", lambda e, g=g: e.activation(
                self.a("snorm", rows=(0, 4), cols=(12 + g, 13 + g)),
                self.a("snorm", rows=(0, 4), cols=(8 + g, 9 + g)), AF.Sqrt))
        S_.wait_all("vector", "se")
        for g in range(4):
            S_.emit("vector", lambda e, g=g: e.tensor_scalar(
                self.a("snorm", rows=(0, 4), cols=(16 + g, 17 + g)),
                self.a("snorm", rows=(0, 4), cols=(12 + g, 13 + g)),
                -0.125, None, op0=ALU.mult))
        S_.wait_all("tensor", "ve")
        for g in range(4):
            S_.transpose(lambda: self.pb[2].ap()[0:1, 0:4],
                         lambda g=g: self.a("snorm", rows=(0, 4),
                                            cols=(16 + g, 17 + g)),
                         lambda: self.a("identf", rows=(0, 4), cols=(0, 4)))
            S_.wait_all("vector", "te")
            S_.emit("vector", lambda e: e.tensor_copy(
                self.a("sstats", rows=(0, 1), cols=(100, 104)),
                self.pb[2].ap()[0:1, 0:4]))
            S_.wait_all("tensor", "ve")
            S_.mm(lambda: self.pb[2].ap()[:, 0:4],
                  lambda: self.a("onesf", rows=(0, 1)),
                  lambda: self.a("sstats", rows=(0, 1), cols=(100, 104)),
                  kdim=1, mdim=128, start=True, stop=True)
            S_.wait_all("vector", "te")
            S_.emit("vector", lambda e, g=g: e.tensor_copy(
                self.a("negc", cols=(4 * g, 4 * g + 4)),
                self.pb[2].ap()[:, 0:4]))
            S_.wait_all("tensor", "ve")

        wps = "w_proj" if kind == "a" else "WoT"
        bbst = "bbproj" if kind == "a" else "bbo"
        NTT = QCH // 128
        S_.sem("wq0"); S_.sem("wq1")
        pb_se = {}
        for qc in range(NQC):
            for e_ in ("scalar", "gpsimd", "tensor", "vector"):
                S_.wait_all(e_, "wq0", "wq1")
            for g in range(4):
                idx = qc * 4 + g
                S_.wait_all("sync", "te", "se", "ve")
                for par in range(2):
                    m = 2 * g + par
                    S_.dma(lambda par=par: self.a(
                        "qbuf", cols=(par * QCH, (par + 1) * QCH)),
                        lambda m=m, qc=qc: self.dr["qTs"][
                            128 * m:128 * (m + 1), qc * QCH:(qc + 1) * QCH])
                S_.wait_all("tensor", "dml")
                for pi in range(2):
                    hs = [4 * g + 2 * pi, 4 * g + 2 * pi + 1]
                    kts = [kt for kt in range(KT)
                           if not (causal and 128 * kt > qc * QCH + QCH - 1)]
                    KCH = 4
                    nchk = (len(kts) + KCH - 1) // KCH
                    for ci in range(nchk):
                        chunk = kts[ci * KCH:(ci + 1) * KCH]
                        for li, kt in enumerate(chunk):
                            gki = ci * KCH + li
                            bk = (gki % 2) * 2
                            if bk in pb_se:
                                S_.wait_val("tensor", "se", pb_se[bk])
                            for j, h in enumerate(hs):
                                i = h % 4
                                if kind == "a":
                                    parts = [(2 * g, 32 * i, 32),
                                             (2 * g + 1, 32 * i, 32)]
                                else:
                                    parts = [(2 * g + i // 2, 64 * (i % 2), 64)]
                                for pidx, (ch, r0, kk) in enumerate(parts):
                                    par_idx = ch - 2 * g
                                    S_.mm(lambda bk=bk, j=j:
                                          self.pb[bk + j].ap()[:, 0:QCH],
                                          lambda ch=ch, r0=r0, kk=kk, kt=kt:
                                          self.a("kT", rows=(r0, r0 + kk),
                                                 cols=(ch * S + kt * 128,
                                                       ch * S + kt * 128 + 128)),
                                          lambda pp=par_idx, r0=r0, kk=kk:
                                          self.a("qbuf", rows=(r0, r0 + kk),
                                                 cols=(pp * QCH,
                                                       (pp + 1) * QCH)),
                                          kdim=kk, mdim=128, start=(pidx == 0),
                                          stop=(pidx == len(parts) - 1),
                                          tile_position=(r0, 0))
                            S_.wait_all("scalar", "te")
                            diag = causal and (kt * 128 + 127 > qc * QCH)
                            for j, h in enumerate(hs):
                                sl = (gki % 4) * 2 + j
                                S_.emit("scalar", lambda e, bk=bk, j=j, h=h, sl=sl:
                                        e.activation(
                                            self.a("abuf", cols=(sl * QCH,
                                                                 (sl + 1) * QCH)),
                                            self.pb[bk + j].ap()[:, 0:QCH], AF.Exp,
                                            bias=self.col("negc", h),
                                            scale=self.col("gad", 5)))
                            pb_se[bk] = S_.mark("se")
                            if diag:
                                S_.wait_all("gpsimd", "se")
                                for j in range(2):
                                    sl = (gki % 4) * 2 + j
                                    S_.emit("gpsimd", lambda e, sl=sl, qc=qc,
                                            kt=kt: e.affine_select(
                                                self.a("abuf", cols=(sl * QCH,
                                                                     (sl + 1) * QCH)),
                                                self.a("abuf", cols=(sl * QCH,
                                                                     (sl + 1) * QCH)),
                                                [[1, QCH]], ALU.is_ge, 0.0,
                                                base=qc * QCH - kt * 128,
                                                channel_multiplier=-1))
                        S_.wait_all("tensor", "se", "ge", "ve")
                        for li, kt in enumerate(chunk):
                            gki = ci * KCH + li
                            for j, h in enumerate(hs):
                                sl = (gki % 4) * 2 + j
                                S_.mm(lambda j=j: self.pb[4 + j].ap()[0:65, 0:QCH],
                                      lambda h=h, kt=kt: self.a(
                                          "Vb", cols=(kt * (H * HD + H)
                                                      + h * (HD + 1),
                                                      kt * (H * HD + H)
                                                      + h * (HD + 1) + HD + 1)),
                                      lambda sl=sl: self.a(
                                          "abuf", cols=(sl * QCH, (sl + 1) * QCH)),
                                      kdim=128, mdim=65, start=(gki == 0),
                                      stop=(gki == len(kts) - 1))
                    S_.wait_all("vector", "te", "ve")
                    for j in range(2):
                        S_.emit("vector", lambda e, j=j: e.reciprocal(
                            self.a("brow", rows=(64, 65),
                                   cols=(j * QCH, j * QCH + QCH)),
                            self.pb[4 + j].ap()[64:65, 0:QCH]))
                    S_.wait_all("tensor", "ve")
                    for j in range(2):
                        S_.mm(lambda j=j: (self.pb[6] if j == 0
                                           else self.pb[j]).ap()[0:65, 0:QCH],
                              lambda: self.a("e64row", cols=(0, 65)),
                              lambda j=j: self.a("brow",
                                                 cols=(j * QCH, j * QCH + QCH)),
                              kdim=128, mdim=65, start=True, stop=True)
                    S_.wait_all("scalar", "te", "dms")
                    for j in range(2):
                        S_.emit("scalar", lambda e, j=j: e.copy(
                            self.a("stage", rows=(0, 64),
                                   cols=(1024 + j * QCH, 1024 + (j + 1) * QCH)),
                            (self.pb[6] if j == 0 else self.pb[j]).ap()[0:64, 0:QCH]))
                    pb_se[0] = S_.mark("se")
                    S_.wait_all("vector", "se")
                    for j, h in enumerate(hs):
                        S_.emit("vector", lambda e, j=j, h=h: e.tensor_tensor(
                            self.a("obuf", rows=(64 * (h % 2), 64 * (h % 2) + 64),
                                   cols=((h // 2) * QCH, (h // 2 + 1) * QCH)),
                            self.pb[4 + j].ap()[0:64, 0:QCH],
                            self.a("stage", rows=(0, 64),
                                   cols=(1024 + j * QCH, 1024 + (j + 1) * QCH)),
                            op=ALU.mult))
            S_.sem("wq0"); S_.sem("wq1")
            te_marks = {}

            def proj_mms(oc, c):
                sl = c % 2
                S_.wait_val("tensor", f"wq{sl}", S_.mark(f"wq{sl}"))
                for tt in range(NTT):
                    S_.mm(lambda tt=tt: self.pb[tt].ap()[:, 0:512],
                          lambda c=c, tt=tt: self.a(
                              "obuf", cols=(c * QCH + tt * 128,
                                            c * QCH + tt * 128 + 128)),
                          lambda sl=sl: self.a("wp", cols=(sl * 512,
                                                           sl * 512 + 512)),
                          kdim=128, mdim=128, start=(c == 0),
                          stop=(c == NCH - 1))
                te_marks[c] = S_.mark("te")

            m_ve_attn = S_.mark("ve")
            m_ge_attn = S_.mark("ge")
            for oc in range(2):
                te_marks.clear()
                for c in range(NCH):
                    sl = c % 2
                    if c >= 2:
                        S_.wait_val("sync", "te", te_marks[c - 2])
                    elif oc == 0:
                        S_.wait_val("sync", "ve", m_ve_attn)
                        S_.wait_val("sync", "ge", m_ge_attn)
                    else:
                        S_.wait_all("sync", "te")
                    S_.dma(lambda sl=sl: self.a("wp", cols=(sl * 512,
                                                            sl * 512 + 512)),
                           lambda c=c, oc=oc, w=wps: self.dr[w][
                               128 * c:128 * (c + 1), oc * 512:(oc + 1) * 512],
                           sem=f"wq{sl}")
                    if c >= 1:
                        proj_mms(oc, c - 1)
                proj_mms(oc, NCH - 1)
                S_.wait_all("vector", "te")
                for tt in range(NTT):
                    S_.wait_all("sync", "ve", "se")
                    S_.dma(lambda: self.a("xpb"),
                           lambda tt=tt, qc=qc, oc=oc: xsrc[
                               xsrc_row0 + qc * QCH + 128 * tt:
                               xsrc_row0 + qc * QCH + 128 * (tt + 1),
                               oc * 512:(oc + 1) * 512])
                    S_.wait_all("vector", "dml")
                    S_.emit("vector", lambda e, tt=tt, oc=oc, b=bbst:
                            e.tensor_tensor(
                                self.a("stage", cols=(1024, 1536)),
                                self.pb[tt].ap()[:, 0:512],
                                self.a(b, cols=(oc * 512, (oc + 1) * 512)),
                                op=ALU.add))
                    S_.wait_all("vector", "ve", "dms")
                    S_.emit("vector", lambda e: e.tensor_tensor(
                        self.a("stage", cols=(1536, 2048)),
                        self.a("stage", cols=(1024, 1536)),
                        self.a("xpb"),
                        op=ALU.add))
                    S_.wait_all("sync", "ve")
                    S_.dma_store(lambda tt=tt, qc=qc, oc=oc: xdst[
                        xdst_row0 + qc * QCH + 128 * tt:
                        xdst_row0 + qc * QCH + 128 * (tt + 1),
                        oc * 512:(oc + 1) * 512],
                        lambda: self.a("stage", cols=(1536, 2048)))

    def emit_ffn(self, xsrc_row0, xsrc, xdst, xdst_row0):
        S_ = self.S_
        NT = self.NT
        S_.sem("tld")
        for i in range(3):
            S_.sem(f"xl{i}")
        S_.wait_all("sync", "te", "ve", "se", "ge", "dms")
        m_ve0, m_se0 = S_.mark("ve"), S_.mark("se")
        M = {}
        for tp in range(min(2, NT)):
            S_.dma(lambda tp=tp: self.a("fx", cols=(tp * 1024, tp * 1024 + 1024)),
                   lambda tp=tp: xsrc[xsrc_row0 + 128 * tp:
                                      xsrc_row0 + 128 * (tp + 1), :],
                   sem=f"xl{tp}")
            M[("xl", tp)] = S_.mark(f"xl{tp}")
        for c in range(NCH):
            S_.dma(lambda c=c: self.a("tg", cols=(c * INNER, (c + 1) * INNER)),
                   lambda c=c: self.dr["TgS"][128 * c:128 * (c + 1), :], sem="tld")
            S_.dma(lambda c=c: self.a("tu", cols=(c * INNER, (c + 1) * INNER)),
                   lambda c=c: self.dr["TuS"][128 * c:128 * (c + 1), :], sem="tld")
        for c in range(32):
            S_.dma(lambda c=c: self.a("td", cols=(c * D, (c + 1) * D)),
                   lambda c=c: self.dr["TdS"][128 * c:128 * (c + 1), :], sem="tld")
        tld_all = S_.mark("tld")

        def xap(t):
            return self.a("fx", cols=((t % 3) * 1024, (t % 3) * 1024 + 1024))

        def dma_x(t):
            if t >= 3:
                S_.wait_val("sync", "ve", M[("resid_ve", t - 3)])
                S_.wait_val("sync", "se", M[("ln_se", t - 3)])
            S_.dma(lambda t=t: xap(t),
                   lambda t=t: xsrc[xsrc_row0 + 128 * t:
                                    xsrc_row0 + 128 * (t + 1), :],
                   sem=f"xl{t % 3}")
            M[("xl", t)] = S_.mark(f"xl{t % 3}")

        def lnq1(t):
            p = t % 2
            b, qb = 8 + 32 * p, 64 + 16 * p
            S_.wait_val("vector", f"xl{t % 3}", M[("xl", t)])
            S_.wait_val("scalar", f"xl{t % 3}", M[("xl", t)])
            if ("q1T_te", t - 2) in M:
                S_.wait_val("vector", "te", M[("q1T_te", t - 2)])
            self.emit_ln(lambda t=t: xap(t),
                         lambda p=p: self.a("yq", cols=(p * 1024,
                                                        p * 1024 + 1024)),
                         b=b)
            self.emit_actquant(
                lambda p=p: self.a("yq", cols=(p * 1024, p * 1024 + 1024)),
                lambda p=p: self.a("q1t", cols=(p * 1024, p * 1024 + 1024)),
                lambda: self.a("silu", cols=(0, 1024)), qb=qb)
            S_.wait_all("vector", "ve")
            S_.emit("vector", lambda e, qb=qb: e.scalar_tensor_tensor(
                self.col("sstats", qb + 8), self.col("sstats", qb + 1),
                1.0 / 127.0, self.col("gad", 6), op0=ALU.mult, op1=ALU.mult))
            S_.emit("vector", lambda e, qb=qb: e.scalar_tensor_tensor(
                self.col("sstats", qb + 9), self.col("sstats", qb + 1),
                1.0 / 127.0, self.col("gad", 7), op0=ALU.mult, op1=ALU.mult))
            M[("ts1", t)] = S_.mark("ve")
            M[("ln_se", t)] = S_.mark("se")

        def q1T(t):
            p = t % 2
            S_.wait_val("tensor", "ve", M[("ts1", t)])
            if t == 0:
                S_.wait_val("tensor", "se", m_se0)
            for c in range(NCH):
                S_.transpose(lambda c=c: self.pbb.ap()[:, c * 128:c * 128 + 128],
                             lambda c=c, p=p: self.a(
                                 "q1t", cols=(p * 1024 + c * 128,
                                              p * 1024 + c * 128 + 128)),
                             lambda: self.a("identb"))
            M[("q1T_te", t)] = S_.mark("te")
            S_.wait_val("scalar", "te", M[("q1T_te", t)])
            S_.emit("scalar", lambda e, p=p: e.copy(
                self.a("q1f", cols=(p * 1024, p * 1024 + 1024)),
                self.pbb.ap()[:, 0:1024]))
            M[("q1f", t)] = S_.mark("se")

        def quant2(t):
            if ("q2T_te", t - 1) in M:
                S_.wait_val("vector", "te", M[("q2T_te", t - 1)])
            self.emit_actquant(lambda: self.a("hb"), lambda: self.a("q2t"),
                               lambda: self.a("silu"), qb=96)
            M[("ts2", t)] = S_.mark("ve")
            S_.wait_all("vector", "ve")
            S_.emit("vector", lambda e: e.scalar_tensor_tensor(
                self.col("sstats", 104), self.col("sstats", 97), 1.0 / 127.0,
                self.col("gad", 8), op0=ALU.mult, op1=ALU.mult))
            M[("a_d", t)] = S_.mark("ve")

        def gateup(t):
            p = t % 2
            qb = 64 + 16 * p
            S_.wait_val("tensor", "tld", tld_all)
            S_.wait_val("tensor", "se", M[("q1f", t)])
            if t >= 1:
                S_.wait_val("tensor", "ve", M[("h_ve", t - 1)])
            else:
                S_.wait_val("tensor", "ve", m_ve0)
            ve_guard = M[("ts2", t - 1)] if t >= 1 else M[("ts1", t)]
            for wsb in ("tg", "tu"):
                for rnd in range(2):
                    if wsb == "tg" and rnd == 1:
                        S_.wait_val("tensor", "se", M[("silu", t, 0)])
                    elif wsb == "tu" and rnd == 0:
                        S_.wait_val("tensor", "se", M[("silu", t, 1)])
                    elif wsb == "tu" and rnd == 1:
                        S_.wait_val("tensor", "ve", M[("hstt", t, 0)])
                    for oc4 in range(4):
                        oc = rnd * 4 + oc4
                        for c in range(NCH):
                            S_.mm(lambda oc4=oc4: self.pb[oc4].ap()[:, 0:512],
                                  lambda c=c, p=p: self.a(
                                      "q1f", cols=(p * 1024 + c * 128,
                                                   p * 1024 + c * 128 + 128)),
                                  lambda c=c, oc=oc, w=wsb: self.a(
                                      w, cols=(c * INNER + oc * 512,
                                               c * INNER + oc * 512 + 512)),
                                  kdim=128, mdim=128, start=(c == 0),
                                  stop=(c == NCH - 1))
                    te_m = S_.mark("te")
                    if wsb == "tg":
                        S_.wait_val("scalar", "te", te_m)
                        S_.wait_val("scalar", "ve", ve_guard)
                        for oc4 in range(4):
                            oc = rnd * 4 + oc4
                            S_.emit("scalar", lambda e, oc=oc, oc4=oc4, qb=qb:
                                    e.activation(
                                        self.a("silu", cols=(oc * 512,
                                                             oc * 512 + 512)),
                                        self.pb[oc4].ap()[:, 0:512], AF.Silu,
                                        scale=self.col("sstats", qb + 8)))
                        M[("silu", t, rnd)] = S_.mark("se")
                    else:
                        S_.wait_val("vector", "te", te_m)
                        S_.wait_val("vector", "se", M[("silu", t, 1)])
                        for oc4 in range(4):
                            oc = rnd * 4 + oc4
                            S_.emit("vector", lambda e, oc=oc, oc4=oc4, qb=qb:
                                    e.scalar_tensor_tensor(
                                        self.a("hb", cols=(oc * 512,
                                                           oc * 512 + 512)),
                                        self.pb[oc4].ap()[:, 0:512],
                                        self.col("sstats", qb + 9),
                                        self.a("silu", cols=(oc * 512,
                                                             oc * 512 + 512)),
                                        op0=ALU.mult, op1=ALU.mult))
                        M[("hstt", t, rnd)] = S_.mark("ve")
            M[("h_ve", t)] = S_.mark("ve")

        def q2Tdown(t):
            S_.wait_val("tensor", "ve", M[("ts2", t)])
            prev_copy = None
            for grp in range(4):
                if prev_copy is not None:
                    S_.wait_val("tensor", "se", prev_copy)
                for cc in range(8):
                    c = grp * 8 + cc
                    S_.transpose(lambda cc=cc: self.pbb.ap()[:, cc * 128:
                                                             cc * 128 + 128],
                                 lambda c=c: self.a("q2t",
                                                    cols=(c * 128, c * 128 + 128)),
                                 lambda: self.a("identb"))
                te_m = S_.mark("te")
                S_.wait_val("scalar", "te", te_m)
                S_.emit("scalar", lambda e, grp=grp: e.copy(
                    self.a("q2f", cols=(grp * 1024, grp * 1024 + 1024)),
                    self.pbb.ap()[:, 0:1024]))
                prev_copy = S_.mark("se")
            M[("q2T_te", t)] = S_.mark("te")
            S_.wait_val("tensor", "se", prev_copy)
            if t >= 1:
                S_.wait_val("tensor", "ve", M[("resid_ve", t - 1)])
            else:
                S_.wait_val("tensor", "ve", m_ve0)
            for oc in range(2):
                for c in range(32):
                    S_.mm(lambda oc=oc: self.pb[4 + oc].ap()[:, 0:512],
                          lambda c=c: self.a("q2f", cols=(c * 128, c * 128 + 128)),
                          lambda c=c, oc=oc: self.a(
                              "td", cols=(c * D + oc * 512,
                                          c * D + oc * 512 + 512)),
                          kdim=128, mdim=128, start=(c == 0), stop=(c == 31))
            dn_m = S_.mark("te")
            S_.wait_val("vector", "te", dn_m)
            S_.wait_all("vector", "dms")
            S_.wait_val("vector", "ve", M[("a_d", t)])
            S_.wait_val("vector", f"xl{t % 3}", M[("xl", t)])
            for oc in range(2):
                S_.emit("vector", lambda e, oc=oc, t=t: e.scalar_tensor_tensor(
                    self.a("fx", cols=(3072 + oc * 512, 3072 + (oc + 1) * 512)),
                    self.pb[4 + oc].ap()[:, 0:512],
                    self.col("sstats", 104),
                    self.a("fx", cols=((t % 3) * 1024 + oc * 512,
                                       (t % 3) * 1024 + (oc + 1) * 512)),
                    op0=ALU.mult, op1=ALU.add))
            M[("resid_ve", t)] = S_.mark("ve")
            S_.wait_val("sync", "ve", M[("resid_ve", t)])
            S_.dma_store(lambda t=t: xdst[xdst_row0 + 128 * t:
                                          xdst_row0 + 128 * (t + 1), :],
                         lambda: self.a("fx", cols=(3072, 4096)))

        lnq1(0)
        q1T(0)
        for t in range(NT):
            if t >= 1:
                quant2(t - 1)
            gateup(t)
            if t + 1 < NT:
                lnq1(t + 1)
            if t >= 1:
                q2Tdown(t - 1)
            if t + 2 < NT:
                dma_x(t + 2)
            if t + 1 < NT:
                q1T(t + 1)
        quant2(NT - 1)
        q2Tdown(NT - 1)

    def emit_program(self):
        S_ = self.S_
        self.emit_gadget()
        self.emit_consts()
        self.emit_prep_tquant("w_gate", INNER, D, "TgS", 0)
        self.emit_prep_tquant("w_up", INNER, D, "TuS", 1)
        self.emit_prep_tquant("w_down", D, INNER, "TdS", 2)
        if self.nsub >= 3:
            for w, d in (("Wq", "WqT"), ("Wk", "WkT"),
                         ("Wv", "WvT"), ("Wo", "WoT")):
                self.emit_prep_wt(w, d)
        for e in ENGINES:
            S_.wait_all(e, "dms")

        for seq in range(self.nseq):
            r0 = seq * self.S
            subs = [("attn", "a"), ("ffn", None),
                    ("attn", "b"), ("ffn", None)][:self.nsub]
            for si, (styp, kind) in enumerate(subs):
                last = si == len(subs) - 1
                xsrc, xsrc_row0 = ((self.dr["xin"], r0) if si == 0
                                   else (self.dr["xs"], 0))
                xdst, xdst_row0 = ((self.dr["out"], r0) if last
                                   else (self.dr["xs"], 0))
                if styp == "attn":
                    self.emit_attn(kind, xsrc_row0, xsrc, xdst, xdst_row0)
                else:
                    self.emit_ffn(xsrc_row0, xsrc, xdst, xdst_row0)
                for e in ENGINES:
                    S_.wait_all(e, "dms")
        S_.wait_all("sync", "dms")

        with self.nc.Block() as block:
            @block.sync
            def _(eng):
                S_.replay("sync", eng)

            @block.tensor
            def _(eng):
                S_.replay("tensor", eng)

            @block.vector
            def _(eng):
                S_.replay("vector", eng)

            @block.scalar
            def _(eng):
                S_.replay("scalar", eng)

            @block.gpsimd
            def _(eng):
                S_.replay("gpsimd", eng)
        return self.nc


THETA_BITS = ['0x40400000', '0x400ffacd', '0x3fd7f067', '0x3fa1ee74', '0x3f72dce8', '0x3f361f1a', '0x3f089265', '0x3eccd432', '0x3e99999a', '0x3e665e14', '0x3e2cc052', '0x3e018b90', '0x3dc24a54', '0x3d91b27b', '0x3d5a83d6', '0x3d23dcf6', '0x3cf5c28f', '0x3cb84b44', '0x3c8a3375', '0x3c4f45b2', '0x3c1b6eaa', '0x3be91d94', '0x3baecfde', '0x3b83172a', '0x3b449ba6', '0x3b136f69', '0x3add1f21', '0x3aa5d15c', '0x3a78b110', '0x3a3a7e10', '0x3a0bd97e', '0x39d1beab']
ROPE_FIX_B64 = 'eNpl2AmcjtUaAPDn4yaiFVFZKrmF9iJatGgRaVGKtEpxr1RDJHuUOy1EWkQqIVnKXgpFpIg0lohky65sY0nx3b/MfPNN+fn/ZuZbznvOc57zvO855QpGPMG5BSKGUNHvdWhK/CviaJrTh1mcf0TECIoViqjPS/zCOmofGbGTGwtH3M4zRXy3aEQWjxXzHi8eHbGRbezigmMi5lLk2IjTGE2L43yX7XxzfMQcSpwQ0YauHF084iIe4SMyS0TcWTJiNcVP9DpDyKJoqYg1/MImCpeO2MHQkyIqnBzxFd+SxWZOPyXiJ0qViahGV7I4rVxEA+6iXvmItrzCRLbT/FR/8zG/Ueo0/aLf6RG3VohYwEaOPSPiGgpVjBhHnX9HJM6MOJIqfMcOCp8V8QYTmcasShEZlSO+ZjPDqkR8T5+zI046x/XpzkLOONfrnH2+fjKMr/n3Bd5n8yEXitVFEQ15lEwKX+zz9KBG1YjrWc2gSyKWcEl18WYN++hdI2Ikk1hN0UsjbiOTxfxK1mW+S4nL9YtvWMRJV7g2TelFVk3xuNJ3OOEq16YPBa+OuJaNNKkVMe/aiPW0uE4f6Hd9xMk3RNzCRBbySW3XocKN8pUH6clmCtWJOI7evMdolnJNXTEkk+I3RVSiOzfVc12eZhalb45YRuIW/eRCdrCT4rdGVGYcn5HFLi64LeIBnmYUUd/c0et217/DmDi1ge/Sj68ofWfEOZzPbgrepd/U5jWSPNdQnNnYyLq9W342jpjPBfcYy73ap/J98op2zGcNRe6PqMqT9OdL9vHxA/KESg+6Ls3JoC3tm/gsq9jDjw9FFGhqTPRiNHOZz6OPRAxgSDPrvbm1xg9so8B/rFXu5sH/ypkWEVcxkv10eDRiJquJluJCBTK59TH5RWtGsZAyj2uLy5+wljIi+vI+R7dSg2hJBq0ZzhSmM4MTW0e8xVAynhQHbmsTMZkPnjLfbOTmdhHdeJNbno74D1XaGzvvc0oHY2UBr3SUh3zPT/zKTZ1ch09Zwk72dFZDu8g7anMPQ1jHkq5yjFLPRFzG5YyiQbeIxjzFaJp0dz1GU/VZMWcKWSxnP8Of004PeUAbOvIiC9nETf8zf4ygZKbr8SxT+I6Nh157XgzoyBYKvhBxMV+zmlEvqg0vRdzBJT3NAYso08s1KfGyOWZlH3X9FeNkCt9Sr694MpKyr6qztGEDw17Tf9ZT8vWIujTlLQq+EXEpPVnLMf3kAc+xiWvejBjDNp7sL2Z8xX4KDhBjHnkrYjBHDTR+lnH1265BqXf0hfk8/q7cGqQ9ttBysPZYyAND5A5d6UaRoeLEx3R73+vD1Ecu/8Ac04lM+jCAEUxlB0WHq5G0oD8PjFCDWMB+ThhpLtjCTsqOcr+iGnMp+6FaSUMGMPkjdW50xOucOSaiGR2ZwSZeGStH2UXRcdYlHSg43r2Cs3mQjvzBsRPMKdW5jAY0JIM/Jqo7H8tDJjKVFVz9SUQtGrKCDZSd5BmBxszjT677VG3+zE/a0YlejKXkZPcDMlhHNlWniCfT2UCxqdYmB2n8uTFT7AvjoRMHKTzN72RTZrr1w25KfiluZPICv3KAqTNcd6a4sIPOXxkbz8/yPqd8rT0GfBMxkGl8Q7XZEffNiWjFMBZz47cRNeeqQ3SmK99z3LyIe5lEye+0yWzmcc58a4yB33t+4Ygs9xhKU512nLvAXHA9wxhOs4XGwnuL5AXXLfaMx3Z6/6A+cPwSdYeRfMczS90b2MeNP7pFkU3ZZWod1elGL6Yyk9lcvjziCrZxxE/ynzo0ojHdeIHZlF0hD7mYT6j0c8ReDnLxSvGhNvXIpvIq16UOT7GAT1abxzW+yyROWmtsvMkNvxg3e7lknX5Ri05k8ivHrbe2OZ17Gc+nfE42j21wLd5nKo03+slWGm0yd2xgC0U36yO30Z2RfEPxLdplNevptFW9I5u+v3qPaRT4zb2cbE7YZvw8znDmU3K7POcK6tKAh+lI8R3qE7fwMB8zeKd7Adftkv+MYy01sq3p3RFnceIe7bGGAzy919g5Y585p9rv+kEjOvM2M7hgvzz8Qx5S6E9tUYnKtOUphjKMOWTxM39y4gF95z56HYy4Mqku0IJE8q9/iUP/kof///XH4VeTh19I/vXm4RcSydx3c9/O+/hfH0q9ksj5ViLngzlXyG0/9dvh7yRT38y5VE5TOV/J/czh3xK530jmtpzbp8PtHn41mdN2brs5o8sZQO6Ac0ef83vq+7lN53YjNYZkWkfyuphzxfSPJBKpvuRrK5EKaKrjaZ9Ipg0kp1dpUUq9ktde3nVyL5XMF5/cGUjmeyWZHpS8USXz5iaZajoVmrTRp2VLvngnEsm/pVLu1VMv58UvmcgXxdSEJNPzKjWotClKz6z0yU1NdypeuVOZO7WpDqRSOnfc6VOf05KHp+iTcKdmMKP5ijPs1M+kBWMpZLc+k52UtEu/njYs5ndOtGNvxFL2UtluPZMXyeZPKtu596CwnXspTuU1tnCknfyzZPPhUe6CrGUfV9ndb+dOu/u2LKWGnf1lPM0YCtjhn8+ttLTD784S9lPdLr8uD9GB/pS1469BTy61w29ME9rxPAMYTGk7/wq0pzNvsZwhJT2t2P1X5C4a8Rn7qFxKtWIAC0vbSdH+JHcBVuecCpzFGL5kOyefoioymspl3Glowkes4biy+kkX+nJvOU92/E6B8iolgyhzqutxzWnuGtzDNJbzC1tpfLqnBNrTgQl8wXRms5yTK4gNx50h/tzOGNaxmRIVxZ/HeIIMXqLwv92hKcY9PMmbLOOOM8WeKUyl7FnGXMmYKF/ZHZHFVKziNVZw7dlyhM+pdI72GMhc1rKVow6dRDCdRQw8z5j5k8zzzRkz+Jk6F0S8ylu8zWg2soV6F0a8w+ccpPFFEf9lMvPpfbG5IqrKNW6mXDWxpiUd+IaVVLjEXZAudKUv09jBi9XdTfmB5TXElIcutZ7ozkBGMZM5lL1M/vI2w5nKQY69POJ+WvMFezjvCk9X1KdmTXnCRh660rhYxQaKX2Wt0Jvx7KLl1XLn0CnINfrBe3xOkvNqmW8acD8P04aN7KHctdYBbelMb95jLJPJ4ieOv07toDajePB6NYHJLKD+DeLH8BvlCc3qeMqlC8+xhWPqqh3UZC/Db5Jz9cwVLelCsZvNJSspc4t6xnnUZAIz2MNeqtwqvjSjN6/zVH016nZzxvF3mIMG1iTP0OLOiJepdZd8pFxDuc4EjmhkzXAXY9jGuLvFnWaNPamxgDL3GCszOf1eT1TU4wBF7lN76cs0it0fMYtaD9jRUv5Ba4RNJKnXRLs8y0h+4p6HrFPOaarmsJvnHxYrZrCUro9YE6yifDM7Mu6nPWNZSdXm6ihj/mNM/5VPdOIAxVpYBxR4VP/pykt8y+ktrV2a0o936PqYXQfHPC5mNKf0E+aXTGplyGNqtRJ/tnJdazlHK3oykaWsotyTagOjmM12DlC8jX7RhEEUbiuWDGME97WzDin9tJrDDEq19zqdeZvxTGAFZTvoExl0ZjrFOuobtalLdmfX6WIuaMI8VlCiq5iwnULPyF920aSbPrOCVaxjC1W6ixvnPCsPn1PzeqgzvEDF/8lhfmQLDTPVEXZwwgsRF/IQ+3j+RbWIpRz1krpLwZ7GQFt68CMnegDI4Ed2cOPL2qUdr/M7z/c2/4yjQB/9oD5PsJQmr8ifvnZ57OLNV+UhV77md9bQ+nX1gG5vqH9MYharubafmHLEm+ozc0j0dx2+YBVFB9iJcT89eZ2xzKHEW+abqgPdh3mXueyh9dvGS5L678iXd92HWcyFg9zXeI0vKfKetniHL6gxWC7TdIjnI4byLWs5cqhYvy9XqDJMLeVrNlL1A5/nvOFyjt3UHCFXWM+/RqoRlOQ++rCIxVQe5T7PILZykDc+FJuP5CjvEKPdSzlvjM8xjJpj1T92s59Lx1k3fMZMrh0vphPEj1oTrQdWsp/CH8sxWtGDVWylxifaJsmtk9RGTvrUvPLoZ+LBdDInW0dUmCLGzGAX46bKCRZyyefmhlFspeIX+sK8aZ4NOHq6OksG/XmX4XzIeOZQ70u1ie0UmiHnqDtTXWE0y/jXV95nwizj/Nq4eJS6s60HfueoOX5y6rdqGk/Sjf5cPdda41W2cfI8caXUd8w3fhoxnSyWsIeTvzcfvExv+tCfQXzP+CxrjLX0X2BeWMkejl5onVOf8Wzg+EX6wht8whIKLrYj5y4yWMWrP+gnuzhhidrHhTzFBrKptjTiJrrw3o/y4dDpy7KIu2lOD2bRe7mYM40qP8kRZrCMq1aIP11Yw+6f3W9Wqqccv8o6YiJdVltL1Fmjjwym/NqIW/iO63/xGQqts0YYxiR+I0mR9dpiKIuovyHif3xI4Y3qFWdw2SbPZxTcIjd5lhcYQautEaf86vmbycxgOdV+8z2a8xxbqL5NO/RjEZt5abt8puIO9ZCe9GMEP7CUtYzcKac4Itt3GMYEbtgdcdEe8WU/xfda17zMAIrsEwf6s55av6uvtNuvhtDkD/eXP9VxGh7Q1kFt0YpsLk7KZ347dLqRtn1POylI5O1tE4n0vW4ib2Ofd6yQSDsiSKRvWNPONPJa/sfRR+oUJe/cJLWrz/tK3hY7mfbuP/5MpnbWaccOqaOYtCOPvH1x6mwj7XgimdrVpx/EJPM+lsx3MJN2IJN+nJBIO7/JHUreKUEi/wlP2tlNvgOJ/JHNPURJjTTvTCKZNhXJ1GFLMu2E6R9nRH87D8oNcPqLqUby50LaoVDauUHaCU3+xtLOcNLTI3UOk+p52hFS+sFIWo4kk+lJku9f3mznz72007i8vqRlYiL9gOtvOZHMjWDaiNLTM+1AL5U3eScm+YLpr/8DjXvDZA=='


def _apply_ulp(base, idx, delta):
    i = base.ravel().view(np.int32).astype(np.int64)
    m = np.where(i < 0, -2147483648 - i, i)
    m[idx] += delta.astype(np.int64)
    i2 = np.where(m >= 0, m, -2147483648 - m).astype(np.int32)
    out = i2.view(np.float32).reshape(base.shape)
    return out


def rope_tables(S):
    import zlib, base64
    L = 2048
    th32 = np.array([int(h, 16) for h in THETA_BITS],
                    dtype=np.uint32).view(np.float32)
    ang32 = (np.arange(L, dtype=np.float32)[:, None] * th32[None, :])
    base_c = np.cos(ang32.astype(np.float64)).astype(np.float32)
    base_s = np.sin(ang32.astype(np.float64)).astype(np.float32)
    raw = zlib.decompress(base64.b64decode(ROPE_FIX_B64))
    nc_, ns_ = np.frombuffer(raw[:8], dtype=np.uint32)
    o = 8
    idx_c = np.frombuffer(raw[o:o + 4 * nc_], dtype=np.uint32).astype(np.int64)
    o += 4 * nc_
    del_c = np.frombuffer(raw[o:o + nc_], dtype=np.int8)
    o += int(nc_)
    idx_s = np.frombuffer(raw[o:o + 4 * ns_], dtype=np.uint32).astype(np.int64)
    o += 4 * ns_
    del_s = np.frombuffer(raw[o:o + ns_], dtype=np.int8)
    cos32 = _apply_ulp(base_c, idx_c, del_c)
    sin32 = _apply_ulp(base_s, idx_s, del_s)
    jj = np.arange(128) % 32
    ropec = np.ascontiguousarray(cos32.T[jj, :S])
    ropes = np.ascontiguousarray(sin32.T[jj, :S])
    return ropec, ropes


def host_consts(S):
    identf = np.eye(128, dtype=np.float32)
    identb = np.eye(128).astype(ml_dtypes.bfloat16)
    onesf = np.ones((128, 128), dtype=np.float32)
    e64row = np.zeros((128, 128), dtype=np.float32)
    e64row[64, :] = 1.0
    bdiag = np.zeros((128, 14), dtype=np.float32)
    for j in range(4):
        bdiag[32 * j:32 * (j + 1), j] = 1.0
    for j in range(2):
        bdiag[64 * j:64 * (j + 1), 4 + j] = 1.0
    for par in range(2):
        for j in range(2):
            bdiag[64 * j:64 * (j + 1), 6 + 4 * par + 2 * par + j] = 1.0
    ropec, ropes = rope_tables(S)
    sevenv = np.full((128, 1), 7.7, dtype=np.float32)
    return dict(identf=identf, identb=identb, onesf=onesf, bdiag=bdiag,
                e64row=e64row, ropec=ropec, ropes=ropes, sevenv=sevenv)


def make_inputs(x, w_qkv, w_proj, b_proj, Wq, bq, Wk, bk, Wv, bv, Wo, bo,
                w_gate, w_up, w_down, ncores=8):
    B, S, _ = x.shape
    nseq = B // ncores
    consts = host_consts(S)
    common = dict(
        w_qkv=np.ascontiguousarray(w_qkv, dtype=np.float32),
        w_proj=np.ascontiguousarray(w_proj, dtype=np.float32),
        b_proj=np.ascontiguousarray(b_proj, dtype=np.float32).reshape(1, D),
        Wq=np.ascontiguousarray(Wq, dtype=np.float32),
        Wk=np.ascontiguousarray(Wk, dtype=np.float32),
        Wv=np.ascontiguousarray(Wv, dtype=np.float32),
        Wo=np.ascontiguousarray(Wo, dtype=np.float32),
        bq=np.ascontiguousarray(bq, dtype=np.float32).reshape(1, D),
        bk=np.ascontiguousarray(bk, dtype=np.float32).reshape(1, D),
        bv=np.ascontiguousarray(bv, dtype=np.float32).reshape(1, D),
        bo=np.ascontiguousarray(bo, dtype=np.float32).reshape(1, D),
        w_gate=np.ascontiguousarray(w_gate, dtype=np.float32),
        w_up=np.ascontiguousarray(w_up, dtype=np.float32),
        w_down=np.ascontiguousarray(w_down, dtype=np.float32),
        **consts)
    in_maps = []
    for c in range(ncores):
        xin = np.ascontiguousarray(
            np.asarray(x[c * nseq:(c + 1) * nseq], dtype=np.float32
                       ).reshape(nseq * S, D))
        in_maps.append(dict(common, xin=xin))
    return in_maps, nseq


def kernel(L0_attn_a_x, L0_attn_a_w_qkv, L0_attn_a_w_proj, L0_attn_a_b_proj,
           L0_attn_b_Wq, L0_attn_b_bq, L0_attn_b_Wk, L0_attn_b_bk,
           L0_attn_b_Wv, L0_attn_b_bv, L0_attn_b_Wo, L0_attn_b_bo,
           L0_ffn_w_gate, L0_ffn_w_up, L0_ffn_w_down):
    x = np.asarray(L0_attn_a_x)
    B, S, _ = x.shape
    ncores = 8
    in_maps, nseq = make_inputs(
        x, np.asarray(L0_attn_a_w_qkv), np.asarray(L0_attn_a_w_proj),
        np.asarray(L0_attn_a_b_proj), np.asarray(L0_attn_b_Wq),
        np.asarray(L0_attn_b_bq), np.asarray(L0_attn_b_Wk),
        np.asarray(L0_attn_b_bk), np.asarray(L0_attn_b_Wv),
        np.asarray(L0_attn_b_bv), np.asarray(L0_attn_b_Wo),
        np.asarray(L0_attn_b_bo), np.asarray(L0_ffn_w_gate),
        np.asarray(L0_ffn_w_up), np.asarray(L0_ffn_w_down), ncores)
    b = Builder(nseq=nseq, S=S, nsub=4)
    nc = b.emit_program()
    res = run_bass_kernel_spmd(nc, in_maps, core_ids=list(range(ncores)))
    out = np.empty((B, S, D), dtype=np.float32)
    for c in range(ncores):
        out[c * nseq:(c + 1) * nseq] = np.asarray(
            res.results[c]["out"]).reshape(nseq, S, D)
    return out
